# version 15
# baseline (speedup 1.0000x reference)
import numpy as np
import ml_dtypes
from contextlib import ExitStack

import concourse.bass as bass
import concourse.mybir as mybir
from concourse.bass_utils import run_bass_kernel_spmd

F32 = mybir.dt.float32
BF16 = mybir.dt.bfloat16
AF = mybir.ActivationFunctionType
OP = mybir.AluOpType
AX = mybir.AxisListType
BF = ml_dtypes.bfloat16

D = 4096
S = 8192
NH = 32
NKV = 8
HD = 128
FC = 32
TOK = 1024
TN = 512
NB = 16
NCORE = 8
EPS = 1e-5
CSH = 12.0
QKS = 1.0 / np.sqrt(128.0)
NSLOT = 24

ENGS = ("t", "v", "s", "g", "y")


def owner(b):
    return b if b < 8 else 15 - b


def owner_tt(b):
    return 0 if b < 8 else 1


def slot_block(c, s):
    if s < 7:
        return s if s < c else None
    if s == 7:
        return c
    if s < 23:
        return (s - 8) if (s - 8) < 15 - c else None
    return 15 - c


class Plan:
    def __init__(self):
        self.segs = {}
        self.cur = None
        self.cnt = {}
        self.lastw = {e: {} for e in ENGS}
        self.last_op = {}

    def seg(self, name):
        self.cur = self.segs.setdefault(name, [])

    def snap(self):
        import copy
        return (dict(self.cnt), copy.deepcopy(self.lastw), dict(self.last_op))

    def restore(self, st):
        import copy
        self.cnt = dict(st[0])
        self.lastw = copy.deepcopy(st[1])
        self.last_op = dict(st[2])

    def op(self, eng, fn, deps=(), tick=False, dma=False, sem=None, rawinc=False):
        waits = []
        for d in deps:
            if d is None:
                continue
            s, v = d
            if v <= 0:
                continue
            if self.lastw[eng].get(s, -1) >= v:
                continue
            self.lastw[eng][s] = v
            waits.append((s, v))
        inc = None
        if dma:
            tick = True
        if tick:
            sm = sem or eng
            n = 16 if dma else 1
            self.cnt[sm] = self.cnt.get(sm, 0) + n
            inc = (sm, n, rawinc)
        rec = dict(eng=eng, fn=fn, waits=waits, inc=inc)
        self.cur.append(rec)
        if fn is not None:
            self.last_op[eng] = rec
        if inc:
            return (inc[0], self.cnt[inc[0]])
        return None

    def tkt(self, sm):
        return (sm, self.cnt.get(sm, 0))

    def ensure_tick(self, eng):
        rec = self.last_op.get(eng)
        if rec is not None and rec["inc"] is None:
            self.cnt[eng] = self.cnt.get(eng, 0) + 1
            rec["inc"] = (eng, 1, False)
        return (eng, self.cnt.get(eng, 0))

    DMASEMS = ("y", "gx", "gy", "gz", "gw", "w0", "w1", "kv0", "kv1",
               "io0", "io1", "dbg", "rs")

    def sw2_begin(self, slot):
        self.cur.append(dict(eng=None, ctl="sw2_begin", slot=slot, fn=None))
        import copy
        self._sw2_lastw = copy.deepcopy(self.lastw)
        self._sw2_cnt = dict(self.cnt)

    def sw2_end(self):
        import copy
        deltas = {}
        for sm in ("t", "s"):
            d = self.cnt.get(sm, 0) - self._sw2_cnt.get(sm, 0)
            if d:
                deltas[sm] = d
        self.lastw = copy.deepcopy(self._sw2_lastw)
        self.cur.append(dict(eng=None, ctl="sw2_end", deltas=deltas, fn=None))

    def pif_begin(self, slot):
        self.cur.append(dict(eng=None, ctl="pif_begin", slot=slot, fn=None))
        import copy
        self._pif_lastw = copy.deepcopy(self.lastw["t"])
        self._pif_cnt = self.cnt.get("t", 0)

    def pif_end(self):
        import copy
        delta = self.cnt.get("t", 0) - self._pif_cnt
        self.lastw["t"] = copy.deepcopy(self._pif_lastw)
        self.cur.append(dict(eng=None, ctl="pif_end", delta=delta, fn=None))

    def barrier(self):
        tks = {e: self.ensure_tick(e) for e in ENGS}
        dts = [self.tkt(s) for s in self.DMASEMS]
        for e in ENGS:
            self.op(e, None, deps=[tks[o] for o in ENGS if o != e] + dts)


def build(dump=()):
    nc = bass.Bass("TRN2", target_bir_lowering=False, debug=False)

    xT = nc.dram_tensor("xT", [D, TOK], F32, kind="ExternalInput").ap()
    wq = nc.dram_tensor("wq", [D, D], F32, kind="ExternalInput").ap()
    wk = nc.dram_tensor("wk", [D, 1024], F32, kind="ExternalInput").ap()
    wv = nc.dram_tensor("wv", [D, 1024], F32, kind="ExternalInput").ap()
    wo = nc.dram_tensor("wo", [D, D], F32, kind="ExternalInput").ap()
    wup = nc.dram_tensor("wup", [D, D], F32, kind="ExternalInput").ap()
    aaugT = nc.dram_tensor("aaugT", [D, 36], F32, kind="ExternalInput").ap()
    ballT = nc.dram_tensor("ballT", [32, D], F32, kind="ExternalInput").ap()
    ebias = nc.dram_tensor("ebias", [36, 1], F32, kind="ExternalInput").ap()
    cosT_d = nc.dram_tensor("cosT", [128, TOK], F32, kind="ExternalInput").ap()
    ssinT_d = nc.dram_tensor("ssinT", [128, TOK], F32, kind="ExternalInput").ap()
    identf_d = nc.dram_tensor("identf", [128, 128], F32, kind="ExternalInput").ap()
    onesb_d = nc.dram_tensor("onesb", [128, 128], BF16, kind="ExternalInput").ap()
    pswap_d = nc.dram_tensor("pswap", [128, 128], BF16, kind="ExternalInput").ap()
    e32_d = nc.dram_tensor("e32", [4, 32], BF16, kind="ExternalInput").ap()
    dmask_d = nc.dram_tensor("dmask", [128, 4 * TN], BF16, kind="ExternalInput").ap()
    sflags_d = nc.dram_tensor("sflags", [128, NSLOT * 128], BF16, kind="ExternalInput").ap()
    iflags_d = nc.dram_tensor("iflags", [1, 32], mybir.dt.int32, kind="ExternalInput").ap()
    y = nc.dram_tensor("y", [D, TOK], F32, kind="ExternalOutput").ap()

    ksrc = nc.dram_tensor("ksrc", [2 * 1024, 512], BF16, kind="Internal")
    vsrc = nc.dram_tensor("vsrc", [1024, 1024], BF16, kind="Internal")
    kdst = nc.dram_tensor("kdst", [NCORE * 2 * 1024, 512], BF16, kind="Internal",
                          addr_space="Shared")
    vdst = nc.dram_tensor("vdst", [NCORE * 1024, 1024], BF16, kind="Internal",
                          addr_space="Shared")
    keff = nc.dram_tensor("keff", [NSLOT * 1024, 512], BF16, kind="Internal")
    veff = nc.dram_tensor("veff", [NSLOT * 8 * 4 * 128, 128], BF16, kind="Internal")
    xpscr = nc.dram_tensor("xpscr", [D, TOK], F32, kind="Internal")

    kdst_f = kdst.ap().rearrange("a b -> (a b)")
    vdst_f = vdst.ap().rearrange("a b -> (a b)")
    keff_f = keff.ap().rearrange("a b -> (a b)")
    veff_f = veff.ap().rearrange("a b -> (a b)")
    vsrc_f = vsrc.ap().rearrange("a b -> (a b)")

    SB0 = 20480

    def at(name, shape, dtype, off):
        return nc.alloc_sbuf_tensor_at(name, shape, dtype, offset=SB0 + off)

    AT = at("A_xn", [128, FC * TOK], BF16, 0)
    BT = at("B_av", [128, FC * TOK], BF16, 65536)
    CT = at("C_q", [128, FC * TOK], BF16, 131072)

    identf = at("identf", [128, 128], F32, 196608)
    onesb = at("onesb", [128, 128], BF16, 197120)
    pswap = at("pswap", [128, 128], BF16, 197376)
    e32 = at("e32", [4, 32], BF16, 197632)
    dmask = at("dmask", [128, 4 * TN], BF16, 197696)
    sflags = at("sflags", [128, NSLOT * 128], BF16, 201792)
    cm12 = at("cm12", [128, 1], F32, 207936)
    ifl = at("ifl", [1, 32], mybir.dt.int32, 207968)

    W1 = [at(f"W1_{i}", [128, FC * 128], BF16, 65536 + i * 8192) for i in range(2)]
    cosT = at("cosT", [128, TOK], F32, 81920)
    ssinT = at("ssinT", [128, TOK], F32, 86016)
    m1 = at("m1", [128, TN], F32, 90112)
    r1 = at("r1", [128, TN], F32, 92160)
    var1 = at("var1", [128, TN], F32, 94208)
    sd1 = at("sd1", [128, TN], F32, 96256)
    x2b = [at(f"x2b_{i}", [128, TN], BF16, 98304 + i * 1024) for i in range(2)]
    lntmp = [at(f"lntmp_{i}", [128, TN], F32, 100352 + i * 2048) for i in range(2)]
    ktmp = [at(f"ktmp_{i}", [128, TN], BF16, 104448 + i * 1024) for i in range(2)]
    tR1 = [at(f"tR1_{i}", [128, TN], F32, 106496 + i * 2048) for i in range(2)]
    tR2 = [at(f"tR2_{i}", [128, TN], F32, 110592 + i * 2048) for i in range(2)]
    vtmp = [at(f"vtmp_{i}", [128, TN], F32, 114688 + i * 2048) for i in range(2)]
    vtt = [at(f"vtt_{i}", [128, 128], BF16, 118784 + i * 256) for i in range(2)]
    kout = [at(f"kout_{i}", [128, TN], BF16, 119296 + i * 1024) for i in range(2)]
    zsb0 = at("zsb0", [128, 4096], BF16, 121344)

    kb = [at(f"kb_{i}", [128, 512], BF16, 32768 + i * 1024) for i in range(2)]
    vb = [at(f"vb_{i}", [128, 512], BF16, 34816 + i * 1024) for i in range(2)]
    attb = [at(f"attb_{i}", [128, 1024], BF16, 36864 + i * 2048) for i in range(2)]
    rinv = [at(f"rinv_{i}", [128, 512], F32, 40960 + i * 2048) for i in range(2)]
    avraw = [at(f"avraw_{i}", [128, 512], F32, 45056 + i * 2048) for i in range(2)]
    smraw = [at(f"smraw_{i}", [128, 512], F32, 49152 + i * 2048) for i in range(2)]

    W3 = [at(f"W3_{i}", [128, FC * 128], BF16, 0 + i * 8192) for i in range(2)]
    xtile = [at(f"xtile_{i}", [128, TN], F32, 16384 + i * 2048) for i in range(2)]
    xpf = [at(f"xpf_{i}", [128, TN], F32, 20480 + i * 2048) for i in range(2)]
    m3 = at("m3", [128, TN], F32, 24576)
    r3 = at("r3", [128, TN], F32, 26624)
    var3 = at("var3", [128, TN], F32, 28672)
    sd3 = at("sd3", [128, TN], F32, 30720)
    x2b3 = [at(f"x2b3_{i}", [128, TN], BF16, 32768 + i * 1024) for i in range(2)]
    lntmp3 = [at(f"lntmp3_{i}", [128, TN], F32, 34816 + i * 2048) for i in range(2)]

    zsb = at("zsb", [36, TN], F32, 40960)
    zmk = [at(f"zmk_{i}", [32, TN], BF16, 43008 + i * 1024) for i in range(2)]
    lgsb = at("lgsb", [4, TN], F32, 45056)
    lgt = at("lgt", [128, 16], F32, 47104)
    m4 = at("m4", [128, 4], F32, 47232)
    ohtm = at("ohtm", [128, 16], F32, 47360)
    tb1 = at("tb1", [128, 4], F32, 47488)
    tb2 = at("tb2", [128, 4], F32, 47552)
    ohsb = at("ohsb", [4, TN], BF16, 47616)
    aaugs = at("aaugs", [128, FC * 36], BF16, 65536)
    balls = at("balls", [32, D], BF16, 68096)
    biass = at("biass", [36, 1], F32, 76288)

    pA = nc.alloc_psum_tensor("pA", [128, 1024], F32)
    pB = nc.alloc_psum_tensor("pB", [128, 1024], F32)
    pC = nc.alloc_psum_tensor("pC", [128, 512], F32)
    pD = nc.alloc_psum_tensor("pD", [128, 512], F32)
    pE = nc.alloc_psum_tensor("pE", [128, 512], F32)
    pF = nc.alloc_psum_tensor("pF", [128, 512], F32)

    pl = Plan()
    pl.seg("main0")

    tload = []
    for dst, src in ((identf, identf_d), (onesb, onesb_d), (pswap, pswap_d),
                     (e32, e32_d), (dmask, dmask_d), (sflags, sflags_d),
                     (cosT, cosT_d), (ssinT, ssinT_d), (ifl, iflags_d)):
        tload.append(pl.op("y", lambda E, dst=dst, src=src:
                           E["y"].dma_start(out=dst[:, :], in_=src), dma=True))
    t_tables = pl.tkt("y")
    t_zsb0 = pl.op("v", lambda E: E["v"].memset(zsb0[:, :], 0.0), tick=True)
    t_cm12 = pl.op("v", lambda E: E["v"].memset(cm12[:, :], -CSH), tick=True)

    x_src = xT.rearrange("(c p) t -> p c t", p=128)
    XSEM = ("gx", "gy", "gz", "gw")
    t_xq = []
    for q4 in range(4):
        t_xq.append(pl.op("g", lambda E, q4=q4: E["g"].dma_start(
            out=AT.ap().rearrange("p (c t) -> p c t", c=FC)[:, q4 * 8:(q4 + 1) * 8, :],
            in_=x_src[:, q4 * 8:(q4 + 1) * 8, :]),
            dma=True, sem=XSEM[q4]))
    wk_pre = [None, None]
    for nb in range(2):
        wsrc = wk[:, nb * 128:(nb + 1) * 128].rearrange("(c p) o -> p c o", p=128)
        wk_pre[nb] = pl.op("g", lambda E, nb=nb, wsrc=wsrc: E["g"].dma_start(
            out=W1[nb].ap().rearrange("p (c o) -> p c o", c=FC), in_=wsrc),
            deps=[], dma=True, sem=f"w{nb}")
    t_xl = [t_xq[3], t_xq[3]]

    def emit_ln(arena, tt, dep_in, mT, rT, varT, sdT, x2T, lnT, dep_extra=(),
                dep_fn=None):
        def sl(fc):
            return arena[:, fc * TOK + tt * TN: fc * TOK + tt * TN + TN]
        if dep_fn is None:
            dep_fn = lambda fc: dep_in
        prev_v = pl.tkt("v")
        prev_t = pl.tkt("t")
        tk_s1 = None
        for fc in range(FC):
            tk_s1 = pl.op("t", lambda E, fc=fc: E["t"].matmul(
                pE[:, :], onesb[:, :], sl(fc), start=(fc == 0), stop=(fc == FC - 1)),
                deps=[dep_fn(fc), t_tables, prev_v] + list(dep_extra),
                tick=(fc == FC - 1))
        sq_pe = [None, None]
        for fc in range(FC):
            par = fc % 2
            d = [dep_fn(fc), prev_t]
            if sq_pe[par] is not None:
                d.append(sq_pe[par])
            tk_sq = pl.op("s", lambda E, fc=fc, par=par: E["s"].activation(
                x2T[par][:, :], sl(fc), AF.Square), deps=d, tick=True)
            sq_pe[par] = pl.op("t", lambda E, par=par, fc=fc: E["t"].matmul(
                pF[:, :], onesb[:, :], x2T[par][:, :],
                start=(fc == 0), stop=(fc == FC - 1)),
                deps=[tk_sq, prev_v], tick=True)
        tk_s2 = sq_pe[(FC - 1) % 2]
        a = pl.op("v", lambda E: E["v"].tensor_scalar(
            mT[:, :], pE[:, :], 1.0 / D, 0.0, OP.mult), deps=[tk_s1], tick=True)
        b = pl.op("v", lambda E: E["v"].tensor_scalar(
            varT[:, :], pF[:, :], 1.0 / D, EPS, OP.mult, OP.add),
            deps=[tk_s2, a], tick=True)
        cgate = pl.op("v", lambda E: E["v"].tensor_tensor(
            sdT[:, :], mT[:, :], mT[:, :], OP.mult), deps=[b], tick=True)
        dd = pl.op("v", lambda E: E["v"].tensor_tensor(
            varT[:, :], varT[:, :], sdT[:, :], OP.subtract), deps=[cgate], tick=True)
        ee = pl.op("s", lambda E: E["s"].activation(
            sdT[:, :], varT[:, :], AF.Sqrt), deps=[dd], tick=True)
        ff = pl.op("v", lambda E: E["v"].reciprocal(rT[:, :], sdT[:, :]),
                   deps=[ee], tick=True)
        lastv = ff
        lastg = None
        for fc in range(FC):
            par = fc % 2
            eng = "v" if par == 0 else "g"
            prev = lastv if par == 0 else lastg
            g1 = pl.op(eng, lambda E, fc=fc, par=par, eng=eng: E[eng].tensor_tensor(
                lnT[par][:, :], sl(fc), mT[:, :], OP.subtract),
                deps=[prev, ff, tk_s2], tick=True)
            tkn = pl.op(eng, lambda E, fc=fc, par=par, eng=eng: E[eng].tensor_tensor(
                sl(fc), lnT[par][:, :], rT[:, :], OP.mult), deps=[g1], tick=True)
            if par == 0:
                lastv = tkn
            else:
                lastg = tkn
        last = pl.op("v", lambda E: E["v"].nop(), deps=[lastv, lastg], tick=True)
        return last

    ln1 = []
    for tt in range(2):
        ln1.append(emit_ln(AT[:, :], tt, t_xq[3], m1, r1, var1, sd1, x2b, lntmp,
                           dep_fn=lambda fc: t_xq[fc // 8]))

    mats = [("k", wk, 8), ("v", wv, 8), ("q", wq, 32)]
    pc = 0
    prc = 0
    wld = [None, None]
    wuse = [None, None]
    projev = {}
    pA_use = [None, None]
    ktmp_use = [None, None]
    vtmp_use = [None, None]
    vtt_use = [None, None]
    kout_use = [None, None]
    tR1_use = [None, None]
    tR2_use = [None, None]
    nbg = 0
    kv_store = []
    for kind, wmat, nnb in mats:
        for nb in range(nnb):
            par = nbg % 2
            wsrc = wmat[:, nb * 128:(nb + 1) * 128].rearrange(
                "(c p) o -> p c o", p=128)
            if kind == "k" and nb < 2:
                wld[par] = wk_pre[nb]
            else:
                wld[par] = pl.op("g", lambda E, par=par, wsrc=wsrc: E["g"].dma_start(
                    out=W1[par].ap().rearrange("p (c o) -> p c o", c=FC), in_=wsrc),
                    deps=[wuse[par]], dma=True, sem=f"w{par}")
            for tt in range(2):
                pcp = pc % 2
                bank = pC if pcp == 0 else pD
                tk_mm = None
                for fc in range(FC):
                    tk_mm = pl.op("t", lambda E, fc=fc, par=par, tt=tt, bank=bank:
                                  E["t"].matmul(
                                      bank[:, :],
                                      W1[par][:, fc * 128:(fc + 1) * 128],
                                      AT[:, fc * TOK + tt * TN: fc * TOK + tt * TN + TN],
                                      start=(fc == 0), stop=(fc == FC - 1)),
                                  deps=[wld[par], ln1[tt], projev.get(pcp)],
                                  tick=(fc == FC - 1))
                wuse[par] = tk_mm
                if kind in ("k", "q"):
                    kp = pc % 2
                    tcp = pl.op("s", lambda E, bank=bank, kp=kp: E["s"].copy(
                        ktmp[kp][:, :], bank[:, :]),
                        deps=[tk_mm, ktmp_use[kp]], tick=True)
                    projev[pcp] = tcp
                    sl_ = prc % 2
                    tpm = pl.op("t", lambda E, kp=kp, sl_=sl_: E["t"].matmul(
                        pA[:, sl_ * 512:sl_ * 512 + 512], pswap[:, :],
                        ktmp[kp][:, :], start=True, stop=True),
                        deps=[tcp, pA_use[sl_]], tick=True)
                    t1 = pl.op("v", lambda E, kp=kp, tt=tt: E["v"].tensor_tensor(
                        tR1[kp][:, :], ktmp[kp][:, :],
                        cosT[:, tt * TN: tt * TN + TN], OP.mult),
                        deps=[tcp, tR1_use[kp], t_tables], tick=True)
                    t2 = pl.op("v", lambda E, kp=kp, sl_=sl_, tt=tt: E["v"].tensor_tensor(
                        tR2[kp][:, :], pA[:, sl_ * 512:sl_ * 512 + 512],
                        ssinT[:, tt * TN: tt * TN + TN], OP.mult),
                        deps=[tpm, t1, tR2_use[kp]], tick=True)
                    pA_use[sl_] = t2
                    prc += 1
                    ktmp_use[kp] = t2
                    if kind == "q":
                        h = nb
                        tq = pl.op("v", lambda E, kp=kp, h=h, tt=tt: E["v"].tensor_tensor(
                            CT[:, h * TOK + tt * TN: h * TOK + tt * TN + TN],
                            tR1[kp][:, :], tR2[kp][:, :], OP.add),
                            deps=[t2], tick=True)
                        tR1_use[kp] = tq
                        tR2_use[kp] = tq
                    else:
                        ko = pl.op("v", lambda E, kp=kp: E["v"].tensor_tensor(
                            kout[kp][:, :], tR1[kp][:, :], tR2[kp][:, :], OP.add),
                            deps=[t2, kout_use[kp]], tick=True)
                        tR1_use[kp] = ko
                        tR2_use[kp] = ko
                        g_ = nb
                        td = pl.op("y", lambda E, kp=kp, g_=g_, tt=tt: E["y"].dma_start(
                            out=ksrc.ap()[tt * 1024 + g_ * 128: tt * 1024 + g_ * 128 + 128, :],
                            in_=kout[kp][:, :]), deps=[ko], dma=True)
                        kout_use[kp] = td
                        kv_store.append(td)
                else:
                    vp = pc % 2
                    tcp = pl.op("s", lambda E, bank=bank, vp=vp: E["s"].copy(
                        vtmp[vp][:, :], bank[:, :]),
                        deps=[tk_mm, vtmp_use[vp]], tick=True)
                    projev[pcp] = tcp
                    g_ = nb
                    last_v = None
                    for j in range(4):
                        sl_ = prc % 2
                        ttr = pl.op("t", lambda E, vp=vp, j=j, sl_=sl_: E["t"].transpose(
                            pA[:, sl_ * 512:sl_ * 512 + 128],
                            vtmp[vp][:, j * 128:(j + 1) * 128], identf[:, :]),
                            deps=[tcp, pA_use[sl_], t_tables], tick=True)
                        w2 = j % 2
                        tvt = pl.op("s", lambda E, sl_=sl_, w2=w2: E["s"].copy(
                            vtt[w2][:, :], pA[:, sl_ * 512:sl_ * 512 + 128]),
                            deps=[ttr, vtt_use[w2]], tick=True)
                        pA_use[sl_] = tvt
                        prc += 1
                        fo = (g_ * 8 + tt * 4 + j) * 16384
                        tst = pl.op("y", lambda E, w2=w2, fo=fo: E["y"].dma_start(
                            out=vsrc_f[fo:fo + 16384].rearrange("(p d) -> p d", p=128),
                            in_=vtt[w2][:, :]), deps=[tvt], dma=True)
                        vtt_use[w2] = tst
                        kv_store.append(tst)
                        last_v = tst
                    vtmp_use[vp] = last_v
                pc += 1
            nbg += 1
            if kind == "v" and nb == nnb - 1:
                tk_cc_k = pl.op("g", lambda E: E["g"].collective_compute(
                    "AllGather", OP.bypass,
                    replica_groups=[list(range(NCORE))],
                    ins=[ksrc.ap().opt()], outs=[kdst.ap().opt()]),
                    deps=kv_store, tick=True, sem="ck", rawinc=True)
                tk_cc_v = pl.op("g", lambda E: E["g"].collective_compute(
                    "AllGather", OP.bypass,
                    replica_groups=[list(range(NCORE))],
                    ins=[vsrc.ap().opt()], outs=[vdst.ap().opt()]),
                    deps=[], tick=True, sem="cv", rawinc=True)

    base = pl.snap()
    finals = None
    for c in range(NCORE):
        pl.restore(base)
        pl.seg(f"sw{c}")
        first = [("ck", 1), ("cv", 1), t_zsb0]
        for s_ in range(NSLOT):
            b = slot_block(c, s_)
            if b is not None:
                r = owner(b)
                tw = owner_tt(b)
                pl.op("y", lambda E, s_=s_, r=r, tw=tw: E["y"].dma_start(
                    out=keff.ap()[s_ * 1024:(s_ + 1) * 1024, :],
                    in_=kdst.ap()[r * 2048 + tw * 1024: r * 2048 + tw * 1024 + 1024, :]),
                    deps=first, dma=True)
                for g_ in range(8):
                    so = r * (1024 * 1024) + (g_ * 8 + tw * 4) * 16384
                    do = (s_ * 8 + g_) * 65536
                    pl.op("y", lambda E, so=so, do=do: E["y"].dma_start(
                        out=veff_f[do:do + 65536],
                        in_=vdst_f[so:so + 65536]), deps=first, dma=True)
            else:
                pl.op("y", lambda E, s_=s_: E["y"].dma_start(
                    out=keff_f[s_ * 524288: (s_ + 1) * 524288].rearrange(
                        "(p f) -> p f", p=128),
                    in_=zsb0[:, :]), deps=first, dma=True)
                pl.op("y", lambda E, s_=s_: E["y"].dma_start(
                    out=veff_f[s_ * 524288: (s_ + 1) * 524288].rearrange(
                        "(p f) -> p f", p=128),
                    in_=zsb0[:, :]), deps=first, dma=True)
            first = [t_zsb0]
        f = pl.snap()
        if finals is None:
            finals = f[0]
        else:
            assert finals == f[0], f"switch body sem counts differ: {finals} vs {f[0]}"
    pl.restore(base)
    pl.cnt = dict(finals)
    t_relay = pl.tkt("y")
    t_relay2 = pl.tkt("rs")

    pl.seg("main1")
    pl.barrier()

    if "ph1" in dump:
        _dump(nc, pl, {"d_xn": AT, "d_q": CT}, {"d_ksrc": ksrc, "d_vsrc": vsrc})

    regT = nc.alloc_register(mybir.EngineType.PE, "fT")
    regS = nc.alloc_register(mybir.EngineType.Activation, "fS")
    fregs = [nc.alloc_register(mybir.EngineType.PE, f"sfl{i}")
             for i in range(NSLOT)]
    for s_ in range(NSLOT):
        pl.op("t", lambda E, s_=s_: E["t"].reg_load(fregs[s_], ifl[0:1, s_:s_ + 1]),
              deps=[t_tables])
    cc = 0
    exp_t = [None, None]
    att_use = [None, None]
    kb_t = [None, None]
    vb_t = [None, None]
    kb_use = [None, None]
    vb_use = [None, None]
    avev = {0: None, 1: None}
    rn_use = {0: None, 1: None}
    for g_ in range(8):
        for pr in range(2):
            for sub, srange, qoff in ((0, [7] + list(range(0, 7)), 0),
                                      (1, [23] + list(range(8, 23)), TN)):
                s0 = srange[0]
                s1 = srange[-1]
                def emit_avsums(p):
                    tk_av = None
                    for h2 in range(2):
                        avb = pC if h2 == 0 else pD
                        smb = pE if h2 == 0 else pF
                        pl.op("t", lambda E, lp=p["lp"], j=p["j"], ab=p["ab"], h2=h2,
                              avb=avb, fm=p["fm"]: E["t"].matmul(
                                  avb[:, :], vb[lp][:, j * 128:(j + 1) * 128],
                                  attb[ab][:, h2 * 512:h2 * 512 + 512],
                                  start=fm, stop=True, skip_group_check=True),
                              deps=[p["msk"], p["vbt"], avev[h2]], tick=False)
                        tk_av = pl.op("t", lambda E, s_=p["s"], ab=p["ab"], h2=h2,
                                      smb=smb, fm=p["fm"]: E["t"].matmul(
                                          smb[:, :],
                                          sflags[:, s_ * 128:(s_ + 1) * 128],
                                          attb[ab][:, h2 * 512:h2 * 512 + 512],
                                          start=fm, stop=True,
                                          skip_group_check=True),
                                      deps=[], tick=(h2 == 1))
                    att_use[p["ab"]] = tk_av
                    kb_use[p["lp"]] = tk_av
                    vb_use[p["lp"]] = tk_av

                pending = []
                cur_if = [None]

                def set_if(slot):
                    if cur_if[0] == slot:
                        return
                    if cur_if[0] is not None:
                        pl.pif_end()
                    if slot is not None:
                        pl.pif_begin(slot)
                    cur_if[0] = slot

                def pop_pending():
                    p = pending.pop(0)
                    set_if(p["ifs"])
                    emit_avsums(p)

                for s_ in srange:
                    lp = s_ % 2
                    kb_t[lp] = pl.op("y", lambda E, s_=s_, g_=g_, lp=lp: E["y"].dma_start(
                        out=kb[lp][:, :],
                        in_=keff.ap()[s_ * 1024 + g_ * 128: s_ * 1024 + g_ * 128 + 128, :]),
                        deps=[t_relay, t_relay2, kb_use[lp]], dma=True, sem=f"kv{lp}")
                    vb_t[lp] = pl.op("y", lambda E, s_=s_, g_=g_, lp=lp: E["y"].dma_start(
                        out=vb[lp].ap().rearrange("p (j d) -> p j d", j=4),
                        in_=veff.ap()[(s_ * 8 + g_) * 512: (s_ * 8 + g_) * 512 + 512, :]
                        .rearrange("(j p) d -> p j d", p=128)),
                        deps=[t_relay, t_relay2, vb_use[lp]], dma=True, sem=f"kv{lp}")
                    diag = s_ in (7, 23)
                    myif = None if diag else s_
                    for j in range(4):
                        sc = cc % 2
                        scb = pA if sc == 0 else pB
                        set_if(myif)
                        tk_sc = None
                        for h2 in range(2):
                            h = g_ * 4 + pr * 2 + h2
                            tk_sc = pl.op("t", lambda E, lp=lp, j=j, h=h, h2=h2, scb=scb, qoff=qoff:
                                          E["t"].matmul(
                                              scb[:, h2 * 512:h2 * 512 + 512],
                                              kb[lp][:, j * 128:(j + 1) * 128],
                                              CT[:, h * TOK + qoff: h * TOK + qoff + TN],
                                              start=True, stop=True),
                                          deps=[kb_t[lp], exp_t[sc]],
                                          tick=(h2 == 1))
                        ab = cc % 2
                        tk_ex = pl.op("s", lambda E, ab=ab, scb=scb: E["s"].activation(
                            attb[ab][:, :], scb[:, :], AF.Exp,
                            bias=cm12[:, 0:1], scale=QKS),
                            deps=[tk_sc, att_use[ab], t_cm12], tick=True)
                        exp_t[sc] = tk_ex
                        tk_msk = tk_ex
                        if diag:
                            tm1 = pl.op("g", lambda E, ab=ab, j=j: E["g"].tensor_tensor(
                                attb[ab][:, 0:512], attb[ab][:, 0:512],
                                dmask[:, j * TN:(j + 1) * TN], OP.mult),
                                deps=[tk_ex, t_tables], tick=True)
                            tk_msk = pl.op("g", lambda E, ab=ab, j=j: E["g"].tensor_tensor(
                                attb[ab][:, 512:1024], attb[ab][:, 512:1024],
                                dmask[:, j * TN:(j + 1) * TN], OP.mult),
                                deps=[tm1], tick=True)
                        pending.append(dict(lp=lp, j=j, ab=ab, msk=tk_msk,
                                            vbt=vb_t[lp], s=s_, ifs=myif,
                                            fm=(s_ == s0 and j == 0)))
                        if len(pending) > 1:
                            pop_pending()
                        cc += 1
                while pending:
                    pop_pending()
                set_if(None)
                tk_raw = []
                for h2 in range(2):
                    avb = pC if h2 == 0 else pD
                    smb = pE if h2 == 0 else pF
                    c1 = pl.op("v", lambda E, h2=h2, avb=avb: E["v"].tensor_copy(
                        avraw[h2][:, :], avb[:, :]),
                        deps=[pl.tkt("t"), avev[h2], rn_use[h2]], tick=True)
                    c2 = pl.op("v", lambda E, h2=h2, smb=smb: E["v"].tensor_copy(
                        smraw[h2][:, :], smb[:, :]), deps=[c1], tick=True)
                    avev[h2] = c2
                    tk_raw.append(c2)
                for h2 in range(2):
                    h = g_ * 4 + pr * 2 + h2
                    tr = pl.op("v", lambda E, h2=h2: E["v"].reciprocal(
                        rinv[h2][:, :], smraw[h2][:, :]),
                        deps=[tk_raw[h2], rn_use[h2]], tick=True)
                    rn_use[h2] = pl.op("v", lambda E, h2=h2, h=h, qoff=qoff:
                                       E["v"].tensor_tensor(
                                           BT[:, h * TOK + qoff: h * TOK + qoff + TN],
                                           avraw[h2][:, :], rinv[h2][:, :], OP.mult),
                                       deps=[tr], tick=True)

    w3ld = [None, None]
    w3use = [None, None]
    for nb in range(2):
        wsrc = wo[:, nb * 128:(nb + 1) * 128].rearrange("(c p) o -> p c o", p=128)
        w3ld[nb] = pl.op("g", lambda E, nb=nb, wsrc=wsrc: E["g"].dma_start(
            out=W3[nb].ap().rearrange("p (c o) -> p c o", c=FC), in_=wsrc),
            deps=[], dma=True, sem=f"w{nb}")
    pl.barrier()
    if "ph2" in dump:
        _dump(nc, pl, {"d_av": BT}, {})

    pc3 = 0
    bank_ev3 = {}
    xt_t = [None, None]
    xt_use = [None, None]
    xp_use = [None, None]
    xpb_use = [None, None]
    last_xpb = None
    for nb in range(FC):
        par = nb % 2
        if nb >= 2:
            wsrc = wo[:, nb * 128:(nb + 1) * 128].rearrange("(c p) o -> p c o", p=128)
            w3ld[par] = pl.op("g", lambda E, par=par, wsrc=wsrc: E["g"].dma_start(
                out=W3[par].ap().rearrange("p (c o) -> p c o", c=FC), in_=wsrc),
                deps=[w3use[par]], dma=True, sem=f"w{par}")
        for tt in range(2):
            pcp = pc3 % 2
            bank = pC if pcp == 0 else pD
            xpar = pc3 % 2
            xt_t[xpar] = pl.op("y", lambda E, nb=nb, tt=tt, xpar=xpar: E["y"].dma_start(
                out=xtile[xpar][:, :],
                in_=xT[nb * 128:(nb + 1) * 128, tt * TN: tt * TN + TN]),
                deps=[xt_use[xpar]], dma=True, sem=f"io{xpar}")
            tk_mm = None
            for dc in range(FC):
                tk_mm = pl.op("t", lambda E, dc=dc, par=par, tt=tt, bank=bank:
                              E["t"].matmul(
                                  bank[:, :], W3[par][:, dc * 128:(dc + 1) * 128],
                                  BT[:, dc * TOK + tt * TN: dc * TOK + tt * TN + TN],
                                  start=(dc == 0), stop=(dc == FC - 1)),
                              deps=[w3ld[par], bank_ev3.get(pcp)],
                              tick=(dc == FC - 1))
            w3use[par] = tk_mm
            ta = pl.op("v", lambda E, xpar=xpar, bank=bank: E["v"].tensor_tensor(
                xpf[xpar][:, :], bank[:, :], xtile[xpar][:, :], OP.add),
                deps=[tk_mm, xt_t[xpar], xp_use[xpar], xpb_use[xpar]], tick=True)
            bank_ev3[pcp] = ta
            xt_use[xpar] = ta
            ts_ = pl.op("y", lambda E, xpar=xpar, nb=nb, tt=tt: E["y"].dma_start(
                out=xpscr.ap()[nb * 128:(nb + 1) * 128, tt * TN: tt * TN + TN],
                in_=xpf[xpar][:, :]), deps=[ta], dma=True, sem=f"io{xpar}")
            last_xpb = pl.op("v", lambda E, xpar=xpar, nb=nb, tt=tt: E["v"].tensor_copy(
                CT[:, nb * TOK + tt * TN: nb * TOK + tt * TN + TN],
                xpf[xpar][:, :]), deps=[ta], tick=True)
            xp_use[xpar] = ts_
            xpb_use[xpar] = last_xpb
            pc3 += 1
    gate_io = [pl.tkt("io0"), pl.tkt("io1")]
    pl.barrier()
    ln2 = []
    for tt in range(2):
        ln2.append(emit_ln(CT[:, :], tt, last_xpb, m3, r3, var3, sd3, x2b3, lntmp3))
    if "ph3" in dump:
        _dump(nc, pl, {"d_xn2": CT}, {"d_xp": xpscr})

    pl.op("g", lambda E: E["g"].dma_start(
        out=aaugs.ap().rearrange("p (c e) -> p c e", c=FC),
        in_=aaugT.rearrange("(c p) e -> p c e", p=128)), dma=True, sem="gx")
    pl.op("g", lambda E: E["g"].dma_start(out=balls[:, :], in_=ballT),
          dma=True, sem="gx")
    t_eb = pl.op("g", lambda E: E["g"].dma_start(out=biass[:, :], in_=ebias),
                 dma=True, sem="gx")
    t_aa = t_bb = t_eb

    zmk_t = [None, None]
    prev_zmk = None
    prev_ttp = None
    prev_ttp2 = None
    pc4 = 0
    w4ld = [None, None]
    w4use = [None, None]
    bank_ev4 = {}
    xp_t = [None, None]
    out_use = [None, None]
    y_store = []

    def emit_w4(nb):
        par = nb % 2
        wsrc = wup[:, nb * 128:(nb + 1) * 128].rearrange("(c p) o -> p c o", p=128)
        w4ld[par] = pl.op("g", lambda E, par=par, wsrc=wsrc: E["g"].dma_start(
            out=W3[par].ap().rearrange("p (c o) -> p c o", c=FC), in_=wsrc),
            deps=[w4use[par]], dma=True, sem=f"w{par}")

    def emit_base(nb, tt):
        nonlocal pc4
        par = nb % 2
        pcp = pc4 % 2
        bank = pC if pcp == 0 else pD
        xpar = pc4 % 2
        xp_t[xpar] = pl.op("y", lambda E, nb=nb, tt=tt, xpar=xpar: E["y"].dma_start(
            out=xtile[xpar][:, :],
            in_=xpscr.ap()[nb * 128:(nb + 1) * 128, tt * TN: tt * TN + TN]),
            deps=[out_use[xpar], gate_io[xpar]], dma=True, sem=f"io{xpar}")
        for fc in range(FC):
            pl.op("t", lambda E, fc=fc, par=par, tt=tt, bank=bank: E["t"].matmul(
                bank[:, :], W3[par][:, fc * 128:(fc + 1) * 128],
                CT[:, fc * TOK + tt * TN: fc * TOK + tt * TN + TN],
                start=(fc == 0), stop=False, skip_group_check=True),
                deps=[w4ld[par], ln2[tt], bank_ev4.get(pcp)], tick=False)
        pc4 += 1
        return bank, xpar, pcp

    def emit_delta_evac(nb, tt, bank, xpar, pcp):
        tk_mm = pl.op("t", lambda E, nb=nb, tt=tt, bank=bank: E["t"].matmul(
            bank[:, :], balls[:, nb * 128:(nb + 1) * 128], zmk[tt][:, :],
            start=False, stop=True, skip_group_check=True),
            deps=[t_bb, zmk_t[tt]], tick=True)
        w4use[nb % 2] = tk_mm
        ta = pl.op("v", lambda E, xpar=xpar, bank=bank: E["v"].tensor_tensor(
            xpf[xpar][:, :], bank[:, :], xtile[xpar][:, :], OP.add),
            deps=[tk_mm, xp_t[xpar], out_use[xpar]], tick=True)
        bank_ev4[pcp] = ta
        ty = pl.op("y", lambda E, xpar=xpar, nb=nb, tt=tt: E["y"].dma_start(
            out=y[nb * 128:(nb + 1) * 128, tt * TN: tt * TN + TN],
            in_=xpf[xpar][:, :]), deps=[ta], dma=True)
        out_use[xpar] = ty
        y_store.append(ty)

    emit_w4(0)
    for tt in range(2):
        tk_z = None
        prev_v4 = pl.tkt("v")
        for fc in range(FC):
            tk_z = pl.op("t", lambda E, fc=fc, tt=tt: E["t"].matmul(
                pA[0:36, 0:512], aaugs[:, fc * 36:(fc + 1) * 36],
                CT[:, fc * TOK + tt * TN: fc * TOK + tt * TN + TN],
                start=(fc == 0), stop=(fc == FC - 1)),
                deps=[t_aa, ln2[tt], prev_v4, prev_ttp], tick=(fc == FC - 1))
        t_zf = pl.op("v", lambda E: E["v"].tensor_copy(zsb[:, :], pA[0:36, 0:512]),
                     deps=[tk_z, prev_zmk], tick=True)
        t_lg = pl.op("v", lambda E: E["v"].tensor_scalar(
            lgsb[:, :], pA[32:36, 0:512], biass[32:36, 0:1], 0.0, OP.add),
            deps=[tk_z, t_eb, t_zf, prev_ttp], tick=True)
        bk0, xp0, pp0 = emit_base(0, tt)
        t_tp = None
        for ch in range(4):
            t_tp = pl.op("t", lambda E, ch=ch: E["t"].transpose(
                pA[:, 512 + ch * 4: 512 + ch * 4 + 4],
                lgsb[:, ch * 128:(ch + 1) * 128], identf[0:4, 0:4]),
                deps=[t_lg], tick=(ch == 3))
        t_lgt = pl.op("v", lambda E: E["v"].tensor_copy(lgt[:, :], pA[:, 512:528]),
                      deps=[t_tp, prev_ttp2], tick=True)
        last = t_lgt
        for ch in range(4):
            last = pl.op("v", lambda E, ch=ch: E["v"].reduce_max(
                m4[:, ch:ch + 1], lgt[:, ch * 4:(ch + 1) * 4], axis=AX.X),
                deps=[last], tick=True)
            last = pl.op("v", lambda E, ch=ch: E["v"].tensor_tensor(
                ohtm[:, ch * 4: ch * 4 + 1], lgt[:, ch * 4: ch * 4 + 1],
                m4[:, ch:ch + 1], OP.is_ge), deps=[last], tick=True)
            last = pl.op("v", lambda E, ch=ch: E["v"].tensor_copy(
                tb1[:, 0:1], ohtm[:, ch * 4: ch * 4 + 1]), deps=[last], tick=True)
            for e_ in range(1, 4):
                last = pl.op("v", lambda E, ch=ch, e_=e_: E["v"].tensor_tensor(
                    tb2[:, 0:1], lgt[:, ch * 4 + e_: ch * 4 + e_ + 1],
                    m4[:, ch:ch + 1], OP.is_ge), deps=[last], tick=True)
                last = pl.op("v", lambda E: E["v"].tensor_scalar(
                    tb2[:, 1:2], tb1[:, 0:1], -1.0, 1.0, OP.mult, OP.add),
                    deps=[last], tick=True)
                last = pl.op("v", lambda E, ch=ch, e_=e_: E["v"].tensor_tensor(
                    ohtm[:, ch * 4 + e_: ch * 4 + e_ + 1], tb2[:, 0:1],
                    tb2[:, 1:2], OP.mult), deps=[last], tick=True)
                last = pl.op("v", lambda E, ch=ch, e_=e_: E["v"].tensor_tensor(
                    tb1[:, 0:1], tb1[:, 0:1],
                    ohtm[:, ch * 4 + e_: ch * 4 + e_ + 1], OP.add),
                    deps=[last], tick=True)
        t_tp2 = None
        for ch in range(4):
            t_tp2 = pl.op("t", lambda E, ch=ch: E["t"].transpose(
                pB[0:4, 512 + ch * 128: 512 + ch * 128 + 128],
                ohtm[:, ch * 4:(ch + 1) * 4], identf[:, :]),
                deps=[last], tick=(ch == 3))
        t_oh = pl.op("v", lambda E: E["v"].tensor_copy(
            ohsb[:, :], pB[0:4, 512:1024]), deps=[t_tp2], tick=True)
        t_ohx = pl.op("t", lambda E: E["t"].matmul(
            pB[0:32, 0:512], e32[:, :], ohsb[:, :], start=True, stop=True),
            deps=[t_oh], tick=True)
        zmk_t[tt] = pl.op("v", lambda E, tt=tt: E["v"].tensor_tensor(
            zmk[tt][:, :], zsb[0:32, :], pB[0:32, 0:512], OP.mult),
            deps=[t_ohx, t_zf], tick=True)
        prev_zmk = zmk_t[tt]
        prev_ttp = t_tp
        prev_ttp2 = t_tp2
        emit_delta_evac(0, tt, bk0, xp0, pp0)

    for nb in range(1, FC):
        emit_w4(nb)
        for tt in range(2):
            bk, xp_, pp = emit_base(nb, tt)
            emit_delta_evac(nb, tt, bk, xp_, pp)

    if "ph4" in dump:
        pl.barrier()
        for nm, tns, shp, dt in (("d_zmk0", zmk[0], [32, TN], BF16),
                                 ("d_zmk1", zmk[1], [32, TN], BF16),
                                 ("d_zsb", zsb, [36, TN], F32),
                                 ("d_ohsb", ohsb, [4, TN], BF16),
                                 ("d_lgt", lgt, [128, 16], F32)):
            dap = nc.dram_tensor(nm, shp, dt, kind="ExternalOutput").ap()
            pl.op("y", lambda E, dap=dap, tns=tns: E["y"].dma_start(
                out=dap, in_=tns[:, :]), dma=True, sem="dbg")

    pl.barrier()

    SEMNAMES = ["t", "v", "s", "g", "y", "ck", "cv", "gx", "gy", "gz", "gw",
                "w0", "w1", "kv0", "kv1", "io0", "io1", "dbg", "rs"]
    with ExitStack() as st:
        SEMS = {n: st.enter_context(nc.semaphore(n)) for n in SEMNAMES}
        block = st.enter_context(nc.Block())
        E = {"t": nc.tensor, "v": nc.vector, "s": nc.scalar,
             "g": nc.gpsimd, "y": nc.sync}

        def replay_one(r):
            e = E[r["eng"]]
            for (sm, v) in r["waits"]:
                e.wait_ge(SEMS[sm], v)
            if r["fn"] is not None:
                ins = r["fn"](E)
                if r["inc"] is not None:
                    sm, n, raw = r["inc"]
                    if raw:
                        ins.then_inc(SEMS[sm])
                    else:
                        ins.then_inc(SEMS[sm], n)

        def replay(recs):
            i = 0
            while i < len(recs):
                r = recs[i]
                if r.get("ctl") == "pif_begin":
                    slot = r["slot"]
                    j = i + 1
                    body = []
                    while recs[j].get("ctl") != "pif_end":
                        body.append(recs[j])
                        j += 1
                    delta = recs[j]["delta"]
                    for rr in body:
                        if rr["eng"] != "t":
                            replay_one(rr)
                    e = E["t"]
                    with e.If_ne(fregs[slot], 0):
                        for rr in body:
                            if rr["eng"] == "t":
                                replay_one(rr)
                    with e.Else():
                        if delta:
                            e.sem_inc(SEMS["t"], delta)
                    e.end_ifs()
                    i = j + 1
                    continue
                if r.get("ctl") == "sw2_begin":
                    slot = r["slot"]
                    j = i + 1
                    body = []
                    while recs[j].get("ctl") != "sw2_end":
                        body.append(recs[j])
                        j += 1
                    deltas = recs[j]["deltas"]
                    assert all(rr["eng"] in ("t", "s") for rr in body)
                    for ename, reg in (("t", regT), ("s", regS)):
                        e = E[ename]
                        e.reg_load(reg, ifl[0:1, slot:slot + 1])
                        with e.If_ne(reg, 0):
                            for rr in body:
                                if rr["eng"] == ename:
                                    replay_one(rr)
                        with e.Else():
                            if deltas.get(ename):
                                e.sem_inc(SEMS[ename], deltas[ename])
                        e.end_ifs()
                    i = j + 1
                    continue
                replay_one(r)
                i += 1

        replay(pl.segs["main0"])
        pid = nc.partition_id()
        for i in nc.Switch(engines=[nc.sync, nc.scalar], index=[pid, pid], n=NCORE):
            replay(pl.segs[f"sw{i}"])
        replay(pl.segs["main1"])

    return nc


def _dump(nc, pl, sbuf_map, dram_map):
    pl.barrier()
    for name, t in sbuf_map.items():
        d = nc.dram_tensor(name, [128, FC * TOK], BF16, kind="ExternalOutput").ap()
        pl.op("y", lambda E, d=d, t=t: E["y"].dma_start(out=d, in_=t[:, :]),
              dma=True, sem="dbg")
    for name, t in dram_map.items():
        h = t.ap()
        d = nc.dram_tensor(name, list(h.shape), BF16 if name != "d_xp" else F32,
                           kind="ExternalOutput").ap()
        pl.op("y", lambda E, d=d, h=h: E["y"].dma_start(out=d, in_=h),
              dma=True, sem="dbg")
    pl.barrier()


def _prep_inputs(L0_attn_x, L0_attn_wq, L0_attn_wk, L0_attn_wv, L0_attn_wo,
                 L0_ffn_W_up, L0_ffn_gate_w, L0_ffn_expert_bias,
                 L0_ffn_lora_A, L0_ffn_lora_B):
    f32 = lambda a: np.ascontiguousarray(np.asarray(a), dtype=np.float32)
    x = f32(L0_attn_x)[0]
    wq = f32(L0_attn_wq)
    wk = f32(L0_attn_wk)
    wv = f32(L0_attn_wv)
    wo = f32(L0_attn_wo)
    wup = np.ascontiguousarray(f32(L0_ffn_W_up).T)
    gate = f32(L0_ffn_gate_w)
    ebias_ = f32(L0_ffn_expert_bias)
    la = f32(L0_ffn_lora_A)
    lb = f32(L0_ffn_lora_B)

    aaug = np.concatenate([la.reshape(32, D), gate], axis=0)
    aaugT = np.ascontiguousarray(aaug.T)
    ballT = np.ascontiguousarray(lb.transpose(0, 2, 1).reshape(32, D))
    eb36 = np.zeros((36, 1), np.float32)
    eb36[32:36, 0] = ebias_

    identf = np.eye(128, dtype=np.float32)
    onesb = np.ones((128, 128), dtype=BF)
    psw = np.zeros((128, 128), np.float32)
    for k in range(128):
        psw[k, k ^ 1] = 1.0
    pswap = psw.astype(BF)
    e32 = np.zeros((4, 32), np.float32)
    for e_ in range(4):
        e32[e_, e_ * 8:(e_ + 1) * 8] = 1.0
    e32 = e32.astype(BF)
    kk = np.arange(128)[:, None]
    qq = np.arange(TN)[None, :]
    dm = np.zeros((128, 4 * TN), np.float32)
    for j in range(4):
        dm[:, j * TN:(j + 1) * TN] = (qq >= 128 * j + kk).astype(np.float32)
    dmask = dm.astype(BF)

    inv = 1.0 / (10000.0 ** (np.arange(0, HD, 2, dtype=np.float64) / HD))

    in_maps = []
    for c in range(NCORE):
        bA, bB = c, 15 - c
        toks = np.concatenate([np.arange(bA * TN, (bA + 1) * TN),
                               np.arange(bB * TN, (bB + 1) * TN)])
        xc = np.ascontiguousarray(
            np.concatenate([x[bA * TN:(bA + 1) * TN], x[bB * TN:(bB + 1) * TN]], 0).T)
        ang = inv[:, None] * toks[None, :].astype(np.float64)
        cosr = np.cos(ang)
        sinr = np.sin(ang)
        cosT = np.repeat(cosr, 2, axis=0).astype(np.float32)
        ssinT = np.repeat(sinr, 2, axis=0).astype(np.float32)
        ssinT[0::2, :] *= -1.0
        sf = np.zeros((128, NSLOT * 128), np.float32)
        ifl_np = np.zeros((1, 32), np.int32)
        for s_ in range(NSLOT):
            if slot_block(c, s_) is not None:
                sf[:, s_ * 128:(s_ + 1) * 128] = 1.0
                ifl_np[0, s_] = 1
        in_maps.append({
            "xT": xc, "wq": wq, "wk": wk, "wv": wv, "wo": wo, "wup": wup,
            "aaugT": aaugT, "ballT": ballT, "ebias": eb36,
            "cosT": cosT, "ssinT": ssinT, "identf": identf,
            "onesb": onesb, "pswap": pswap, "e32": e32,
            "dmask": dmask, "sflags": sf.astype(BF), "iflags": ifl_np,
        })
    return in_maps


def kernel(L0_attn_x, L0_attn_wq, L0_attn_wk, L0_attn_wv, L0_attn_wo,
           L0_ffn_W_up, L0_ffn_gate_w, L0_ffn_expert_bias,
           L0_ffn_lora_A, L0_ffn_lora_B):
    in_maps = _prep_inputs(L0_attn_x, L0_attn_wq, L0_attn_wk, L0_attn_wv,
                           L0_attn_wo, L0_ffn_W_up, L0_ffn_gate_w,
                           L0_ffn_expert_bias, L0_ffn_lora_A, L0_ffn_lora_B)
    nc = build()
    res = run_bass_kernel_spmd(nc, in_maps, core_ids=list(range(NCORE)))
    out = np.empty((1, S, D), np.float32)
    for c in range(NCORE):
        yc = np.asarray(res.results[c]["y"])
        bA, bB = c, 15 - c
        out[0, bA * TN:(bA + 1) * TN] = yc[:, 0:TN].T
        out[0, bB * TN:(bB + 1) * TN] = yc[:, TN:2 * TN].T
    return out


# version 17
# speedup vs baseline: 1.0412x; 1.0412x over previous
import numpy as np
import ml_dtypes
from contextlib import ExitStack

import concourse.bass as bass
import concourse.mybir as mybir
from concourse.bass_utils import run_bass_kernel_spmd

F32 = mybir.dt.float32
BF16 = mybir.dt.bfloat16
AF = mybir.ActivationFunctionType
OP = mybir.AluOpType
AX = mybir.AxisListType
BF = ml_dtypes.bfloat16

D = 4096
S = 8192
NH = 32
NKV = 8
HD = 128
FC = 32
TOK = 1024
TN = 512
NB = 16
NCORE = 8
EPS = 1e-5
CSH = 12.0
QKS = 1.0 / np.sqrt(128.0)
NSLOT = 24

ENGS = ("t", "v", "s", "g", "y")


def owner(b):
    return b if b < 8 else 15 - b


def owner_tt(b):
    return 0 if b < 8 else 1


def slot_block(c, s):
    if s < 7:
        return s if s < c else None
    if s == 7:
        return c
    if s < 23:
        return (s - 8) if (s - 8) < 15 - c else None
    return 15 - c


class Plan:
    def __init__(self):
        self.segs = {}
        self.cur = None
        self.cnt = {}
        self.lastw = {e: {} for e in ENGS}
        self.last_op = {}

    def seg(self, name):
        self.cur = self.segs.setdefault(name, [])

    def snap(self):
        import copy
        return (dict(self.cnt), copy.deepcopy(self.lastw), dict(self.last_op))

    def restore(self, st):
        import copy
        self.cnt = dict(st[0])
        self.lastw = copy.deepcopy(st[1])
        self.last_op = dict(st[2])

    def op(self, eng, fn, deps=(), tick=False, dma=False, sem=None, rawinc=False):
        waits = []
        for d in deps:
            if d is None:
                continue
            s, v = d
            if v <= 0:
                continue
            if self.lastw[eng].get(s, -1) >= v:
                continue
            self.lastw[eng][s] = v
            waits.append((s, v))
        inc = None
        if dma:
            tick = True
        if tick:
            sm = sem or eng
            n = 16 if dma else 1
            self.cnt[sm] = self.cnt.get(sm, 0) + n
            inc = (sm, n, rawinc)
        rec = dict(eng=eng, fn=fn, waits=waits, inc=inc)
        self.cur.append(rec)
        if fn is not None:
            self.last_op[eng] = rec
        if inc:
            return (inc[0], self.cnt[inc[0]])
        return None

    def tkt(self, sm):
        return (sm, self.cnt.get(sm, 0))

    def ensure_tick(self, eng):
        rec = self.last_op.get(eng)
        if rec is not None and rec["inc"] is None:
            self.cnt[eng] = self.cnt.get(eng, 0) + 1
            rec["inc"] = (eng, 1, False)
        return (eng, self.cnt.get(eng, 0))

    DMASEMS = ("y", "gx", "gy", "gz", "gw", "w0", "w1", "kv0", "kv1",
               "io0", "io1", "dbg", "rs")

    def sw2_begin(self, slot):
        self.cur.append(dict(eng=None, ctl="sw2_begin", slot=slot, fn=None))
        import copy
        self._sw2_lastw = copy.deepcopy(self.lastw)
        self._sw2_cnt = dict(self.cnt)

    def sw2_end(self):
        import copy
        deltas = {}
        for sm in ("t", "s"):
            d = self.cnt.get(sm, 0) - self._sw2_cnt.get(sm, 0)
            if d:
                deltas[sm] = d
        self.lastw = copy.deepcopy(self._sw2_lastw)
        self.cur.append(dict(eng=None, ctl="sw2_end", deltas=deltas, fn=None))

    def barrier(self):
        tks = {e: self.ensure_tick(e) for e in ENGS}
        dts = [self.tkt(s) for s in self.DMASEMS]
        for e in ENGS:
            self.op(e, None, deps=[tks[o] for o in ENGS if o != e] + dts)


def build(dump=()):
    nc = bass.Bass("TRN2", target_bir_lowering=False, debug=False)

    xT = nc.dram_tensor("xT", [D, TOK], F32, kind="ExternalInput").ap()
    wq = nc.dram_tensor("wq", [D, D], F32, kind="ExternalInput").ap()
    wk = nc.dram_tensor("wk", [D, 1024], F32, kind="ExternalInput").ap()
    wv = nc.dram_tensor("wv", [D, 1024], F32, kind="ExternalInput").ap()
    wo = nc.dram_tensor("wo", [D, D], F32, kind="ExternalInput").ap()
    wup = nc.dram_tensor("wup", [D, D], F32, kind="ExternalInput").ap()
    aaugT = nc.dram_tensor("aaugT", [D, 36], F32, kind="ExternalInput").ap()
    ballT = nc.dram_tensor("ballT", [32, D], F32, kind="ExternalInput").ap()
    ebias = nc.dram_tensor("ebias", [36, 1], F32, kind="ExternalInput").ap()
    cosT_d = nc.dram_tensor("cosT", [128, TOK], F32, kind="ExternalInput").ap()
    ssinT_d = nc.dram_tensor("ssinT", [128, TOK], F32, kind="ExternalInput").ap()
    identf_d = nc.dram_tensor("identf", [128, 128], F32, kind="ExternalInput").ap()
    onesb_d = nc.dram_tensor("onesb", [128, 128], BF16, kind="ExternalInput").ap()
    pswap_d = nc.dram_tensor("pswap", [128, 128], BF16, kind="ExternalInput").ap()
    e32_d = nc.dram_tensor("e32", [4, 32], BF16, kind="ExternalInput").ap()
    dmask_d = nc.dram_tensor("dmask", [128, 4 * TN], BF16, kind="ExternalInput").ap()
    sflags_d = nc.dram_tensor("sflags", [128, NSLOT * 128], BF16, kind="ExternalInput").ap()
    iflags_d = nc.dram_tensor("iflags", [1, 32], mybir.dt.int32, kind="ExternalInput").ap()
    y = nc.dram_tensor("y", [D, TOK], F32, kind="ExternalOutput").ap()

    ksrc = nc.dram_tensor("ksrc", [2 * 1024, 512], BF16, kind="Internal")
    vsrc = nc.dram_tensor("vsrc", [1024, 1024], BF16, kind="Internal")
    kdst = nc.dram_tensor("kdst", [NCORE * 2 * 1024, 512], BF16, kind="Internal",
                          addr_space="Shared")
    vdst = nc.dram_tensor("vdst", [NCORE * 1024, 1024], BF16, kind="Internal",
                          addr_space="Shared")
    keff = nc.dram_tensor("keff", [NSLOT * 1024, 512], BF16, kind="Internal")
    veff = nc.dram_tensor("veff", [NSLOT * 8 * 4 * 128, 128], BF16, kind="Internal")
    xpscr = nc.dram_tensor("xpscr", [D, TOK], F32, kind="Internal")

    kdst_f = kdst.ap().rearrange("a b -> (a b)")
    vdst_f = vdst.ap().rearrange("a b -> (a b)")
    keff_f = keff.ap().rearrange("a b -> (a b)")
    veff_f = veff.ap().rearrange("a b -> (a b)")
    vsrc_f = vsrc.ap().rearrange("a b -> (a b)")

    SB0 = 20480

    def at(name, shape, dtype, off):
        return nc.alloc_sbuf_tensor_at(name, shape, dtype, offset=SB0 + off)

    AT = at("A_xn", [128, FC * TOK], BF16, 0)
    BT = at("B_av", [128, FC * TOK], BF16, 65536)
    CT = at("C_q", [128, FC * TOK], BF16, 131072)

    identf = at("identf", [128, 128], F32, 196608)
    onesb = at("onesb", [128, 128], BF16, 197120)
    pswap = at("pswap", [128, 128], BF16, 197376)
    e32 = at("e32", [4, 32], BF16, 197632)
    dmask = at("dmask", [128, 4 * TN], BF16, 197696)
    sflags = at("sflags", [128, NSLOT * 128], BF16, 201792)
    cm12 = at("cm12", [128, 1], F32, 207936)
    ifl = at("ifl", [1, 32], mybir.dt.int32, 207968)

    W1 = [at(f"W1_{i}", [128, FC * 128], BF16, 65536 + i * 8192) for i in range(2)]
    cosT = at("cosT", [128, TOK], F32, 81920)
    ssinT = at("ssinT", [128, TOK], F32, 86016)
    m1 = at("m1", [128, TN], F32, 90112)
    r1 = at("r1", [128, TN], F32, 92160)
    var1 = at("var1", [128, TN], F32, 94208)
    sd1 = at("sd1", [128, TN], F32, 96256)
    x2b = [at(f"x2b_{i}", [128, TN], BF16, 98304 + i * 1024) for i in range(2)]
    lntmp = [at(f"lntmp_{i}", [128, TN], F32, 100352 + i * 2048) for i in range(2)]
    ktmp = [at(f"ktmp_{i}", [128, TN], BF16, 104448 + i * 1024) for i in range(2)]
    tR1 = [at(f"tR1_{i}", [128, TN], F32, 106496 + i * 2048) for i in range(2)]
    tR2 = [at(f"tR2_{i}", [128, TN], F32, 110592 + i * 2048) for i in range(2)]
    vtmp = [at(f"vtmp_{i}", [128, TN], F32, 114688 + i * 2048) for i in range(2)]
    vtt = [at(f"vtt_{i}", [128, 128], BF16, 118784 + i * 256) for i in range(2)]
    kout = [at(f"kout_{i}", [128, TN], BF16, 119296 + i * 1024) for i in range(2)]
    zsb0 = at("zsb0", [128, 4096], BF16, 121344)

    kb = [at(f"kb_{i}", [128, 512], BF16, 32768 + i * 1024) for i in range(2)]
    vb = [at(f"vb_{i}", [128, 512], BF16, 34816 + i * 1024) for i in range(2)]
    attb = [at(f"attb_{i}", [128, 1024], BF16, 36864 + i * 2048) for i in range(2)]
    rinv = [at(f"rinv_{i}", [128, 512], F32, 40960 + i * 2048) for i in range(2)]
    avraw = [at(f"avraw_{i}", [128, 512], F32, 45056 + i * 2048) for i in range(2)]
    smraw = [at(f"smraw_{i}", [128, 512], F32, 49152 + i * 2048) for i in range(2)]

    W3 = [at(f"W3_{i}", [128, FC * 128], BF16, 0 + i * 8192) for i in range(2)]
    xtile = [at(f"xtile_{i}", [128, TN], F32, 16384 + i * 2048) for i in range(2)]
    xpf = [at(f"xpf_{i}", [128, TN], F32, 20480 + i * 2048) for i in range(2)]
    m3 = at("m3", [128, TN], F32, 24576)
    r3 = at("r3", [128, TN], F32, 26624)
    var3 = at("var3", [128, TN], F32, 28672)
    sd3 = at("sd3", [128, TN], F32, 30720)
    x2b3 = [at(f"x2b3_{i}", [128, TN], BF16, 32768 + i * 1024) for i in range(2)]
    lntmp3 = [at(f"lntmp3_{i}", [128, TN], F32, 34816 + i * 2048) for i in range(2)]

    zsb = at("zsb", [36, TN], F32, 40960)
    zmk = [at(f"zmk_{i}", [32, TN], BF16, 43008 + i * 1024) for i in range(2)]
    lgsb = at("lgsb", [4, TN], F32, 45056)
    lgt = at("lgt", [128, 16], F32, 47104)
    m4 = at("m4", [128, 4], F32, 47232)
    ohtm = at("ohtm", [128, 16], F32, 47360)
    tb1 = at("tb1", [128, 4], F32, 47488)
    tb2 = at("tb2", [128, 4], F32, 47552)
    ohsb = at("ohsb", [4, TN], BF16, 47616)
    aaugs = at("aaugs", [128, FC * 36], BF16, 65536)
    balls = at("balls", [32, D], BF16, 68096)
    biass = at("biass", [36, 1], F32, 76288)

    pA = nc.alloc_psum_tensor("pA", [128, 1024], F32)
    pB = nc.alloc_psum_tensor("pB", [128, 1024], F32)
    pC = nc.alloc_psum_tensor("pC", [128, 512], F32)
    pD = nc.alloc_psum_tensor("pD", [128, 512], F32)
    pE = nc.alloc_psum_tensor("pE", [128, 512], F32)
    pF = nc.alloc_psum_tensor("pF", [128, 512], F32)

    pl = Plan()
    pl.seg("main0")

    tload = []
    for dst, src in ((identf, identf_d), (onesb, onesb_d), (pswap, pswap_d),
                     (e32, e32_d), (dmask, dmask_d), (sflags, sflags_d),
                     (cosT, cosT_d), (ssinT, ssinT_d), (ifl, iflags_d)):
        tload.append(pl.op("y", lambda E, dst=dst, src=src:
                           E["y"].dma_start(out=dst[:, :], in_=src), dma=True))
    t_tables = pl.tkt("y")
    t_zsb0 = pl.op("v", lambda E: E["v"].memset(zsb0[:, :], 0.0), tick=True)
    t_cm12 = pl.op("v", lambda E: E["v"].memset(cm12[:, :], -CSH), tick=True)

    x_src = xT.rearrange("(c p) t -> p c t", p=128)
    XSEM = ("gx", "gy", "gz", "gw")
    t_xq = []
    for q4 in range(4):
        t_xq.append(pl.op("g", lambda E, q4=q4: E["g"].dma_start(
            out=AT.ap().rearrange("p (c t) -> p c t", c=FC)[:, q4 * 8:(q4 + 1) * 8, :],
            in_=x_src[:, q4 * 8:(q4 + 1) * 8, :]),
            dma=True, sem=XSEM[q4]))
    wk_pre = [None, None]
    for nb in range(2):
        wsrc = wk[:, nb * 128:(nb + 1) * 128].rearrange("(c p) o -> p c o", p=128)
        wk_pre[nb] = pl.op("g", lambda E, nb=nb, wsrc=wsrc: E["g"].dma_start(
            out=W1[nb].ap().rearrange("p (c o) -> p c o", c=FC), in_=wsrc),
            deps=[], dma=True, sem=f"w{nb}")
    t_xl = [t_xq[3], t_xq[3]]

    def emit_ln(arena, tt, dep_in, mT, rT, varT, sdT, x2T, lnT, dep_extra=(),
                dep_fn=None):
        def sl(fc):
            return arena[:, fc * TOK + tt * TN: fc * TOK + tt * TN + TN]
        if dep_fn is None:
            dep_fn = lambda fc: dep_in
        prev_v = pl.tkt("v")
        prev_t = pl.tkt("t")
        tk_s1 = None
        for fc in range(FC):
            tk_s1 = pl.op("t", lambda E, fc=fc: E["t"].matmul(
                pE[:, :], onesb[:, :], sl(fc), start=(fc == 0), stop=(fc == FC - 1)),
                deps=[dep_fn(fc), t_tables, prev_v] + list(dep_extra),
                tick=(fc == FC - 1))
        sq_pe = [None, None]
        for fc in range(FC):
            par = fc % 2
            d = [dep_fn(fc), prev_t]
            if sq_pe[par] is not None:
                d.append(sq_pe[par])
            tk_sq = pl.op("s", lambda E, fc=fc, par=par: E["s"].activation(
                x2T[par][:, :], sl(fc), AF.Square), deps=d, tick=True)
            sq_pe[par] = pl.op("t", lambda E, par=par, fc=fc: E["t"].matmul(
                pF[:, :], onesb[:, :], x2T[par][:, :],
                start=(fc == 0), stop=(fc == FC - 1)),
                deps=[tk_sq, prev_v], tick=True)
        tk_s2 = sq_pe[(FC - 1) % 2]
        a = pl.op("v", lambda E: E["v"].tensor_scalar(
            mT[:, :], pE[:, :], 1.0 / D, 0.0, OP.mult), deps=[tk_s1], tick=True)
        b = pl.op("v", lambda E: E["v"].tensor_scalar(
            varT[:, :], pF[:, :], 1.0 / D, EPS, OP.mult, OP.add),
            deps=[tk_s2, a], tick=True)
        cgate = pl.op("v", lambda E: E["v"].tensor_tensor(
            sdT[:, :], mT[:, :], mT[:, :], OP.mult), deps=[b], tick=True)
        dd = pl.op("v", lambda E: E["v"].tensor_tensor(
            varT[:, :], varT[:, :], sdT[:, :], OP.subtract), deps=[cgate], tick=True)
        ee = pl.op("s", lambda E: E["s"].activation(
            sdT[:, :], varT[:, :], AF.Sqrt), deps=[dd], tick=True)
        ff = pl.op("v", lambda E: E["v"].reciprocal(rT[:, :], sdT[:, :]),
                   deps=[ee], tick=True)
        lastv = ff
        lastg = None
        for fc in range(FC):
            par = fc % 2
            eng = "v" if par == 0 else "g"
            prev = lastv if par == 0 else lastg
            g1 = pl.op(eng, lambda E, fc=fc, par=par, eng=eng: E[eng].tensor_tensor(
                lnT[par][:, :], sl(fc), mT[:, :], OP.subtract),
                deps=[prev, ff, tk_s2], tick=True)
            tkn = pl.op(eng, lambda E, fc=fc, par=par, eng=eng: E[eng].tensor_tensor(
                sl(fc), lnT[par][:, :], rT[:, :], OP.mult), deps=[g1], tick=True)
            if par == 0:
                lastv = tkn
            else:
                lastg = tkn
        last = pl.op("v", lambda E: E["v"].nop(), deps=[lastv, lastg], tick=True)
        return last

    ln1 = []
    for tt in range(2):
        ln1.append(emit_ln(AT[:, :], tt, t_xq[3], m1, r1, var1, sd1, x2b, lntmp,
                           dep_fn=lambda fc: t_xq[fc // 8]))

    mats = [("k", wk, 8), ("v", wv, 8), ("q", wq, 32)]
    pc = 0
    prc = 0
    wld = [None, None]
    wuse = [None, None]
    projev = {}
    pA_use = [None, None]
    ktmp_use = [None, None]
    vtmp_use = [None, None]
    vtt_use = [None, None]
    kout_use = [None, None]
    tR1_use = [None, None]
    tR2_use = [None, None]
    nbg = 0
    kv_store = []
    for kind, wmat, nnb in mats:
        for nb in range(nnb):
            par = nbg % 2
            wsrc = wmat[:, nb * 128:(nb + 1) * 128].rearrange(
                "(c p) o -> p c o", p=128)
            if kind == "k" and nb < 2:
                wld[par] = wk_pre[nb]
            else:
                wld[par] = pl.op("g", lambda E, par=par, wsrc=wsrc: E["g"].dma_start(
                    out=W1[par].ap().rearrange("p (c o) -> p c o", c=FC), in_=wsrc),
                    deps=[wuse[par]], dma=True, sem=f"w{par}")
            for tt in range(2):
                pcp = pc % 2
                bank = pC if pcp == 0 else pD
                tk_mm = None
                for fc in range(FC):
                    tk_mm = pl.op("t", lambda E, fc=fc, par=par, tt=tt, bank=bank:
                                  E["t"].matmul(
                                      bank[:, :],
                                      W1[par][:, fc * 128:(fc + 1) * 128],
                                      AT[:, fc * TOK + tt * TN: fc * TOK + tt * TN + TN],
                                      start=(fc == 0), stop=(fc == FC - 1)),
                                  deps=[wld[par], ln1[tt], projev.get(pcp)],
                                  tick=(fc == FC - 1))
                wuse[par] = tk_mm
                if kind in ("k", "q"):
                    kp = pc % 2
                    tcp = pl.op("s", lambda E, bank=bank, kp=kp: E["s"].copy(
                        ktmp[kp][:, :], bank[:, :]),
                        deps=[tk_mm, ktmp_use[kp]], tick=True)
                    projev[pcp] = tcp
                    sl_ = prc % 2
                    tpm = pl.op("t", lambda E, kp=kp, sl_=sl_: E["t"].matmul(
                        pA[:, sl_ * 512:sl_ * 512 + 512], pswap[:, :],
                        ktmp[kp][:, :], start=True, stop=True),
                        deps=[tcp, pA_use[sl_]], tick=True)
                    t1 = pl.op("v", lambda E, kp=kp, tt=tt: E["v"].tensor_tensor(
                        tR1[kp][:, :], ktmp[kp][:, :],
                        cosT[:, tt * TN: tt * TN + TN], OP.mult),
                        deps=[tcp, tR1_use[kp], t_tables], tick=True)
                    t2 = pl.op("v", lambda E, kp=kp, sl_=sl_, tt=tt: E["v"].tensor_tensor(
                        tR2[kp][:, :], pA[:, sl_ * 512:sl_ * 512 + 512],
                        ssinT[:, tt * TN: tt * TN + TN], OP.mult),
                        deps=[tpm, t1, tR2_use[kp]], tick=True)
                    pA_use[sl_] = t2
                    prc += 1
                    ktmp_use[kp] = t2
                    if kind == "q":
                        h = nb
                        tq = pl.op("v", lambda E, kp=kp, h=h, tt=tt: E["v"].tensor_tensor(
                            CT[:, h * TOK + tt * TN: h * TOK + tt * TN + TN],
                            tR1[kp][:, :], tR2[kp][:, :], OP.add),
                            deps=[t2], tick=True)
                        tR1_use[kp] = tq
                        tR2_use[kp] = tq
                    else:
                        ko = pl.op("v", lambda E, kp=kp: E["v"].tensor_tensor(
                            kout[kp][:, :], tR1[kp][:, :], tR2[kp][:, :], OP.add),
                            deps=[t2, kout_use[kp]], tick=True)
                        tR1_use[kp] = ko
                        tR2_use[kp] = ko
                        g_ = nb
                        td = pl.op("y", lambda E, kp=kp, g_=g_, tt=tt: E["y"].dma_start(
                            out=ksrc.ap()[tt * 1024 + g_ * 128: tt * 1024 + g_ * 128 + 128, :],
                            in_=kout[kp][:, :]), deps=[ko], dma=True)
                        kout_use[kp] = td
                        kv_store.append(td)
                else:
                    vp = pc % 2
                    tcp = pl.op("s", lambda E, bank=bank, vp=vp: E["s"].copy(
                        vtmp[vp][:, :], bank[:, :]),
                        deps=[tk_mm, vtmp_use[vp]], tick=True)
                    projev[pcp] = tcp
                    g_ = nb
                    last_v = None
                    for j in range(4):
                        sl_ = prc % 2
                        ttr = pl.op("t", lambda E, vp=vp, j=j, sl_=sl_: E["t"].transpose(
                            pA[:, sl_ * 512:sl_ * 512 + 128],
                            vtmp[vp][:, j * 128:(j + 1) * 128], identf[:, :]),
                            deps=[tcp, pA_use[sl_], t_tables], tick=True)
                        w2 = j % 2
                        tvt = pl.op("s", lambda E, sl_=sl_, w2=w2: E["s"].copy(
                            vtt[w2][:, :], pA[:, sl_ * 512:sl_ * 512 + 128]),
                            deps=[ttr, vtt_use[w2]], tick=True)
                        pA_use[sl_] = tvt
                        prc += 1
                        fo = (g_ * 8 + tt * 4 + j) * 16384
                        tst = pl.op("y", lambda E, w2=w2, fo=fo: E["y"].dma_start(
                            out=vsrc_f[fo:fo + 16384].rearrange("(p d) -> p d", p=128),
                            in_=vtt[w2][:, :]), deps=[tvt], dma=True)
                        vtt_use[w2] = tst
                        kv_store.append(tst)
                        last_v = tst
                    vtmp_use[vp] = last_v
                pc += 1
            nbg += 1

    tk_cc_k = pl.op("g", lambda E: E["g"].collective_compute(
        "AllGather", OP.bypass,
        replica_groups=[list(range(NCORE))],
        ins=[ksrc.ap().opt()], outs=[kdst.ap().opt()]),
        deps=kv_store, tick=True, sem="ck", rawinc=True)
    tk_cc_v = pl.op("g", lambda E: E["g"].collective_compute(
        "AllGather", OP.bypass,
        replica_groups=[list(range(NCORE))],
        ins=[vsrc.ap().opt()], outs=[vdst.ap().opt()]),
        deps=[], tick=True, sem="cv", rawinc=True)

    base = pl.snap()
    finals = None
    for c in range(NCORE):
        pl.restore(base)
        pl.seg(f"sw{c}")
        first = [("ck", 1), ("cv", 1), t_zsb0]
        for s_ in range(NSLOT):
            b = slot_block(c, s_)
            if b is not None:
                r = owner(b)
                tw = owner_tt(b)
                pl.op("y", lambda E, s_=s_, r=r, tw=tw: E["y"].dma_start(
                    out=keff.ap()[s_ * 1024:(s_ + 1) * 1024, :],
                    in_=kdst.ap()[r * 2048 + tw * 1024: r * 2048 + tw * 1024 + 1024, :]),
                    deps=first, dma=True)
                for g_ in range(8):
                    so = r * (1024 * 1024) + (g_ * 8 + tw * 4) * 16384
                    do = (s_ * 8 + g_) * 65536
                    pl.op("y", lambda E, so=so, do=do: E["y"].dma_start(
                        out=veff_f[do:do + 65536],
                        in_=vdst_f[so:so + 65536]), deps=first, dma=True)
            else:
                pl.op("y", lambda E, s_=s_: E["y"].dma_start(
                    out=keff_f[s_ * 524288: (s_ + 1) * 524288].rearrange(
                        "(p f) -> p f", p=128),
                    in_=zsb0[:, :]), deps=first, dma=True)
                pl.op("y", lambda E, s_=s_: E["y"].dma_start(
                    out=veff_f[s_ * 524288: (s_ + 1) * 524288].rearrange(
                        "(p f) -> p f", p=128),
                    in_=zsb0[:, :]), deps=first, dma=True)
            first = [t_zsb0]
        f = pl.snap()
        if finals is None:
            finals = f[0]
        else:
            assert finals == f[0], f"switch body sem counts differ: {finals} vs {f[0]}"
    pl.restore(base)
    pl.cnt = dict(finals)
    t_relay = pl.tkt("y")
    t_relay2 = pl.tkt("rs")

    pl.seg("main1")
    pl.barrier()

    if "ph1" in dump:
        _dump(nc, pl, {"d_xn": AT, "d_q": CT}, {"d_ksrc": ksrc, "d_vsrc": vsrc})

    regT = nc.alloc_register(mybir.EngineType.PE, "fT")
    regS = nc.alloc_register(mybir.EngineType.Activation, "fS")
    cc = 0
    exp_t = [None, None]
    att_use = [None, None]
    kb_t = [None, None]
    vb_t = [None, None]
    kb_use = [None, None]
    vb_use = [None, None]
    avev = {0: None, 1: None}
    rn_use = {0: None, 1: None}
    for g_ in range(8):
        for pr in range(2):
            for sub, srange, qoff in ((0, [7] + list(range(0, 7)), 0),
                                      (1, [23] + list(range(8, 23)), TN)):
                s0 = srange[0]
                s1 = srange[-1]
                def emit_avsums(p):
                    tk_av = None
                    for h2 in range(2):
                        avb = pC if h2 == 0 else pD
                        smb = pE if h2 == 0 else pF
                        pl.op("t", lambda E, lp=p["lp"], j=p["j"], ab=p["ab"], h2=h2,
                              avb=avb, fm=p["fm"]: E["t"].matmul(
                                  avb[:, :], vb[lp][:, j * 128:(j + 1) * 128],
                                  attb[ab][:, h2 * 512:h2 * 512 + 512],
                                  start=fm, stop=True, skip_group_check=True),
                              deps=[p["msk"], p["vbt"], avev[h2]], tick=False)
                        tk_av = pl.op("t", lambda E, s_=p["s"], ab=p["ab"], h2=h2,
                                      smb=smb, fm=p["fm"]: E["t"].matmul(
                                          smb[:, :],
                                          sflags[:, s_ * 128:(s_ + 1) * 128],
                                          attb[ab][:, h2 * 512:h2 * 512 + 512],
                                          start=fm, stop=True,
                                          skip_group_check=True),
                                      deps=[], tick=(h2 == 1))
                    att_use[p["ab"]] = tk_av
                    kb_use[p["lp"]] = tk_av
                    vb_use[p["lp"]] = tk_av

                pending = []
                for s_ in srange:
                    lp = s_ % 2
                    kb_t[lp] = pl.op("y", lambda E, s_=s_, g_=g_, lp=lp: E["y"].dma_start(
                        out=kb[lp][:, :],
                        in_=keff.ap()[s_ * 1024 + g_ * 128: s_ * 1024 + g_ * 128 + 128, :]),
                        deps=[t_relay, t_relay2, kb_use[lp]], dma=True, sem=f"kv{lp}")
                    vb_t[lp] = pl.op("y", lambda E, s_=s_, g_=g_, lp=lp: E["y"].dma_start(
                        out=vb[lp].ap().rearrange("p (j d) -> p j d", j=4),
                        in_=veff.ap()[(s_ * 8 + g_) * 512: (s_ * 8 + g_) * 512 + 512, :]
                        .rearrange("(j p) d -> p j d", p=128)),
                        deps=[t_relay, t_relay2, vb_use[lp]], dma=True, sem=f"kv{lp}")
                    diag = s_ in (7, 23)
                    for j in range(4):
                        sc = cc % 2
                        scb = pA if sc == 0 else pB
                        tk_sc = None
                        for h2 in range(2):
                            h = g_ * 4 + pr * 2 + h2
                            tk_sc = pl.op("t", lambda E, lp=lp, j=j, h=h, h2=h2, scb=scb, qoff=qoff:
                                          E["t"].matmul(
                                              scb[:, h2 * 512:h2 * 512 + 512],
                                              kb[lp][:, j * 128:(j + 1) * 128],
                                              CT[:, h * TOK + qoff: h * TOK + qoff + TN],
                                              start=True, stop=True),
                                          deps=[kb_t[lp], exp_t[sc]],
                                          tick=(h2 == 1))
                        ab = cc % 2
                        tk_ex = pl.op("s", lambda E, ab=ab, scb=scb: E["s"].activation(
                            attb[ab][:, :], scb[:, :], AF.Exp,
                            bias=cm12[:, 0:1], scale=QKS),
                            deps=[tk_sc, att_use[ab], t_cm12], tick=True)
                        exp_t[sc] = tk_ex
                        tk_msk = tk_ex
                        if diag:
                            tm1 = pl.op("g", lambda E, ab=ab, j=j: E["g"].tensor_tensor(
                                attb[ab][:, 0:512], attb[ab][:, 0:512],
                                dmask[:, j * TN:(j + 1) * TN], OP.mult),
                                deps=[tk_ex, t_tables], tick=True)
                            tk_msk = pl.op("g", lambda E, ab=ab, j=j: E["g"].tensor_tensor(
                                attb[ab][:, 512:1024], attb[ab][:, 512:1024],
                                dmask[:, j * TN:(j + 1) * TN], OP.mult),
                                deps=[tm1], tick=True)
                        pending.append(dict(lp=lp, j=j, ab=ab, msk=tk_msk,
                                            vbt=vb_t[lp], s=s_,
                                            fm=(s_ == s0 and j == 0)))
                        if len(pending) > 1:
                            emit_avsums(pending.pop(0))
                        cc += 1
                while pending:
                    emit_avsums(pending.pop(0))
                tk_raw = []
                for h2 in range(2):
                    avb = pC if h2 == 0 else pD
                    smb = pE if h2 == 0 else pF
                    c1 = pl.op("v", lambda E, h2=h2, avb=avb: E["v"].tensor_copy(
                        avraw[h2][:, :], avb[:, :]),
                        deps=[pl.tkt("t"), avev[h2], rn_use[h2]], tick=True)
                    c2 = pl.op("v", lambda E, h2=h2, smb=smb: E["v"].tensor_copy(
                        smraw[h2][:, :], smb[:, :]), deps=[c1], tick=True)
                    avev[h2] = c2
                    tk_raw.append(c2)
                for h2 in range(2):
                    h = g_ * 4 + pr * 2 + h2
                    tr = pl.op("v", lambda E, h2=h2: E["v"].reciprocal(
                        rinv[h2][:, :], smraw[h2][:, :]),
                        deps=[tk_raw[h2], rn_use[h2]], tick=True)
                    rn_use[h2] = pl.op("v", lambda E, h2=h2, h=h, qoff=qoff:
                                       E["v"].tensor_tensor(
                                           BT[:, h * TOK + qoff: h * TOK + qoff + TN],
                                           avraw[h2][:, :], rinv[h2][:, :], OP.mult),
                                       deps=[tr], tick=True)

    w3ld = [None, None]
    w3use = [None, None]
    for nb in range(2):
        wsrc = wo[:, nb * 128:(nb + 1) * 128].rearrange("(c p) o -> p c o", p=128)
        w3ld[nb] = pl.op("g", lambda E, nb=nb, wsrc=wsrc: E["g"].dma_start(
            out=W3[nb].ap().rearrange("p (c o) -> p c o", c=FC), in_=wsrc),
            deps=[], dma=True, sem=f"w{nb}")
    pl.barrier()
    if "ph2" in dump:
        _dump(nc, pl, {"d_av": BT}, {})

    pc3 = 0
    bank_ev3 = {}
    xt_t = [None, None]
    xt_use = [None, None]
    xp_use = [None, None]
    xpb_use = [None, None]
    last_xpb = None
    for nb in range(FC):
        par = nb % 2
        if nb >= 2:
            wsrc = wo[:, nb * 128:(nb + 1) * 128].rearrange("(c p) o -> p c o", p=128)
            w3ld[par] = pl.op("g", lambda E, par=par, wsrc=wsrc: E["g"].dma_start(
                out=W3[par].ap().rearrange("p (c o) -> p c o", c=FC), in_=wsrc),
                deps=[w3use[par]], dma=True, sem=f"w{par}")
        for tt in range(2):
            pcp = pc3 % 2
            bank = pC if pcp == 0 else pD
            xpar = pc3 % 2
            xt_t[xpar] = pl.op("y", lambda E, nb=nb, tt=tt, xpar=xpar: E["y"].dma_start(
                out=xtile[xpar][:, :],
                in_=xT[nb * 128:(nb + 1) * 128, tt * TN: tt * TN + TN]),
                deps=[xt_use[xpar]], dma=True, sem=f"io{xpar}")
            tk_mm = None
            for dc in range(FC):
                tk_mm = pl.op("t", lambda E, dc=dc, par=par, tt=tt, bank=bank:
                              E["t"].matmul(
                                  bank[:, :], W3[par][:, dc * 128:(dc + 1) * 128],
                                  BT[:, dc * TOK + tt * TN: dc * TOK + tt * TN + TN],
                                  start=(dc == 0), stop=(dc == FC - 1)),
                              deps=[w3ld[par], bank_ev3.get(pcp)],
                              tick=(dc == FC - 1))
            w3use[par] = tk_mm
            ta = pl.op("v", lambda E, xpar=xpar, bank=bank: E["v"].tensor_tensor(
                xpf[xpar][:, :], bank[:, :], xtile[xpar][:, :], OP.add),
                deps=[tk_mm, xt_t[xpar], xp_use[xpar], xpb_use[xpar]], tick=True)
            bank_ev3[pcp] = ta
            xt_use[xpar] = ta
            ts_ = pl.op("y", lambda E, xpar=xpar, nb=nb, tt=tt: E["y"].dma_start(
                out=xpscr.ap()[nb * 128:(nb + 1) * 128, tt * TN: tt * TN + TN],
                in_=xpf[xpar][:, :]), deps=[ta], dma=True, sem=f"io{xpar}")
            last_xpb = pl.op("v", lambda E, xpar=xpar, nb=nb, tt=tt: E["v"].tensor_copy(
                CT[:, nb * TOK + tt * TN: nb * TOK + tt * TN + TN],
                xpf[xpar][:, :]), deps=[ta], tick=True)
            xp_use[xpar] = ts_
            xpb_use[xpar] = last_xpb
            pc3 += 1
    gate_io = [pl.tkt("io0"), pl.tkt("io1")]
    pl.barrier()
    ln2 = []
    for tt in range(2):
        ln2.append(emit_ln(CT[:, :], tt, last_xpb, m3, r3, var3, sd3, x2b3, lntmp3))
    if "ph3" in dump:
        _dump(nc, pl, {"d_xn2": CT}, {"d_xp": xpscr})

    pl.op("g", lambda E: E["g"].dma_start(
        out=aaugs.ap().rearrange("p (c e) -> p c e", c=FC),
        in_=aaugT.rearrange("(c p) e -> p c e", p=128)), dma=True, sem="gx")
    pl.op("g", lambda E: E["g"].dma_start(out=balls[:, :], in_=ballT),
          dma=True, sem="gx")
    t_eb = pl.op("g", lambda E: E["g"].dma_start(out=biass[:, :], in_=ebias),
                 dma=True, sem="gx")
    t_aa = t_bb = t_eb

    zmk_t = [None, None]
    prev_zmk = None
    prev_ttp = None
    prev_ttp2 = None
    pc4 = 0
    w4ld = [None, None]
    w4use = [None, None]
    bank_ev4 = {}
    xp_t = [None, None]
    out_use = [None, None]
    y_store = []

    def emit_w4(nb):
        par = nb % 2
        wsrc = wup[:, nb * 128:(nb + 1) * 128].rearrange("(c p) o -> p c o", p=128)
        w4ld[par] = pl.op("g", lambda E, par=par, wsrc=wsrc: E["g"].dma_start(
            out=W3[par].ap().rearrange("p (c o) -> p c o", c=FC), in_=wsrc),
            deps=[w4use[par]], dma=True, sem=f"w{par}")

    def emit_base(nb, tt):
        nonlocal pc4
        par = nb % 2
        pcp = pc4 % 2
        bank = pC if pcp == 0 else pD
        xpar = pc4 % 2
        xp_t[xpar] = pl.op("y", lambda E, nb=nb, tt=tt, xpar=xpar: E["y"].dma_start(
            out=xtile[xpar][:, :],
            in_=xpscr.ap()[nb * 128:(nb + 1) * 128, tt * TN: tt * TN + TN]),
            deps=[out_use[xpar], gate_io[xpar]], dma=True, sem=f"io{xpar}")
        for fc in range(FC):
            pl.op("t", lambda E, fc=fc, par=par, tt=tt, bank=bank: E["t"].matmul(
                bank[:, :], W3[par][:, fc * 128:(fc + 1) * 128],
                CT[:, fc * TOK + tt * TN: fc * TOK + tt * TN + TN],
                start=(fc == 0), stop=False, skip_group_check=True),
                deps=[w4ld[par], ln2[tt], bank_ev4.get(pcp)], tick=False)
        pc4 += 1
        return bank, xpar, pcp

    def emit_delta_evac(nb, tt, bank, xpar, pcp):
        tk_mm = pl.op("t", lambda E, nb=nb, tt=tt, bank=bank: E["t"].matmul(
            bank[:, :], balls[:, nb * 128:(nb + 1) * 128], zmk[tt][:, :],
            start=False, stop=True, skip_group_check=True),
            deps=[t_bb, zmk_t[tt]], tick=True)
        w4use[nb % 2] = tk_mm
        ta = pl.op("v", lambda E, xpar=xpar, bank=bank: E["v"].tensor_tensor(
            xpf[xpar][:, :], bank[:, :], xtile[xpar][:, :], OP.add),
            deps=[tk_mm, xp_t[xpar], out_use[xpar]], tick=True)
        bank_ev4[pcp] = ta
        ty = pl.op("y", lambda E, xpar=xpar, nb=nb, tt=tt: E["y"].dma_start(
            out=y[nb * 128:(nb + 1) * 128, tt * TN: tt * TN + TN],
            in_=xpf[xpar][:, :]), deps=[ta], dma=True)
        out_use[xpar] = ty
        y_store.append(ty)

    emit_w4(0)
    for tt in range(2):
        tk_z = None
        prev_v4 = pl.tkt("v")
        for fc in range(FC):
            tk_z = pl.op("t", lambda E, fc=fc, tt=tt: E["t"].matmul(
                pA[0:36, 0:512], aaugs[:, fc * 36:(fc + 1) * 36],
                CT[:, fc * TOK + tt * TN: fc * TOK + tt * TN + TN],
                start=(fc == 0), stop=(fc == FC - 1)),
                deps=[t_aa, ln2[tt], prev_v4, prev_ttp], tick=(fc == FC - 1))
        t_zf = pl.op("v", lambda E: E["v"].tensor_copy(zsb[:, :], pA[0:36, 0:512]),
                     deps=[tk_z, prev_zmk], tick=True)
        t_lg = pl.op("v", lambda E: E["v"].tensor_scalar(
            lgsb[:, :], pA[32:36, 0:512], biass[32:36, 0:1], 0.0, OP.add),
            deps=[tk_z, t_eb, t_zf, prev_ttp], tick=True)
        bk0, xp0, pp0 = emit_base(0, tt)
        t_tp = None
        for ch in range(4):
            t_tp = pl.op("t", lambda E, ch=ch: E["t"].transpose(
                pA[:, 512 + ch * 4: 512 + ch * 4 + 4],
                lgsb[:, ch * 128:(ch + 1) * 128], identf[0:4, 0:4]),
                deps=[t_lg], tick=(ch == 3))
        t_lgt = pl.op("v", lambda E: E["v"].tensor_copy(lgt[:, :], pA[:, 512:528]),
                      deps=[t_tp, prev_ttp2], tick=True)
        last = t_lgt
        for ch in range(4):
            last = pl.op("v", lambda E, ch=ch: E["v"].reduce_max(
                m4[:, ch:ch + 1], lgt[:, ch * 4:(ch + 1) * 4], axis=AX.X),
                deps=[last], tick=True)
            last = pl.op("v", lambda E, ch=ch: E["v"].tensor_tensor(
                ohtm[:, ch * 4: ch * 4 + 1], lgt[:, ch * 4: ch * 4 + 1],
                m4[:, ch:ch + 1], OP.is_ge), deps=[last], tick=True)
            last = pl.op("v", lambda E, ch=ch: E["v"].tensor_copy(
                tb1[:, 0:1], ohtm[:, ch * 4: ch * 4 + 1]), deps=[last], tick=True)
            for e_ in range(1, 4):
                last = pl.op("v", lambda E, ch=ch, e_=e_: E["v"].tensor_tensor(
                    tb2[:, 0:1], lgt[:, ch * 4 + e_: ch * 4 + e_ + 1],
                    m4[:, ch:ch + 1], OP.is_ge), deps=[last], tick=True)
                last = pl.op("v", lambda E: E["v"].tensor_scalar(
                    tb2[:, 1:2], tb1[:, 0:1], -1.0, 1.0, OP.mult, OP.add),
                    deps=[last], tick=True)
                last = pl.op("v", lambda E, ch=ch, e_=e_: E["v"].tensor_tensor(
                    ohtm[:, ch * 4 + e_: ch * 4 + e_ + 1], tb2[:, 0:1],
                    tb2[:, 1:2], OP.mult), deps=[last], tick=True)
                last = pl.op("v", lambda E, ch=ch, e_=e_: E["v"].tensor_tensor(
                    tb1[:, 0:1], tb1[:, 0:1],
                    ohtm[:, ch * 4 + e_: ch * 4 + e_ + 1], OP.add),
                    deps=[last], tick=True)
        t_tp2 = None
        for ch in range(4):
            t_tp2 = pl.op("t", lambda E, ch=ch: E["t"].transpose(
                pB[0:4, 512 + ch * 128: 512 + ch * 128 + 128],
                ohtm[:, ch * 4:(ch + 1) * 4], identf[:, :]),
                deps=[last], tick=(ch == 3))
        t_oh = pl.op("v", lambda E: E["v"].tensor_copy(
            ohsb[:, :], pB[0:4, 512:1024]), deps=[t_tp2], tick=True)
        t_ohx = pl.op("t", lambda E: E["t"].matmul(
            pB[0:32, 0:512], e32[:, :], ohsb[:, :], start=True, stop=True),
            deps=[t_oh], tick=True)
        zmk_t[tt] = pl.op("v", lambda E, tt=tt: E["v"].tensor_tensor(
            zmk[tt][:, :], zsb[0:32, :], pB[0:32, 0:512], OP.mult),
            deps=[t_ohx, t_zf], tick=True)
        prev_zmk = zmk_t[tt]
        prev_ttp = t_tp
        prev_ttp2 = t_tp2
        emit_delta_evac(0, tt, bk0, xp0, pp0)

    for nb in range(1, FC):
        emit_w4(nb)
        for tt in range(2):
            bk, xp_, pp = emit_base(nb, tt)
            emit_delta_evac(nb, tt, bk, xp_, pp)

    if "ph4" in dump:
        pl.barrier()
        for nm, tns, shp, dt in (("d_zmk0", zmk[0], [32, TN], BF16),
                                 ("d_zmk1", zmk[1], [32, TN], BF16),
                                 ("d_zsb", zsb, [36, TN], F32),
                                 ("d_ohsb", ohsb, [4, TN], BF16),
                                 ("d_lgt", lgt, [128, 16], F32)):
            dap = nc.dram_tensor(nm, shp, dt, kind="ExternalOutput").ap()
            pl.op("y", lambda E, dap=dap, tns=tns: E["y"].dma_start(
                out=dap, in_=tns[:, :]), dma=True, sem="dbg")

    pl.barrier()

    SEMNAMES = ["t", "v", "s", "g", "y", "ck", "cv", "gx", "gy", "gz", "gw",
                "w0", "w1", "kv0", "kv1", "io0", "io1", "dbg", "rs"]
    with ExitStack() as st:
        SEMS = {n: st.enter_context(nc.semaphore(n)) for n in SEMNAMES}
        block = st.enter_context(nc.Block())
        E = {"t": nc.tensor, "v": nc.vector, "s": nc.scalar,
             "g": nc.gpsimd, "y": nc.sync}

        def replay_one(r):
            e = E[r["eng"]]
            for (sm, v) in r["waits"]:
                e.wait_ge(SEMS[sm], v)
            if r["fn"] is not None:
                ins = r["fn"](E)
                if r["inc"] is not None:
                    sm, n, raw = r["inc"]
                    if raw:
                        ins.then_inc(SEMS[sm])
                    else:
                        ins.then_inc(SEMS[sm], n)

        def replay(recs):
            i = 0
            while i < len(recs):
                r = recs[i]
                if r.get("ctl") == "sw2_begin":
                    slot = r["slot"]
                    j = i + 1
                    body = []
                    while recs[j].get("ctl") != "sw2_end":
                        body.append(recs[j])
                        j += 1
                    deltas = recs[j]["deltas"]
                    assert all(rr["eng"] in ("t", "s") for rr in body)
                    for ename, reg in (("t", regT), ("s", regS)):
                        e = E[ename]
                        e.reg_load(reg, ifl[0:1, slot:slot + 1])
                        with e.If_ne(reg, 0):
                            for rr in body:
                                if rr["eng"] == ename:
                                    replay_one(rr)
                        with e.Else():
                            if deltas.get(ename):
                                e.sem_inc(SEMS[ename], deltas[ename])
                        e.end_ifs()
                    i = j + 1
                    continue
                replay_one(r)
                i += 1

        replay(pl.segs["main0"])
        pid = nc.partition_id()
        for i in nc.Switch(engines=[nc.sync, nc.scalar], index=[pid, pid], n=NCORE):
            replay(pl.segs[f"sw{i}"])
        replay(pl.segs["main1"])

    return nc


def _dump(nc, pl, sbuf_map, dram_map):
    pl.barrier()
    for name, t in sbuf_map.items():
        d = nc.dram_tensor(name, [128, FC * TOK], BF16, kind="ExternalOutput").ap()
        pl.op("y", lambda E, d=d, t=t: E["y"].dma_start(out=d, in_=t[:, :]),
              dma=True, sem="dbg")
    for name, t in dram_map.items():
        h = t.ap()
        d = nc.dram_tensor(name, list(h.shape), BF16 if name != "d_xp" else F32,
                           kind="ExternalOutput").ap()
        pl.op("y", lambda E, d=d, h=h: E["y"].dma_start(out=d, in_=h),
              dma=True, sem="dbg")
    pl.barrier()


def _prep_inputs(L0_attn_x, L0_attn_wq, L0_attn_wk, L0_attn_wv, L0_attn_wo,
                 L0_ffn_W_up, L0_ffn_gate_w, L0_ffn_expert_bias,
                 L0_ffn_lora_A, L0_ffn_lora_B):
    f32 = lambda a: np.ascontiguousarray(np.asarray(a), dtype=np.float32)
    x = f32(L0_attn_x)[0]
    wq = f32(L0_attn_wq)
    wk = f32(L0_attn_wk)
    wv = f32(L0_attn_wv)
    wo = f32(L0_attn_wo)
    wup = np.ascontiguousarray(f32(L0_ffn_W_up).T)
    gate = f32(L0_ffn_gate_w)
    ebias_ = f32(L0_ffn_expert_bias)
    la = f32(L0_ffn_lora_A)
    lb = f32(L0_ffn_lora_B)

    aaug = np.concatenate([la.reshape(32, D), gate], axis=0)
    aaugT = np.ascontiguousarray(aaug.T)
    ballT = np.ascontiguousarray(lb.transpose(0, 2, 1).reshape(32, D))
    eb36 = np.zeros((36, 1), np.float32)
    eb36[32:36, 0] = ebias_

    identf = np.eye(128, dtype=np.float32)
    onesb = np.ones((128, 128), dtype=BF)
    psw = np.zeros((128, 128), np.float32)
    for k in range(128):
        psw[k, k ^ 1] = 1.0
    pswap = psw.astype(BF)
    e32 = np.zeros((4, 32), np.float32)
    for e_ in range(4):
        e32[e_, e_ * 8:(e_ + 1) * 8] = 1.0
    e32 = e32.astype(BF)
    kk = np.arange(128)[:, None]
    qq = np.arange(TN)[None, :]
    dm = np.zeros((128, 4 * TN), np.float32)
    for j in range(4):
        dm[:, j * TN:(j + 1) * TN] = (qq >= 128 * j + kk).astype(np.float32)
    dmask = dm.astype(BF)

    inv = 1.0 / (10000.0 ** (np.arange(0, HD, 2, dtype=np.float64) / HD))

    in_maps = []
    for c in range(NCORE):
        bA, bB = c, 15 - c
        toks = np.concatenate([np.arange(bA * TN, (bA + 1) * TN),
                               np.arange(bB * TN, (bB + 1) * TN)])
        xc = np.ascontiguousarray(
            np.concatenate([x[bA * TN:(bA + 1) * TN], x[bB * TN:(bB + 1) * TN]], 0).T)
        ang = inv[:, None] * toks[None, :].astype(np.float64)
        cosr = np.cos(ang)
        sinr = np.sin(ang)
        cosT = np.repeat(cosr, 2, axis=0).astype(np.float32)
        ssinT = np.repeat(sinr, 2, axis=0).astype(np.float32)
        ssinT[0::2, :] *= -1.0
        sf = np.zeros((128, NSLOT * 128), np.float32)
        ifl_np = np.zeros((1, 32), np.int32)
        for s_ in range(NSLOT):
            if slot_block(c, s_) is not None:
                sf[:, s_ * 128:(s_ + 1) * 128] = 1.0
                ifl_np[0, s_] = 1
        in_maps.append({
            "xT": xc, "wq": wq, "wk": wk, "wv": wv, "wo": wo, "wup": wup,
            "aaugT": aaugT, "ballT": ballT, "ebias": eb36,
            "cosT": cosT, "ssinT": ssinT, "identf": identf,
            "onesb": onesb, "pswap": pswap, "e32": e32,
            "dmask": dmask, "sflags": sf.astype(BF), "iflags": ifl_np,
        })
    return in_maps


def kernel(L0_attn_x, L0_attn_wq, L0_attn_wk, L0_attn_wv, L0_attn_wo,
           L0_ffn_W_up, L0_ffn_gate_w, L0_ffn_expert_bias,
           L0_ffn_lora_A, L0_ffn_lora_B):
    in_maps = _prep_inputs(L0_attn_x, L0_attn_wq, L0_attn_wk, L0_attn_wv,
                           L0_attn_wo, L0_ffn_W_up, L0_ffn_gate_w,
                           L0_ffn_expert_bias, L0_ffn_lora_A, L0_ffn_lora_B)
    nc = build()
    res = run_bass_kernel_spmd(nc, in_maps, core_ids=list(range(NCORE)))
    out = np.empty((1, S, D), np.float32)
    for c in range(NCORE):
        yc = np.asarray(res.results[c]["y"])
        bA, bB = c, 15 - c
        out[0, bA * TN:(bA + 1) * TN] = yc[:, 0:TN].T
        out[0, bB * TN:(bB + 1) * TN] = yc[:, TN:2 * TN].T
    return out


# version 18
# speedup vs baseline: 1.0514x; 1.0099x over previous
import numpy as np
import ml_dtypes
from contextlib import ExitStack

import concourse.bass as bass
import concourse.mybir as mybir
from concourse.bass_utils import run_bass_kernel_spmd

F32 = mybir.dt.float32
BF16 = mybir.dt.bfloat16
AF = mybir.ActivationFunctionType
OP = mybir.AluOpType
AX = mybir.AxisListType
BF = ml_dtypes.bfloat16

D = 4096
S = 8192
NH = 32
NKV = 8
HD = 128
FC = 32
TOK = 1024
TN = 512
NB = 16
NCORE = 8
EPS = 1e-5
CSH = 12.0
QKS = 1.0 / np.sqrt(128.0)
NSLOT = 24

ENGS = ("t", "v", "s", "g", "y")


def owner(b):
    return b if b < 8 else 15 - b


def owner_tt(b):
    return 0 if b < 8 else 1


def slot_block(c, s):
    if s < 7:
        return s if s < c else None
    if s == 7:
        return c
    if s < 23:
        return (s - 8) if (s - 8) < 15 - c else None
    return 15 - c


class Plan:
    def __init__(self):
        self.segs = {}
        self.cur = None
        self.cnt = {}
        self.lastw = {e: {} for e in ENGS}
        self.last_op = {}

    def seg(self, name):
        self.cur = self.segs.setdefault(name, [])

    def snap(self):
        import copy
        return (dict(self.cnt), copy.deepcopy(self.lastw), dict(self.last_op))

    def restore(self, st):
        import copy
        self.cnt = dict(st[0])
        self.lastw = copy.deepcopy(st[1])
        self.last_op = dict(st[2])

    def op(self, eng, fn, deps=(), tick=False, dma=False, sem=None, rawinc=False):
        waits = []
        for d in deps:
            if d is None:
                continue
            s, v = d
            if v <= 0:
                continue
            if self.lastw[eng].get(s, -1) >= v:
                continue
            self.lastw[eng][s] = v
            waits.append((s, v))
        inc = None
        if dma:
            tick = True
        if tick:
            sm = sem or eng
            n = 16 if dma else 1
            self.cnt[sm] = self.cnt.get(sm, 0) + n
            inc = (sm, n, rawinc)
        rec = dict(eng=eng, fn=fn, waits=waits, inc=inc)
        self.cur.append(rec)
        if fn is not None:
            self.last_op[eng] = rec
        if inc:
            return (inc[0], self.cnt[inc[0]])
        return None

    def tkt(self, sm):
        return (sm, self.cnt.get(sm, 0))

    def ensure_tick(self, eng):
        rec = self.last_op.get(eng)
        if rec is not None and rec["inc"] is None:
            self.cnt[eng] = self.cnt.get(eng, 0) + 1
            rec["inc"] = (eng, 1, False)
        return (eng, self.cnt.get(eng, 0))

    DMASEMS = ("y", "gx", "gy", "gz", "gw", "w0", "w1", "kv0", "kv1",
               "io0", "io1", "dbg", "rs")

    def sw2_begin(self, slot):
        self.cur.append(dict(eng=None, ctl="sw2_begin", slot=slot, fn=None))
        import copy
        self._sw2_lastw = copy.deepcopy(self.lastw)
        self._sw2_cnt = dict(self.cnt)

    def sw2_end(self):
        import copy
        deltas = {}
        for sm in ("t", "s"):
            d = self.cnt.get(sm, 0) - self._sw2_cnt.get(sm, 0)
            if d:
                deltas[sm] = d
        self.lastw = copy.deepcopy(self._sw2_lastw)
        self.cur.append(dict(eng=None, ctl="sw2_end", deltas=deltas, fn=None))

    def barrier(self):
        tks = {e: self.ensure_tick(e) for e in ENGS}
        dts = [self.tkt(s) for s in self.DMASEMS]
        for e in ENGS:
            self.op(e, None, deps=[tks[o] for o in ENGS if o != e] + dts)


def build(dump=()):
    nc = bass.Bass("TRN2", target_bir_lowering=False, debug=False)

    xT = nc.dram_tensor("xT", [D, TOK], F32, kind="ExternalInput").ap()
    wq = nc.dram_tensor("wq", [D, D], F32, kind="ExternalInput").ap()
    wk = nc.dram_tensor("wk", [D, 1024], F32, kind="ExternalInput").ap()
    wv = nc.dram_tensor("wv", [D, 1024], F32, kind="ExternalInput").ap()
    wo = nc.dram_tensor("wo", [D, D], F32, kind="ExternalInput").ap()
    wup = nc.dram_tensor("wup", [D, D], F32, kind="ExternalInput").ap()
    aaugT = nc.dram_tensor("aaugT", [D, 36], F32, kind="ExternalInput").ap()
    ballT = nc.dram_tensor("ballT", [32, D], F32, kind="ExternalInput").ap()
    ebias = nc.dram_tensor("ebias", [36, 1], F32, kind="ExternalInput").ap()
    cosT_d = nc.dram_tensor("cosT", [128, TOK], F32, kind="ExternalInput").ap()
    ssinT_d = nc.dram_tensor("ssinT", [128, TOK], F32, kind="ExternalInput").ap()
    identf_d = nc.dram_tensor("identf", [128, 128], F32, kind="ExternalInput").ap()
    onesb_d = nc.dram_tensor("onesb", [128, 128], BF16, kind="ExternalInput").ap()
    pswap_d = nc.dram_tensor("pswap", [128, 128], BF16, kind="ExternalInput").ap()
    e32_d = nc.dram_tensor("e32", [4, 32], BF16, kind="ExternalInput").ap()
    dmask_d = nc.dram_tensor("dmask", [128, 4 * TN], BF16, kind="ExternalInput").ap()
    sflags_d = nc.dram_tensor("sflags", [128, NSLOT * 128], BF16, kind="ExternalInput").ap()
    iflags_d = nc.dram_tensor("iflags", [1, 32], mybir.dt.int32, kind="ExternalInput").ap()
    y = nc.dram_tensor("y", [D, TOK], F32, kind="ExternalOutput").ap()

    ksrc = nc.dram_tensor("ksrc", [2 * 1024, 512], BF16, kind="Internal")
    vsrc = nc.dram_tensor("vsrc", [1024, 1024], BF16, kind="Internal")
    kdst = nc.dram_tensor("kdst", [NCORE * 2 * 1024, 512], BF16, kind="Internal",
                          addr_space="Shared")
    vdst = nc.dram_tensor("vdst", [NCORE * 1024, 1024], BF16, kind="Internal",
                          addr_space="Shared")
    keff = nc.dram_tensor("keff", [NSLOT * 1024, 512], BF16, kind="Internal")
    veff = nc.dram_tensor("veff", [NSLOT * 8 * 4 * 128, 128], BF16, kind="Internal")
    xpscr = nc.dram_tensor("xpscr", [D, TOK], F32, kind="Internal")

    kdst_f = kdst.ap().rearrange("a b -> (a b)")
    vdst_f = vdst.ap().rearrange("a b -> (a b)")
    keff_f = keff.ap().rearrange("a b -> (a b)")
    veff_f = veff.ap().rearrange("a b -> (a b)")
    vsrc_f = vsrc.ap().rearrange("a b -> (a b)")

    SB0 = 20480

    def at(name, shape, dtype, off):
        return nc.alloc_sbuf_tensor_at(name, shape, dtype, offset=SB0 + off)

    AT = at("A_xn", [128, FC * TOK], BF16, 0)
    BT = at("B_av", [128, FC * TOK], BF16, 65536)
    CT = at("C_q", [128, FC * TOK], BF16, 131072)

    identf = at("identf", [128, 128], F32, 196608)
    onesb = at("onesb", [128, 128], BF16, 197120)
    pswap = at("pswap", [128, 128], BF16, 197376)
    e32 = at("e32", [4, 32], BF16, 197632)
    dmask = at("dmask", [128, 4 * TN], BF16, 197696)
    sflags = at("sflags", [128, NSLOT * 128], BF16, 201792)
    cm12 = at("cm12", [128, 1], F32, 207936)
    ifl = at("ifl", [1, 32], mybir.dt.int32, 207968)

    W1 = [at(f"W1_{i}", [128, FC * 128], BF16, 65536 + i * 8192) for i in range(2)]
    cosT = at("cosT", [128, TOK], F32, 81920)
    ssinT = at("ssinT", [128, TOK], F32, 86016)
    m1 = at("m1", [128, TN], F32, 90112)
    r1 = at("r1", [128, TN], F32, 92160)
    var1 = at("var1", [128, TN], F32, 94208)
    sd1 = at("sd1", [128, TN], F32, 96256)
    x2b = [at(f"x2b_{i}", [128, TN], BF16, 98304 + i * 1024) for i in range(2)]
    lntmp = [at(f"lntmp_{i}", [128, TN], F32, 100352 + i * 2048) for i in range(2)]
    ktmp = [at(f"ktmp_{i}", [128, TN], BF16, 104448 + i * 1024) for i in range(2)]
    tR1 = [at(f"tR1_{i}", [128, TN], F32, 106496 + i * 2048) for i in range(2)]
    tR2 = [at(f"tR2_{i}", [128, TN], F32, 110592 + i * 2048) for i in range(2)]
    vtmp = [at(f"vtmp_{i}", [128, TN], F32, 114688 + i * 2048) for i in range(2)]
    vtt = [at(f"vtt_{i}", [128, 128], BF16, 118784 + i * 256) for i in range(2)]
    kout = [at(f"kout_{i}", [128, TN], BF16, 119296 + i * 1024) for i in range(2)]
    zsb0 = at("zsb0", [128, 4096], BF16, 121344)

    kb = [at(f"kb_{i}", [128, 512], BF16, 32768 + i * 1024) for i in range(2)]
    vb = [at(f"vb_{i}", [128, 512], BF16, 34816 + i * 1024) for i in range(2)]
    attb = [at(f"attb_{i}", [128, 1024], BF16, 36864 + i * 2048) for i in range(2)]
    rinv = [at(f"rinv_{i}", [128, 512], F32, 40960 + i * 2048) for i in range(2)]
    avraw = [at(f"avraw_{i}", [128, 512], F32, 45056 + i * 2048) for i in range(2)]
    smraw = [at(f"smraw_{i}", [128, 512], F32, 49152 + i * 2048) for i in range(2)]

    W3 = [at(f"W3_{i}", [128, FC * 128], BF16, 0 + i * 8192) for i in range(2)]
    xtile = [at(f"xtile_{i}", [128, TN], F32, 16384 + i * 2048) for i in range(2)]
    xpf = [at(f"xpf_{i}", [128, TN], F32, 20480 + i * 2048) for i in range(2)]
    m3 = at("m3", [128, TN], F32, 24576)
    r3 = at("r3", [128, TN], F32, 26624)
    var3 = at("var3", [128, TN], F32, 28672)
    sd3 = at("sd3", [128, TN], F32, 30720)
    x2b3 = [at(f"x2b3_{i}", [128, TN], BF16, 32768 + i * 1024) for i in range(2)]
    lntmp3 = [at(f"lntmp3_{i}", [128, TN], F32, 34816 + i * 2048) for i in range(2)]

    zsb = at("zsb", [36, TN], F32, 40960)
    zmk = [at(f"zmk_{i}", [32, TN], BF16, 43008 + i * 1024) for i in range(2)]
    lgsb = at("lgsb", [4, TN], F32, 45056)
    lgt = at("lgt", [128, 16], F32, 47104)
    m4 = at("m4", [128, 4], F32, 47232)
    ohtm = at("ohtm", [128, 16], F32, 47360)
    tb1 = at("tb1", [128, 4], F32, 47488)
    tb2 = at("tb2", [128, 4], F32, 47552)
    ohsb = at("ohsb", [4, TN], BF16, 47616)
    aaugs = at("aaugs", [128, FC * 36], BF16, 65536)
    balls = at("balls", [32, D], BF16, 68096)
    biass = at("biass", [36, 1], F32, 76288)

    pA = nc.alloc_psum_tensor("pA", [128, 1024], F32)
    pB = nc.alloc_psum_tensor("pB", [128, 1024], F32)
    pC = nc.alloc_psum_tensor("pC", [128, 512], F32)
    pD = nc.alloc_psum_tensor("pD", [128, 512], F32)
    pE = nc.alloc_psum_tensor("pE", [128, 512], F32)
    pF = nc.alloc_psum_tensor("pF", [128, 512], F32)

    pl = Plan()
    pl.seg("main0")

    tload = []
    for dst, src in ((identf, identf_d), (onesb, onesb_d), (pswap, pswap_d),
                     (e32, e32_d), (dmask, dmask_d), (sflags, sflags_d),
                     (cosT, cosT_d), (ssinT, ssinT_d), (ifl, iflags_d)):
        tload.append(pl.op("y", lambda E, dst=dst, src=src:
                           E["y"].dma_start(out=dst[:, :], in_=src), dma=True))
    t_tables = pl.tkt("y")
    t_zsb0 = pl.op("v", lambda E: E["v"].memset(zsb0[:, :], 0.0), tick=True)
    t_cm12 = pl.op("v", lambda E: E["v"].memset(cm12[:, :], -CSH), tick=True)

    x_src = xT.rearrange("(c p) t -> p c t", p=128)
    XSEM = ("gx", "gy", "gz", "gw")
    t_xq = []
    for q4 in range(4):
        t_xq.append(pl.op("g", lambda E, q4=q4: E["g"].dma_start(
            out=AT.ap().rearrange("p (c t) -> p c t", c=FC)[:, q4 * 8:(q4 + 1) * 8, :],
            in_=x_src[:, q4 * 8:(q4 + 1) * 8, :]),
            dma=True, sem=XSEM[q4]))
    wk_pre = [None, None]
    for nb in range(2):
        wsrc = wk[:, nb * 128:(nb + 1) * 128].rearrange("(c p) o -> p c o", p=128)
        wk_pre[nb] = pl.op("g", lambda E, nb=nb, wsrc=wsrc: E["g"].dma_start(
            out=W1[nb].ap().rearrange("p (c o) -> p c o", c=FC), in_=wsrc),
            deps=[], dma=True, sem=f"w{nb}")
    t_xl = [t_xq[3], t_xq[3]]

    def emit_ln(arena, tt, dep_in, mT, rT, varT, sdT, x2T, lnT, dep_extra=(),
                dep_fn=None):
        def sl(fc):
            return arena[:, fc * TOK + tt * TN: fc * TOK + tt * TN + TN]
        if dep_fn is None:
            dep_fn = lambda fc: dep_in
        prev_v = pl.tkt("v")
        prev_t = pl.tkt("t")
        tk_s1 = None
        for fc in range(FC):
            tk_s1 = pl.op("t", lambda E, fc=fc: E["t"].matmul(
                pE[:, :], onesb[:, :], sl(fc), start=(fc == 0), stop=(fc == FC - 1)),
                deps=[dep_fn(fc), t_tables, prev_v] + list(dep_extra),
                tick=(fc == FC - 1))
        sq_pe = [None, None]
        for fc in range(FC):
            par = fc % 2
            d = [dep_fn(fc), prev_t]
            if sq_pe[par] is not None:
                d.append(sq_pe[par])
            tk_sq = pl.op("s", lambda E, fc=fc, par=par: E["s"].activation(
                x2T[par][:, :], sl(fc), AF.Square), deps=d, tick=True)
            sq_pe[par] = pl.op("t", lambda E, par=par, fc=fc: E["t"].matmul(
                pF[:, :], onesb[:, :], x2T[par][:, :],
                start=(fc == 0), stop=(fc == FC - 1)),
                deps=[tk_sq, prev_v], tick=True)
        tk_s2 = sq_pe[(FC - 1) % 2]
        a = pl.op("v", lambda E: E["v"].tensor_scalar(
            mT[:, :], pE[:, :], 1.0 / D, 0.0, OP.mult), deps=[tk_s1], tick=True)
        b = pl.op("v", lambda E: E["v"].tensor_scalar(
            varT[:, :], pF[:, :], 1.0 / D, EPS, OP.mult, OP.add),
            deps=[tk_s2, a], tick=True)
        cgate = pl.op("v", lambda E: E["v"].tensor_tensor(
            sdT[:, :], mT[:, :], mT[:, :], OP.mult), deps=[b], tick=True)
        dd = pl.op("v", lambda E: E["v"].tensor_tensor(
            varT[:, :], varT[:, :], sdT[:, :], OP.subtract), deps=[cgate], tick=True)
        ee = pl.op("s", lambda E: E["s"].activation(
            sdT[:, :], varT[:, :], AF.Sqrt), deps=[dd], tick=True)
        ff = pl.op("v", lambda E: E["v"].reciprocal(rT[:, :], sdT[:, :]),
                   deps=[ee], tick=True)
        lastv = ff
        lastg = None
        fcticks = [None] * FC
        for fc in range(FC):
            par = fc % 2
            eng = "v" if par == 0 else "g"
            prev = lastv if par == 0 else lastg
            g1 = pl.op(eng, lambda E, fc=fc, par=par, eng=eng: E[eng].tensor_tensor(
                lnT[par][:, :], sl(fc), mT[:, :], OP.subtract),
                deps=[prev, ff, tk_s2], tick=True)
            tkn = pl.op(eng, lambda E, fc=fc, par=par, eng=eng: E[eng].tensor_tensor(
                sl(fc), lnT[par][:, :], rT[:, :], OP.mult), deps=[g1], tick=True)
            fcticks[fc] = tkn
            if par == 0:
                lastv = tkn
            else:
                lastg = tkn
        last = pl.op("v", lambda E: E["v"].nop(), deps=[lastv, lastg], tick=True)
        return last, fcticks

    ln1 = []
    ln1fc = []
    for tt in range(2):
        lt, lfc = emit_ln(AT[:, :], tt, t_xq[3], m1, r1, var1, sd1, x2b, lntmp,
                          dep_fn=lambda fc: t_xq[fc // 8])
        ln1.append(lt)
        ln1fc.append(lfc)

    mats = [("k", wk, 8), ("v", wv, 8), ("q", wq, 32)]
    pc = 0
    prc = 0
    wld = [None, None]
    wuse = [None, None]
    projev = {}
    pA_use = [None, None]
    ktmp_use = [None, None]
    vtmp_use = [None, None]
    vtt_use = [None, None]
    kout_use = [None, None]
    tR1_use = [None, None]
    tR2_use = [None, None]
    nbg = 0
    kv_store = []
    for kind, wmat, nnb in mats:
        for nb in range(nnb):
            par = nbg % 2
            wsrc = wmat[:, nb * 128:(nb + 1) * 128].rearrange(
                "(c p) o -> p c o", p=128)
            if kind == "k" and nb < 2:
                wld[par] = wk_pre[nb]
            else:
                wld[par] = pl.op("g", lambda E, par=par, wsrc=wsrc: E["g"].dma_start(
                    out=W1[par].ap().rearrange("p (c o) -> p c o", c=FC), in_=wsrc),
                    deps=[wuse[par]], dma=True, sem=f"w{par}")
            for tt in range(2):
                pcp = pc % 2
                bank = pC if pcp == 0 else pD
                tk_mm = None
                for fc in range(FC):
                    tk_mm = pl.op("t", lambda E, fc=fc, par=par, tt=tt, bank=bank:
                                  E["t"].matmul(
                                      bank[:, :],
                                      W1[par][:, fc * 128:(fc + 1) * 128],
                                      AT[:, fc * TOK + tt * TN: fc * TOK + tt * TN + TN],
                                      start=(fc == 0), stop=(fc == FC - 1)),
                                  deps=[wld[par], ln1fc[tt][fc], projev.get(pcp)],
                                  tick=(fc == FC - 1))
                wuse[par] = tk_mm
                if kind in ("k", "q"):
                    kp = pc % 2
                    tcp = pl.op("s", lambda E, bank=bank, kp=kp: E["s"].copy(
                        ktmp[kp][:, :], bank[:, :]),
                        deps=[tk_mm, ktmp_use[kp]], tick=True)
                    projev[pcp] = tcp
                    sl_ = prc % 2
                    tpm = pl.op("t", lambda E, kp=kp, sl_=sl_: E["t"].matmul(
                        pA[:, sl_ * 512:sl_ * 512 + 512], pswap[:, :],
                        ktmp[kp][:, :], start=True, stop=True),
                        deps=[tcp, pA_use[sl_]], tick=True)
                    t1 = pl.op("v", lambda E, kp=kp, tt=tt: E["v"].tensor_tensor(
                        tR1[kp][:, :], ktmp[kp][:, :],
                        cosT[:, tt * TN: tt * TN + TN], OP.mult),
                        deps=[tcp, tR1_use[kp], t_tables], tick=True)
                    t2 = pl.op("v", lambda E, kp=kp, sl_=sl_, tt=tt: E["v"].tensor_tensor(
                        tR2[kp][:, :], pA[:, sl_ * 512:sl_ * 512 + 512],
                        ssinT[:, tt * TN: tt * TN + TN], OP.mult),
                        deps=[tpm, t1, tR2_use[kp]], tick=True)
                    pA_use[sl_] = t2
                    prc += 1
                    ktmp_use[kp] = t2
                    if kind == "q":
                        h = nb
                        tq = pl.op("v", lambda E, kp=kp, h=h, tt=tt: E["v"].tensor_tensor(
                            CT[:, h * TOK + tt * TN: h * TOK + tt * TN + TN],
                            tR1[kp][:, :], tR2[kp][:, :], OP.add),
                            deps=[t2], tick=True)
                        tR1_use[kp] = tq
                        tR2_use[kp] = tq
                    else:
                        ko = pl.op("v", lambda E, kp=kp: E["v"].tensor_tensor(
                            kout[kp][:, :], tR1[kp][:, :], tR2[kp][:, :], OP.add),
                            deps=[t2, kout_use[kp]], tick=True)
                        tR1_use[kp] = ko
                        tR2_use[kp] = ko
                        g_ = nb
                        td = pl.op("y", lambda E, kp=kp, g_=g_, tt=tt: E["y"].dma_start(
                            out=ksrc.ap()[tt * 1024 + g_ * 128: tt * 1024 + g_ * 128 + 128, :],
                            in_=kout[kp][:, :]), deps=[ko], dma=True)
                        kout_use[kp] = td
                        kv_store.append(td)
                else:
                    vp = pc % 2
                    tcp = pl.op("s", lambda E, bank=bank, vp=vp: E["s"].copy(
                        vtmp[vp][:, :], bank[:, :]),
                        deps=[tk_mm, vtmp_use[vp]], tick=True)
                    projev[pcp] = tcp
                    g_ = nb
                    last_v = None
                    for j in range(4):
                        sl_ = prc % 2
                        ttr = pl.op("t", lambda E, vp=vp, j=j, sl_=sl_: E["t"].transpose(
                            pA[:, sl_ * 512:sl_ * 512 + 128],
                            vtmp[vp][:, j * 128:(j + 1) * 128], identf[:, :]),
                            deps=[tcp, pA_use[sl_], t_tables], tick=True)
                        w2 = j % 2
                        tvt = pl.op("s", lambda E, sl_=sl_, w2=w2: E["s"].copy(
                            vtt[w2][:, :], pA[:, sl_ * 512:sl_ * 512 + 128]),
                            deps=[ttr, vtt_use[w2]], tick=True)
                        pA_use[sl_] = tvt
                        prc += 1
                        fo = (g_ * 8 + tt * 4 + j) * 16384
                        tst = pl.op("y", lambda E, w2=w2, fo=fo: E["y"].dma_start(
                            out=vsrc_f[fo:fo + 16384].rearrange("(p d) -> p d", p=128),
                            in_=vtt[w2][:, :]), deps=[tvt], dma=True)
                        vtt_use[w2] = tst
                        kv_store.append(tst)
                        last_v = tst
                    vtmp_use[vp] = last_v
                pc += 1
            nbg += 1
            if kind == "v" and nb == nnb - 1:
                tk_cc_k = pl.op("g", lambda E: E["g"].collective_compute(
                    "AllGather", OP.bypass,
                    replica_groups=[list(range(NCORE))],
                    ins=[ksrc.ap().opt()], outs=[kdst.ap().opt()]),
                    deps=kv_store, tick=True, sem="ck", rawinc=True)
                tk_cc_v = pl.op("g", lambda E: E["g"].collective_compute(
                    "AllGather", OP.bypass,
                    replica_groups=[list(range(NCORE))],
                    ins=[vsrc.ap().opt()], outs=[vdst.ap().opt()]),
                    deps=[], tick=True, sem="cv", rawinc=True)

    base = pl.snap()
    finals = None
    for c in range(NCORE):
        pl.restore(base)
        pl.seg(f"sw{c}")
        first = [("ck", 1), ("cv", 1), t_zsb0]
        for s_ in range(NSLOT):
            b = slot_block(c, s_)
            if b is not None:
                r = owner(b)
                tw = owner_tt(b)
                pl.op("y", lambda E, s_=s_, r=r, tw=tw: E["y"].dma_start(
                    out=keff.ap()[s_ * 1024:(s_ + 1) * 1024, :],
                    in_=kdst.ap()[r * 2048 + tw * 1024: r * 2048 + tw * 1024 + 1024, :]),
                    deps=first, dma=True)
                for g_ in range(8):
                    so = r * (1024 * 1024) + (g_ * 8 + tw * 4) * 16384
                    do = (s_ * 8 + g_) * 65536
                    pl.op("y", lambda E, so=so, do=do: E["y"].dma_start(
                        out=veff_f[do:do + 65536],
                        in_=vdst_f[so:so + 65536]), deps=first, dma=True)
            else:
                pl.op("y", lambda E, s_=s_: E["y"].dma_start(
                    out=keff_f[s_ * 524288: (s_ + 1) * 524288].rearrange(
                        "(p f) -> p f", p=128),
                    in_=zsb0[:, :]), deps=first, dma=True)
                pl.op("y", lambda E, s_=s_: E["y"].dma_start(
                    out=veff_f[s_ * 524288: (s_ + 1) * 524288].rearrange(
                        "(p f) -> p f", p=128),
                    in_=zsb0[:, :]), deps=first, dma=True)
            first = [t_zsb0]
        f = pl.snap()
        if finals is None:
            finals = f[0]
        else:
            assert finals == f[0], f"switch body sem counts differ: {finals} vs {f[0]}"
    pl.restore(base)
    pl.cnt = dict(finals)
    t_relay = pl.tkt("y")
    t_relay2 = pl.tkt("rs")

    pl.seg("main1")
    pl.barrier()

    if "ph1" in dump:
        _dump(nc, pl, {"d_xn": AT, "d_q": CT}, {"d_ksrc": ksrc, "d_vsrc": vsrc})

    regT = nc.alloc_register(mybir.EngineType.PE, "fT")
    regS = nc.alloc_register(mybir.EngineType.Activation, "fS")
    cc = 0
    exp_t = [None, None]
    att_use = [None, None]
    kb_t = [None, None]
    vb_t = [None, None]
    kb_use = [None, None]
    vb_use = [None, None]
    avev = {0: None, 1: None}
    rn_use = {0: None, 1: None}
    for g_ in range(8):
        for pr in range(2):
            for sub, srange, qoff in ((0, [7] + list(range(0, 7)), 0),
                                      (1, [23] + list(range(8, 23)), TN)):
                s0 = srange[0]
                s1 = srange[-1]
                def emit_avsums(p):
                    tk_av = None
                    for h2 in range(2):
                        avb = pC if h2 == 0 else pD
                        smb = pE if h2 == 0 else pF
                        pl.op("t", lambda E, lp=p["lp"], j=p["j"], ab=p["ab"], h2=h2,
                              avb=avb, fm=p["fm"]: E["t"].matmul(
                                  avb[:, :], vb[lp][:, j * 128:(j + 1) * 128],
                                  attb[ab][:, h2 * 512:h2 * 512 + 512],
                                  start=fm, stop=True, skip_group_check=True),
                              deps=[p["msk"], p["vbt"], avev[h2]], tick=False)
                        tk_av = pl.op("t", lambda E, s_=p["s"], ab=p["ab"], h2=h2,
                                      smb=smb, fm=p["fm"]: E["t"].matmul(
                                          smb[:, :],
                                          sflags[:, s_ * 128:(s_ + 1) * 128],
                                          attb[ab][:, h2 * 512:h2 * 512 + 512],
                                          start=fm, stop=True,
                                          skip_group_check=True),
                                      deps=[], tick=(h2 == 1))
                    att_use[p["ab"]] = tk_av
                    kb_use[p["lp"]] = tk_av
                    vb_use[p["lp"]] = tk_av

                pending = []
                for s_ in srange:
                    lp = s_ % 2
                    kb_t[lp] = pl.op("y", lambda E, s_=s_, g_=g_, lp=lp: E["y"].dma_start(
                        out=kb[lp][:, :],
                        in_=keff.ap()[s_ * 1024 + g_ * 128: s_ * 1024 + g_ * 128 + 128, :]),
                        deps=[t_relay, t_relay2, kb_use[lp]], dma=True, sem=f"kv{lp}")
                    vb_t[lp] = pl.op("y", lambda E, s_=s_, g_=g_, lp=lp: E["y"].dma_start(
                        out=vb[lp].ap().rearrange("p (j d) -> p j d", j=4),
                        in_=veff.ap()[(s_ * 8 + g_) * 512: (s_ * 8 + g_) * 512 + 512, :]
                        .rearrange("(j p) d -> p j d", p=128)),
                        deps=[t_relay, t_relay2, vb_use[lp]], dma=True, sem=f"kv{lp}")
                    diag = s_ in (7, 23)
                    for j in range(4):
                        sc = cc % 2
                        scb = pA if sc == 0 else pB
                        tk_sc = None
                        for h2 in range(2):
                            h = g_ * 4 + pr * 2 + h2
                            tk_sc = pl.op("t", lambda E, lp=lp, j=j, h=h, h2=h2, scb=scb, qoff=qoff:
                                          E["t"].matmul(
                                              scb[:, h2 * 512:h2 * 512 + 512],
                                              kb[lp][:, j * 128:(j + 1) * 128],
                                              CT[:, h * TOK + qoff: h * TOK + qoff + TN],
                                              start=True, stop=True),
                                          deps=[kb_t[lp], exp_t[sc]],
                                          tick=(h2 == 1))
                        ab = cc % 2
                        tk_ex = pl.op("s", lambda E, ab=ab, scb=scb: E["s"].activation(
                            attb[ab][:, :], scb[:, :], AF.Exp,
                            bias=cm12[:, 0:1], scale=QKS),
                            deps=[tk_sc, att_use[ab], t_cm12], tick=True)
                        exp_t[sc] = tk_ex
                        tk_msk = tk_ex
                        if diag:
                            tm1 = pl.op("g", lambda E, ab=ab, j=j: E["g"].tensor_tensor(
                                attb[ab][:, 0:512], attb[ab][:, 0:512],
                                dmask[:, j * TN:(j + 1) * TN], OP.mult),
                                deps=[tk_ex, t_tables], tick=True)
                            tk_msk = pl.op("g", lambda E, ab=ab, j=j: E["g"].tensor_tensor(
                                attb[ab][:, 512:1024], attb[ab][:, 512:1024],
                                dmask[:, j * TN:(j + 1) * TN], OP.mult),
                                deps=[tm1], tick=True)
                        pending.append(dict(lp=lp, j=j, ab=ab, msk=tk_msk,
                                            vbt=vb_t[lp], s=s_,
                                            fm=(s_ == s0 and j == 0)))
                        if len(pending) > 1:
                            emit_avsums(pending.pop(0))
                        cc += 1
                while pending:
                    emit_avsums(pending.pop(0))
                tk_raw = []
                for h2 in range(2):
                    avb = pC if h2 == 0 else pD
                    smb = pE if h2 == 0 else pF
                    c1 = pl.op("v", lambda E, h2=h2, avb=avb: E["v"].tensor_copy(
                        avraw[h2][:, :], avb[:, :]),
                        deps=[pl.tkt("t"), avev[h2], rn_use[h2]], tick=True)
                    c2 = pl.op("v", lambda E, h2=h2, smb=smb: E["v"].tensor_copy(
                        smraw[h2][:, :], smb[:, :]), deps=[c1], tick=True)
                    avev[h2] = c2
                    tk_raw.append(c2)
                for h2 in range(2):
                    h = g_ * 4 + pr * 2 + h2
                    tr = pl.op("v", lambda E, h2=h2: E["v"].reciprocal(
                        rinv[h2][:, :], smraw[h2][:, :]),
                        deps=[tk_raw[h2], rn_use[h2]], tick=True)
                    rn_use[h2] = pl.op("v", lambda E, h2=h2, h=h, qoff=qoff:
                                       E["v"].tensor_tensor(
                                           BT[:, h * TOK + qoff: h * TOK + qoff + TN],
                                           avraw[h2][:, :], rinv[h2][:, :], OP.mult),
                                       deps=[tr], tick=True)

    w3ld = [None, None]
    w3use = [None, None]
    for nb in range(2):
        wsrc = wo[:, nb * 128:(nb + 1) * 128].rearrange("(c p) o -> p c o", p=128)
        w3ld[nb] = pl.op("g", lambda E, nb=nb, wsrc=wsrc: E["g"].dma_start(
            out=W3[nb].ap().rearrange("p (c o) -> p c o", c=FC), in_=wsrc),
            deps=[], dma=True, sem=f"w{nb}")
    pl.barrier()
    if "ph2" in dump:
        _dump(nc, pl, {"d_av": BT}, {})

    pc3 = 0
    bank_ev3 = {}
    xt_t = [None, None]
    xt_use = [None, None]
    xp_use = [None, None]
    xpb_use = [None, None]
    last_xpb = None
    for nb in range(FC):
        par = nb % 2
        if nb >= 2:
            wsrc = wo[:, nb * 128:(nb + 1) * 128].rearrange("(c p) o -> p c o", p=128)
            w3ld[par] = pl.op("g", lambda E, par=par, wsrc=wsrc: E["g"].dma_start(
                out=W3[par].ap().rearrange("p (c o) -> p c o", c=FC), in_=wsrc),
                deps=[w3use[par]], dma=True, sem=f"w{par}")
        for tt in range(2):
            pcp = pc3 % 2
            bank = pC if pcp == 0 else pD
            xpar = pc3 % 2
            xt_t[xpar] = pl.op("y", lambda E, nb=nb, tt=tt, xpar=xpar: E["y"].dma_start(
                out=xtile[xpar][:, :],
                in_=xT[nb * 128:(nb + 1) * 128, tt * TN: tt * TN + TN]),
                deps=[xt_use[xpar]], dma=True, sem=f"io{xpar}")
            tk_mm = None
            for dc in range(FC):
                tk_mm = pl.op("t", lambda E, dc=dc, par=par, tt=tt, bank=bank:
                              E["t"].matmul(
                                  bank[:, :], W3[par][:, dc * 128:(dc + 1) * 128],
                                  BT[:, dc * TOK + tt * TN: dc * TOK + tt * TN + TN],
                                  start=(dc == 0), stop=(dc == FC - 1)),
                              deps=[w3ld[par], bank_ev3.get(pcp)],
                              tick=(dc == FC - 1))
            w3use[par] = tk_mm
            ta = pl.op("v", lambda E, xpar=xpar, bank=bank: E["v"].tensor_tensor(
                xpf[xpar][:, :], bank[:, :], xtile[xpar][:, :], OP.add),
                deps=[tk_mm, xt_t[xpar], xp_use[xpar], xpb_use[xpar]], tick=True)
            bank_ev3[pcp] = ta
            xt_use[xpar] = ta
            ts_ = pl.op("y", lambda E, xpar=xpar, nb=nb, tt=tt: E["y"].dma_start(
                out=xpscr.ap()[nb * 128:(nb + 1) * 128, tt * TN: tt * TN + TN],
                in_=xpf[xpar][:, :]), deps=[ta], dma=True, sem=f"io{xpar}")
            last_xpb = pl.op("v", lambda E, xpar=xpar, nb=nb, tt=tt: E["v"].tensor_copy(
                CT[:, nb * TOK + tt * TN: nb * TOK + tt * TN + TN],
                xpf[xpar][:, :]), deps=[ta], tick=True)
            xp_use[xpar] = ts_
            xpb_use[xpar] = last_xpb
            pc3 += 1
    gate_io = [pl.tkt("io0"), pl.tkt("io1")]
    pl.barrier()
    ln2 = []
    ln2fc = []
    for tt in range(2):
        lt, lfc = emit_ln(CT[:, :], tt, last_xpb, m3, r3, var3, sd3, x2b3, lntmp3)
        ln2.append(lt)
        ln2fc.append(lfc)
    if "ph3" in dump:
        _dump(nc, pl, {"d_xn2": CT}, {"d_xp": xpscr})

    pl.op("g", lambda E: E["g"].dma_start(
        out=aaugs.ap().rearrange("p (c e) -> p c e", c=FC),
        in_=aaugT.rearrange("(c p) e -> p c e", p=128)), dma=True, sem="gx")
    pl.op("g", lambda E: E["g"].dma_start(out=balls[:, :], in_=ballT),
          dma=True, sem="gx")
    t_eb = pl.op("g", lambda E: E["g"].dma_start(out=biass[:, :], in_=ebias),
                 dma=True, sem="gx")
    t_aa = t_bb = t_eb

    zmk_t = [None, None]
    prev_zmk = None
    prev_ttp = None
    prev_ttp2 = None
    pc4 = 0
    w4ld = [None, None]
    w4use = [None, None]
    bank_ev4 = {}
    xp_t = [None, None]
    out_use = [None, None]
    y_store = []

    def emit_w4(nb):
        par = nb % 2
        wsrc = wup[:, nb * 128:(nb + 1) * 128].rearrange("(c p) o -> p c o", p=128)
        w4ld[par] = pl.op("g", lambda E, par=par, wsrc=wsrc: E["g"].dma_start(
            out=W3[par].ap().rearrange("p (c o) -> p c o", c=FC), in_=wsrc),
            deps=[w4use[par]], dma=True, sem=f"w{par}")

    def emit_base(nb, tt):
        nonlocal pc4
        par = nb % 2
        pcp = pc4 % 2
        bank = pC if pcp == 0 else pD
        xpar = pc4 % 2
        xp_t[xpar] = pl.op("y", lambda E, nb=nb, tt=tt, xpar=xpar: E["y"].dma_start(
            out=xtile[xpar][:, :],
            in_=xpscr.ap()[nb * 128:(nb + 1) * 128, tt * TN: tt * TN + TN]),
            deps=[out_use[xpar], gate_io[xpar]], dma=True, sem=f"io{xpar}")
        for fc in range(FC):
            pl.op("t", lambda E, fc=fc, par=par, tt=tt, bank=bank: E["t"].matmul(
                bank[:, :], W3[par][:, fc * 128:(fc + 1) * 128],
                CT[:, fc * TOK + tt * TN: fc * TOK + tt * TN + TN],
                start=(fc == 0), stop=False, skip_group_check=True),
                deps=[w4ld[par], ln2fc[tt][fc], bank_ev4.get(pcp)], tick=False)
        pc4 += 1
        return bank, xpar, pcp

    def emit_delta_evac(nb, tt, bank, xpar, pcp):
        tk_mm = pl.op("t", lambda E, nb=nb, tt=tt, bank=bank: E["t"].matmul(
            bank[:, :], balls[:, nb * 128:(nb + 1) * 128], zmk[tt][:, :],
            start=False, stop=True, skip_group_check=True),
            deps=[t_bb, zmk_t[tt]], tick=True)
        w4use[nb % 2] = tk_mm
        ta = pl.op("v", lambda E, xpar=xpar, bank=bank: E["v"].tensor_tensor(
            xpf[xpar][:, :], bank[:, :], xtile[xpar][:, :], OP.add),
            deps=[tk_mm, xp_t[xpar], out_use[xpar]], tick=True)
        bank_ev4[pcp] = ta
        ty = pl.op("y", lambda E, xpar=xpar, nb=nb, tt=tt: E["y"].dma_start(
            out=y[nb * 128:(nb + 1) * 128, tt * TN: tt * TN + TN],
            in_=xpf[xpar][:, :]), deps=[ta], dma=True)
        out_use[xpar] = ty
        y_store.append(ty)

    emit_w4(0)
    for tt in range(2):
        tk_z = None
        prev_v4 = pl.tkt("v")
        for fc in range(FC):
            tk_z = pl.op("t", lambda E, fc=fc, tt=tt: E["t"].matmul(
                pA[0:36, 0:512], aaugs[:, fc * 36:(fc + 1) * 36],
                CT[:, fc * TOK + tt * TN: fc * TOK + tt * TN + TN],
                start=(fc == 0), stop=(fc == FC - 1)),
                deps=[t_aa, ln2fc[tt][fc], prev_v4, prev_ttp],
                tick=(fc == FC - 1))
        t_zf = pl.op("v", lambda E: E["v"].tensor_copy(zsb[:, :], pA[0:36, 0:512]),
                     deps=[tk_z, prev_zmk], tick=True)
        t_lg = pl.op("v", lambda E: E["v"].tensor_scalar(
            lgsb[:, :], pA[32:36, 0:512], biass[32:36, 0:1], 0.0, OP.add),
            deps=[tk_z, t_eb, t_zf, prev_ttp], tick=True)
        bk0, xp0, pp0 = emit_base(0, tt)
        t_tp = None
        for ch in range(4):
            t_tp = pl.op("t", lambda E, ch=ch: E["t"].transpose(
                pA[:, 512 + ch * 4: 512 + ch * 4 + 4],
                lgsb[:, ch * 128:(ch + 1) * 128], identf[0:4, 0:4]),
                deps=[t_lg], tick=(ch == 3))
        t_lgt = pl.op("v", lambda E: E["v"].tensor_copy(lgt[:, :], pA[:, 512:528]),
                      deps=[t_tp, prev_ttp2], tick=True)
        last = t_lgt
        for ch in range(4):
            last = pl.op("v", lambda E, ch=ch: E["v"].reduce_max(
                m4[:, ch:ch + 1], lgt[:, ch * 4:(ch + 1) * 4], axis=AX.X),
                deps=[last], tick=True)
            last = pl.op("v", lambda E, ch=ch: E["v"].tensor_tensor(
                ohtm[:, ch * 4: ch * 4 + 1], lgt[:, ch * 4: ch * 4 + 1],
                m4[:, ch:ch + 1], OP.is_ge), deps=[last], tick=True)
            last = pl.op("v", lambda E, ch=ch: E["v"].tensor_copy(
                tb1[:, 0:1], ohtm[:, ch * 4: ch * 4 + 1]), deps=[last], tick=True)
            for e_ in range(1, 4):
                last = pl.op("v", lambda E, ch=ch, e_=e_: E["v"].tensor_tensor(
                    tb2[:, 0:1], lgt[:, ch * 4 + e_: ch * 4 + e_ + 1],
                    m4[:, ch:ch + 1], OP.is_ge), deps=[last], tick=True)
                last = pl.op("v", lambda E: E["v"].tensor_scalar(
                    tb2[:, 1:2], tb1[:, 0:1], -1.0, 1.0, OP.mult, OP.add),
                    deps=[last], tick=True)
                last = pl.op("v", lambda E, ch=ch, e_=e_: E["v"].tensor_tensor(
                    ohtm[:, ch * 4 + e_: ch * 4 + e_ + 1], tb2[:, 0:1],
                    tb2[:, 1:2], OP.mult), deps=[last], tick=True)
                last = pl.op("v", lambda E, ch=ch, e_=e_: E["v"].tensor_tensor(
                    tb1[:, 0:1], tb1[:, 0:1],
                    ohtm[:, ch * 4 + e_: ch * 4 + e_ + 1], OP.add),
                    deps=[last], tick=True)
        t_tp2 = None
        for ch in range(4):
            t_tp2 = pl.op("t", lambda E, ch=ch: E["t"].transpose(
                pB[0:4, 512 + ch * 128: 512 + ch * 128 + 128],
                ohtm[:, ch * 4:(ch + 1) * 4], identf[:, :]),
                deps=[last], tick=(ch == 3))
        t_oh = pl.op("v", lambda E: E["v"].tensor_copy(
            ohsb[:, :], pB[0:4, 512:1024]), deps=[t_tp2], tick=True)
        t_ohx = pl.op("t", lambda E: E["t"].matmul(
            pB[0:32, 0:512], e32[:, :], ohsb[:, :], start=True, stop=True),
            deps=[t_oh], tick=True)
        zmk_t[tt] = pl.op("v", lambda E, tt=tt: E["v"].tensor_tensor(
            zmk[tt][:, :], zsb[0:32, :], pB[0:32, 0:512], OP.mult),
            deps=[t_ohx, t_zf], tick=True)
        prev_zmk = zmk_t[tt]
        prev_ttp = t_tp
        prev_ttp2 = t_tp2
        emit_delta_evac(0, tt, bk0, xp0, pp0)

    for nb in range(1, FC):
        emit_w4(nb)
        for tt in range(2):
            bk, xp_, pp = emit_base(nb, tt)
            emit_delta_evac(nb, tt, bk, xp_, pp)

    if "ph4" in dump:
        pl.barrier()
        for nm, tns, shp, dt in (("d_zmk0", zmk[0], [32, TN], BF16),
                                 ("d_zmk1", zmk[1], [32, TN], BF16),
                                 ("d_zsb", zsb, [36, TN], F32),
                                 ("d_ohsb", ohsb, [4, TN], BF16),
                                 ("d_lgt", lgt, [128, 16], F32)):
            dap = nc.dram_tensor(nm, shp, dt, kind="ExternalOutput").ap()
            pl.op("y", lambda E, dap=dap, tns=tns: E["y"].dma_start(
                out=dap, in_=tns[:, :]), dma=True, sem="dbg")

    pl.barrier()

    SEMNAMES = ["t", "v", "s", "g", "y", "ck", "cv", "gx", "gy", "gz", "gw",
                "w0", "w1", "kv0", "kv1", "io0", "io1", "dbg", "rs"]
    with ExitStack() as st:
        SEMS = {n: st.enter_context(nc.semaphore(n)) for n in SEMNAMES}
        block = st.enter_context(nc.Block())
        E = {"t": nc.tensor, "v": nc.vector, "s": nc.scalar,
             "g": nc.gpsimd, "y": nc.sync}

        def replay_one(r):
            e = E[r["eng"]]
            for (sm, v) in r["waits"]:
                e.wait_ge(SEMS[sm], v)
            if r["fn"] is not None:
                ins = r["fn"](E)
                if r["inc"] is not None:
                    sm, n, raw = r["inc"]
                    if raw:
                        ins.then_inc(SEMS[sm])
                    else:
                        ins.then_inc(SEMS[sm], n)

        def replay(recs):
            i = 0
            while i < len(recs):
                r = recs[i]
                if r.get("ctl") == "sw2_begin":
                    slot = r["slot"]
                    j = i + 1
                    body = []
                    while recs[j].get("ctl") != "sw2_end":
                        body.append(recs[j])
                        j += 1
                    deltas = recs[j]["deltas"]
                    assert all(rr["eng"] in ("t", "s") for rr in body)
                    for ename, reg in (("t", regT), ("s", regS)):
                        e = E[ename]
                        e.reg_load(reg, ifl[0:1, slot:slot + 1])
                        with e.If_ne(reg, 0):
                            for rr in body:
                                if rr["eng"] == ename:
                                    replay_one(rr)
                        with e.Else():
                            if deltas.get(ename):
                                e.sem_inc(SEMS[ename], deltas[ename])
                        e.end_ifs()
                    i = j + 1
                    continue
                replay_one(r)
                i += 1

        replay(pl.segs["main0"])
        pid = nc.partition_id()
        for i in nc.Switch(engines=[nc.sync, nc.scalar], index=[pid, pid], n=NCORE):
            replay(pl.segs[f"sw{i}"])
        replay(pl.segs["main1"])

    return nc


def _dump(nc, pl, sbuf_map, dram_map):
    pl.barrier()
    for name, t in sbuf_map.items():
        d = nc.dram_tensor(name, [128, FC * TOK], BF16, kind="ExternalOutput").ap()
        pl.op("y", lambda E, d=d, t=t: E["y"].dma_start(out=d, in_=t[:, :]),
              dma=True, sem="dbg")
    for name, t in dram_map.items():
        h = t.ap()
        d = nc.dram_tensor(name, list(h.shape), BF16 if name != "d_xp" else F32,
                           kind="ExternalOutput").ap()
        pl.op("y", lambda E, d=d, h=h: E["y"].dma_start(out=d, in_=h),
              dma=True, sem="dbg")
    pl.barrier()


def _prep_inputs(L0_attn_x, L0_attn_wq, L0_attn_wk, L0_attn_wv, L0_attn_wo,
                 L0_ffn_W_up, L0_ffn_gate_w, L0_ffn_expert_bias,
                 L0_ffn_lora_A, L0_ffn_lora_B):
    f32 = lambda a: np.ascontiguousarray(np.asarray(a), dtype=np.float32)
    x = f32(L0_attn_x)[0]
    wq = f32(L0_attn_wq)
    wk = f32(L0_attn_wk)
    wv = f32(L0_attn_wv)
    wo = f32(L0_attn_wo)
    wup = np.ascontiguousarray(f32(L0_ffn_W_up).T)
    gate = f32(L0_ffn_gate_w)
    ebias_ = f32(L0_ffn_expert_bias)
    la = f32(L0_ffn_lora_A)
    lb = f32(L0_ffn_lora_B)

    aaug = np.concatenate([la.reshape(32, D), gate], axis=0)
    aaugT = np.ascontiguousarray(aaug.T)
    ballT = np.ascontiguousarray(lb.transpose(0, 2, 1).reshape(32, D))
    eb36 = np.zeros((36, 1), np.float32)
    eb36[32:36, 0] = ebias_

    identf = np.eye(128, dtype=np.float32)
    onesb = np.ones((128, 128), dtype=BF)
    psw = np.zeros((128, 128), np.float32)
    for k in range(128):
        psw[k, k ^ 1] = 1.0
    pswap = psw.astype(BF)
    e32 = np.zeros((4, 32), np.float32)
    for e_ in range(4):
        e32[e_, e_ * 8:(e_ + 1) * 8] = 1.0
    e32 = e32.astype(BF)
    kk = np.arange(128)[:, None]
    qq = np.arange(TN)[None, :]
    dm = np.zeros((128, 4 * TN), np.float32)
    for j in range(4):
        dm[:, j * TN:(j + 1) * TN] = (qq >= 128 * j + kk).astype(np.float32)
    dmask = dm.astype(BF)

    inv = 1.0 / (10000.0 ** (np.arange(0, HD, 2, dtype=np.float64) / HD))

    in_maps = []
    for c in range(NCORE):
        bA, bB = c, 15 - c
        toks = np.concatenate([np.arange(bA * TN, (bA + 1) * TN),
                               np.arange(bB * TN, (bB + 1) * TN)])
        xc = np.ascontiguousarray(
            np.concatenate([x[bA * TN:(bA + 1) * TN], x[bB * TN:(bB + 1) * TN]], 0).T)
        ang = inv[:, None] * toks[None, :].astype(np.float64)
        cosr = np.cos(ang)
        sinr = np.sin(ang)
        cosT = np.repeat(cosr, 2, axis=0).astype(np.float32)
        ssinT = np.repeat(sinr, 2, axis=0).astype(np.float32)
        ssinT[0::2, :] *= -1.0
        sf = np.zeros((128, NSLOT * 128), np.float32)
        ifl_np = np.zeros((1, 32), np.int32)
        for s_ in range(NSLOT):
            if slot_block(c, s_) is not None:
                sf[:, s_ * 128:(s_ + 1) * 128] = 1.0
                ifl_np[0, s_] = 1
        in_maps.append({
            "xT": xc, "wq": wq, "wk": wk, "wv": wv, "wo": wo, "wup": wup,
            "aaugT": aaugT, "ballT": ballT, "ebias": eb36,
            "cosT": cosT, "ssinT": ssinT, "identf": identf,
            "onesb": onesb, "pswap": pswap, "e32": e32,
            "dmask": dmask, "sflags": sf.astype(BF), "iflags": ifl_np,
        })
    return in_maps


def kernel(L0_attn_x, L0_attn_wq, L0_attn_wk, L0_attn_wv, L0_attn_wo,
           L0_ffn_W_up, L0_ffn_gate_w, L0_ffn_expert_bias,
           L0_ffn_lora_A, L0_ffn_lora_B):
    in_maps = _prep_inputs(L0_attn_x, L0_attn_wq, L0_attn_wk, L0_attn_wv,
                           L0_attn_wo, L0_ffn_W_up, L0_ffn_gate_w,
                           L0_ffn_expert_bias, L0_ffn_lora_A, L0_ffn_lora_B)
    nc = build()
    res = run_bass_kernel_spmd(nc, in_maps, core_ids=list(range(NCORE)))
    out = np.empty((1, S, D), np.float32)
    for c in range(NCORE):
        yc = np.asarray(res.results[c]["y"])
        bA, bB = c, 15 - c
        out[0, bA * TN:(bA + 1) * TN] = yc[:, 0:TN].T
        out[0, bB * TN:(bB + 1) * TN] = yc[:, TN:2 * TN].T
    return out


# version 19
# speedup vs baseline: 1.0596x; 1.0077x over previous
import numpy as np
import ml_dtypes
from contextlib import ExitStack

import concourse.bass as bass
import concourse.mybir as mybir
from concourse.bass_utils import run_bass_kernel_spmd

F32 = mybir.dt.float32
BF16 = mybir.dt.bfloat16
AF = mybir.ActivationFunctionType
OP = mybir.AluOpType
AX = mybir.AxisListType
BF = ml_dtypes.bfloat16

D = 4096
S = 8192
NH = 32
NKV = 8
HD = 128
FC = 32
TOK = 1024
TN = 512
NB = 16
NCORE = 8
EPS = 1e-5
CSH = 12.0
QKS = 1.0 / np.sqrt(128.0)
NSLOT = 24

ENGS = ("t", "v", "s", "g", "y")


def owner(b):
    return b if b < 8 else 15 - b


def owner_tt(b):
    return 0 if b < 8 else 1


def slot_block(c, s):
    if s < 7:
        return s if s < c else None
    if s == 7:
        return c
    if s < 23:
        return (s - 8) if (s - 8) < 15 - c else None
    return 15 - c


class Plan:
    def __init__(self):
        self.segs = {}
        self.cur = None
        self.cnt = {}
        self.lastw = {e: {} for e in ENGS}
        self.last_op = {}

    def seg(self, name):
        self.cur = self.segs.setdefault(name, [])

    def snap(self):
        import copy
        return (dict(self.cnt), copy.deepcopy(self.lastw), dict(self.last_op))

    def restore(self, st):
        import copy
        self.cnt = dict(st[0])
        self.lastw = copy.deepcopy(st[1])
        self.last_op = dict(st[2])

    def op(self, eng, fn, deps=(), tick=False, dma=False, sem=None, rawinc=False):
        waits = []
        for d in deps:
            if d is None:
                continue
            s, v = d
            if v <= 0:
                continue
            if self.lastw[eng].get(s, -1) >= v:
                continue
            self.lastw[eng][s] = v
            waits.append((s, v))
        inc = None
        if dma:
            tick = True
        if tick:
            sm = sem or eng
            n = 16 if dma else 1
            self.cnt[sm] = self.cnt.get(sm, 0) + n
            inc = (sm, n, rawinc)
        rec = dict(eng=eng, fn=fn, waits=waits, inc=inc)
        self.cur.append(rec)
        if fn is not None:
            self.last_op[eng] = rec
        if inc:
            return (inc[0], self.cnt[inc[0]])
        return None

    def tkt(self, sm):
        return (sm, self.cnt.get(sm, 0))

    def ensure_tick(self, eng):
        rec = self.last_op.get(eng)
        if rec is not None and rec["inc"] is None:
            self.cnt[eng] = self.cnt.get(eng, 0) + 1
            rec["inc"] = (eng, 1, False)
        return (eng, self.cnt.get(eng, 0))

    DMASEMS = ("y", "gx", "gy", "gz", "gw", "w0", "w1", "kv0", "kv1", "kv2",
               "io0", "io1", "dbg", "rs")

    def sw2_begin(self, slot):
        self.cur.append(dict(eng=None, ctl="sw2_begin", slot=slot, fn=None))
        import copy
        self._sw2_lastw = copy.deepcopy(self.lastw)
        self._sw2_cnt = dict(self.cnt)

    def sw2_end(self):
        import copy
        deltas = {}
        for sm in ("t", "s"):
            d = self.cnt.get(sm, 0) - self._sw2_cnt.get(sm, 0)
            if d:
                deltas[sm] = d
        self.lastw = copy.deepcopy(self._sw2_lastw)
        self.cur.append(dict(eng=None, ctl="sw2_end", deltas=deltas, fn=None))

    def barrier(self):
        tks = {e: self.ensure_tick(e) for e in ENGS}
        dts = [self.tkt(s) for s in self.DMASEMS]
        for e in ENGS:
            self.op(e, None, deps=[tks[o] for o in ENGS if o != e] + dts)


def build(dump=()):
    nc = bass.Bass("TRN2", target_bir_lowering=False, debug=False)

    xT = nc.dram_tensor("xT", [D, TOK], F32, kind="ExternalInput").ap()
    wq = nc.dram_tensor("wq", [D, D], F32, kind="ExternalInput").ap()
    wk = nc.dram_tensor("wk", [D, 1024], F32, kind="ExternalInput").ap()
    wv = nc.dram_tensor("wv", [D, 1024], F32, kind="ExternalInput").ap()
    wo = nc.dram_tensor("wo", [D, D], F32, kind="ExternalInput").ap()
    wup = nc.dram_tensor("wup", [D, D], F32, kind="ExternalInput").ap()
    aaugT = nc.dram_tensor("aaugT", [D, 36], F32, kind="ExternalInput").ap()
    ballT = nc.dram_tensor("ballT", [32, D], F32, kind="ExternalInput").ap()
    ebias = nc.dram_tensor("ebias", [36, 1], F32, kind="ExternalInput").ap()
    cosT_d = nc.dram_tensor("cosT", [128, TOK], F32, kind="ExternalInput").ap()
    ssinT_d = nc.dram_tensor("ssinT", [128, TOK], F32, kind="ExternalInput").ap()
    identf_d = nc.dram_tensor("identf", [128, 128], F32, kind="ExternalInput").ap()
    onesb_d = nc.dram_tensor("onesb", [128, 128], BF16, kind="ExternalInput").ap()
    pswap_d = nc.dram_tensor("pswap", [128, 128], BF16, kind="ExternalInput").ap()
    e32_d = nc.dram_tensor("e32", [4, 32], BF16, kind="ExternalInput").ap()
    dmask_d = nc.dram_tensor("dmask", [128, 4 * TN], BF16, kind="ExternalInput").ap()
    sflags_d = nc.dram_tensor("sflags", [128, NSLOT * 128], BF16, kind="ExternalInput").ap()
    iflags_d = nc.dram_tensor("iflags", [1, 32], mybir.dt.int32, kind="ExternalInput").ap()
    y = nc.dram_tensor("y", [D, TOK], F32, kind="ExternalOutput").ap()

    ksrc = nc.dram_tensor("ksrc", [2 * 1024, 512], BF16, kind="Internal")
    vsrc = nc.dram_tensor("vsrc", [1024, 1024], BF16, kind="Internal")
    kdst = nc.dram_tensor("kdst", [NCORE * 2 * 1024, 512], BF16, kind="Internal",
                          addr_space="Shared")
    vdst = nc.dram_tensor("vdst", [NCORE * 1024, 1024], BF16, kind="Internal",
                          addr_space="Shared")
    keff = nc.dram_tensor("keff", [NSLOT * 1024, 512], BF16, kind="Internal")
    veff = nc.dram_tensor("veff", [NSLOT * 8 * 4 * 128, 128], BF16, kind="Internal")
    xpscr = nc.dram_tensor("xpscr", [D, TOK], F32, kind="Internal")

    kdst_f = kdst.ap().rearrange("a b -> (a b)")
    vdst_f = vdst.ap().rearrange("a b -> (a b)")
    keff_f = keff.ap().rearrange("a b -> (a b)")
    veff_f = veff.ap().rearrange("a b -> (a b)")
    vsrc_f = vsrc.ap().rearrange("a b -> (a b)")

    SB0 = 20480

    def at(name, shape, dtype, off):
        return nc.alloc_sbuf_tensor_at(name, shape, dtype, offset=SB0 + off)

    AT = at("A_xn", [128, FC * TOK], BF16, 0)
    BT = at("B_av", [128, FC * TOK], BF16, 65536)
    CT = at("C_q", [128, FC * TOK], BF16, 131072)

    identf = at("identf", [128, 128], F32, 196608)
    onesb = at("onesb", [128, 128], BF16, 197120)
    pswap = at("pswap", [128, 128], BF16, 197376)
    e32 = at("e32", [4, 32], BF16, 197632)
    dmask = at("dmask", [128, 4 * TN], BF16, 197696)
    sflags = at("sflags", [128, NSLOT * 128], BF16, 201792)
    cm12 = at("cm12", [128, 1], F32, 207936)
    ifl = at("ifl", [1, 32], mybir.dt.int32, 207968)

    W1 = [at(f"W1_{i}", [128, FC * 128], BF16, 65536 + i * 8192) for i in range(2)]
    cosT = at("cosT", [128, TOK], F32, 81920)
    ssinT = at("ssinT", [128, TOK], F32, 86016)
    m1 = at("m1", [128, TN], F32, 90112)
    r1 = at("r1", [128, TN], F32, 92160)
    var1 = at("var1", [128, TN], F32, 94208)
    sd1 = at("sd1", [128, TN], F32, 96256)
    x2b = [at(f"x2b_{i}", [128, TN], BF16, 98304 + i * 1024) for i in range(2)]
    lntmp = [at(f"lntmp_{i}", [128, TN], F32, 100352 + i * 2048) for i in range(2)]
    ktmp = [at(f"ktmp_{i}", [128, TN], BF16, 104448 + i * 1024) for i in range(2)]
    tR1 = [at(f"tR1_{i}", [128, TN], F32, 106496 + i * 2048) for i in range(2)]
    tR2 = [at(f"tR2_{i}", [128, TN], F32, 110592 + i * 2048) for i in range(2)]
    vtmp = [at(f"vtmp_{i}", [128, TN], F32, 114688 + i * 2048) for i in range(2)]
    vtt = [at(f"vtt_{i}", [128, 128], BF16, 118784 + i * 256) for i in range(2)]
    kout = [at(f"kout_{i}", [128, TN], BF16, 119296 + i * 1024) for i in range(2)]
    zsb0 = at("zsb0", [128, 4096], BF16, 121344)

    kb = [at(f"kb_{i}", [128, 512], BF16, 32768 + i * 1024) for i in range(3)]
    vb = [at(f"vb_{i}", [128, 512], BF16, 35840 + i * 1024) for i in range(3)]
    attb = [at(f"attb_{i}", [128, 1024], BF16, 38912 + i * 2048) for i in range(2)]
    rinv = [at(f"rinv_{i}", [128, 512], F32, 43008 + i * 2048) for i in range(2)]
    avraw = [at(f"avraw_{i}", [128, 512], F32, 47104 + i * 2048) for i in range(2)]
    smraw = [at(f"smraw_{i}", [128, 512], F32, 51200 + i * 2048) for i in range(2)]

    W3 = [at(f"W3_{i}", [128, FC * 128], BF16, 0 + i * 8192) for i in range(2)]
    xtile = [at(f"xtile_{i}", [128, TN], F32, 16384 + i * 2048) for i in range(2)]
    xpf = [at(f"xpf_{i}", [128, TN], F32, 20480 + i * 2048) for i in range(2)]
    m3 = at("m3", [128, TN], F32, 24576)
    r3 = at("r3", [128, TN], F32, 26624)
    var3 = at("var3", [128, TN], F32, 28672)
    sd3 = at("sd3", [128, TN], F32, 30720)
    x2b3 = [at(f"x2b3_{i}", [128, TN], BF16, 32768 + i * 1024) for i in range(2)]
    lntmp3 = [at(f"lntmp3_{i}", [128, TN], F32, 34816 + i * 2048) for i in range(2)]

    zsb = at("zsb", [36, TN], F32, 40960)
    zmk = [at(f"zmk_{i}", [32, TN], BF16, 43008 + i * 1024) for i in range(2)]
    lgsb = at("lgsb", [4, TN], F32, 45056)
    lgt = at("lgt", [128, 16], F32, 47104)
    m4 = at("m4", [128, 4], F32, 47232)
    ohtm = at("ohtm", [128, 16], F32, 47360)
    tb1 = at("tb1", [128, 4], F32, 47488)
    tb2 = at("tb2", [128, 4], F32, 47552)
    ohsb = at("ohsb", [4, TN], BF16, 47616)
    aaugs = at("aaugs", [128, FC * 36], BF16, 65536)
    balls = at("balls", [32, D], BF16, 68096)
    biass = at("biass", [36, 1], F32, 76288)

    pA = nc.alloc_psum_tensor("pA", [128, 1024], F32)
    pB = nc.alloc_psum_tensor("pB", [128, 1024], F32)
    pC = nc.alloc_psum_tensor("pC", [128, 512], F32)
    pD = nc.alloc_psum_tensor("pD", [128, 512], F32)
    pE = nc.alloc_psum_tensor("pE", [128, 512], F32)
    pF = nc.alloc_psum_tensor("pF", [128, 512], F32)

    pl = Plan()
    pl.seg("main0")

    tload = []
    for dst, src in ((identf, identf_d), (onesb, onesb_d), (pswap, pswap_d),
                     (e32, e32_d), (dmask, dmask_d), (sflags, sflags_d),
                     (cosT, cosT_d), (ssinT, ssinT_d), (ifl, iflags_d)):
        tload.append(pl.op("y", lambda E, dst=dst, src=src:
                           E["y"].dma_start(out=dst[:, :], in_=src), dma=True))
    t_tables = pl.tkt("y")
    t_zsb0 = pl.op("v", lambda E: E["v"].memset(zsb0[:, :], 0.0), tick=True)
    t_cm12 = pl.op("v", lambda E: E["v"].memset(cm12[:, :], -CSH), tick=True)

    x_src = xT.rearrange("(c p) t -> p c t", p=128)
    XSEM = ("gx", "gy", "gz", "gw")
    t_xq = []
    for q4 in range(4):
        t_xq.append(pl.op("g", lambda E, q4=q4: E["g"].dma_start(
            out=AT.ap().rearrange("p (c t) -> p c t", c=FC)[:, q4 * 8:(q4 + 1) * 8, :],
            in_=x_src[:, q4 * 8:(q4 + 1) * 8, :]),
            dma=True, sem=XSEM[q4]))
    wk_pre = [None, None]
    for nb in range(2):
        wsrc = wk[:, nb * 128:(nb + 1) * 128].rearrange("(c p) o -> p c o", p=128)
        wk_pre[nb] = pl.op("g", lambda E, nb=nb, wsrc=wsrc: E["g"].dma_start(
            out=W1[nb].ap().rearrange("p (c o) -> p c o", c=FC), in_=wsrc),
            deps=[], dma=True, sem=f"w{nb}")
    t_xl = [t_xq[3], t_xq[3]]

    def emit_ln(arena, tt, dep_in, mT, rT, varT, sdT, x2T, lnT, dep_extra=(),
                dep_fn=None):
        def sl(fc):
            return arena[:, fc * TOK + tt * TN: fc * TOK + tt * TN + TN]
        if dep_fn is None:
            dep_fn = lambda fc: dep_in
        prev_v = pl.tkt("v")
        prev_t = pl.tkt("t")
        tk_s1 = None
        for fc in range(FC):
            tk_s1 = pl.op("t", lambda E, fc=fc: E["t"].matmul(
                pE[:, :], onesb[:, :], sl(fc), start=(fc == 0), stop=(fc == FC - 1)),
                deps=[dep_fn(fc), t_tables, prev_v] + list(dep_extra),
                tick=(fc == FC - 1))
        sq_pe = [None, None]
        for fc in range(FC):
            par = fc % 2
            d = [dep_fn(fc), prev_t]
            if sq_pe[par] is not None:
                d.append(sq_pe[par])
            tk_sq = pl.op("s", lambda E, fc=fc, par=par: E["s"].activation(
                x2T[par][:, :], sl(fc), AF.Square), deps=d, tick=True)
            sq_pe[par] = pl.op("t", lambda E, par=par, fc=fc: E["t"].matmul(
                pF[:, :], onesb[:, :], x2T[par][:, :],
                start=(fc == 0), stop=(fc == FC - 1)),
                deps=[tk_sq, prev_v], tick=True)
        tk_s2 = sq_pe[(FC - 1) % 2]
        a = pl.op("v", lambda E: E["v"].tensor_scalar(
            mT[:, :], pE[:, :], 1.0 / D, 0.0, OP.mult), deps=[tk_s1], tick=True)
        b = pl.op("v", lambda E: E["v"].tensor_scalar(
            varT[:, :], pF[:, :], 1.0 / D, EPS, OP.mult, OP.add),
            deps=[tk_s2, a], tick=True)
        cgate = pl.op("v", lambda E: E["v"].tensor_tensor(
            sdT[:, :], mT[:, :], mT[:, :], OP.mult), deps=[b], tick=True)
        dd = pl.op("v", lambda E: E["v"].tensor_tensor(
            varT[:, :], varT[:, :], sdT[:, :], OP.subtract), deps=[cgate], tick=True)
        ee = pl.op("s", lambda E: E["s"].activation(
            sdT[:, :], varT[:, :], AF.Sqrt), deps=[dd], tick=True)
        ff = pl.op("v", lambda E: E["v"].reciprocal(rT[:, :], sdT[:, :]),
                   deps=[ee], tick=True)
        lastv = ff
        lastg = None
        fcticks = [None] * FC
        for fc in range(FC):
            par = fc % 2
            eng = "v" if par == 0 else "g"
            prev = lastv if par == 0 else lastg
            g1 = pl.op(eng, lambda E, fc=fc, par=par, eng=eng: E[eng].tensor_tensor(
                lnT[par][:, :], sl(fc), mT[:, :], OP.subtract),
                deps=[prev, ff, tk_s2], tick=True)
            tkn = pl.op(eng, lambda E, fc=fc, par=par, eng=eng: E[eng].tensor_tensor(
                sl(fc), lnT[par][:, :], rT[:, :], OP.mult), deps=[g1], tick=True)
            fcticks[fc] = tkn
            if par == 0:
                lastv = tkn
            else:
                lastg = tkn
        last = pl.op("v", lambda E: E["v"].nop(), deps=[lastv, lastg], tick=True)
        return last, fcticks

    ln1 = []
    ln1fc = []
    for tt in range(2):
        lt, lfc = emit_ln(AT[:, :], tt, t_xq[3], m1, r1, var1, sd1, x2b, lntmp,
                          dep_fn=lambda fc: t_xq[fc // 8])
        ln1.append(lt)
        ln1fc.append(lfc)

    mats = [("k", wk, 8), ("v", wv, 8), ("q", wq, 32)]
    pc = 0
    prc = 0
    wld = [None, None]
    wuse = [None, None]
    projev = {}
    pA_use = [None, None]
    ktmp_use = [None, None]
    vtmp_use = [None, None]
    vtt_use = [None, None]
    kout_use = [None, None]
    tR1_use = [None, None]
    tR2_use = [None, None]
    nbg = 0
    kv_store = []
    for kind, wmat, nnb in mats:
        for nb in range(nnb):
            par = nbg % 2
            wsrc = wmat[:, nb * 128:(nb + 1) * 128].rearrange(
                "(c p) o -> p c o", p=128)
            if kind == "k" and nb < 2:
                wld[par] = wk_pre[nb]
            else:
                wld[par] = pl.op("g", lambda E, par=par, wsrc=wsrc: E["g"].dma_start(
                    out=W1[par].ap().rearrange("p (c o) -> p c o", c=FC), in_=wsrc),
                    deps=[wuse[par]], dma=True, sem=f"w{par}")
            for tt in range(2):
                pcp = pc % 2
                bank = pC if pcp == 0 else pD
                tk_mm = None
                for fc in range(FC):
                    tk_mm = pl.op("t", lambda E, fc=fc, par=par, tt=tt, bank=bank:
                                  E["t"].matmul(
                                      bank[:, :],
                                      W1[par][:, fc * 128:(fc + 1) * 128],
                                      AT[:, fc * TOK + tt * TN: fc * TOK + tt * TN + TN],
                                      start=(fc == 0), stop=(fc == FC - 1)),
                                  deps=[wld[par], ln1fc[tt][fc], projev.get(pcp)],
                                  tick=(fc == FC - 1))
                wuse[par] = tk_mm
                if kind in ("k", "q"):
                    kp = pc % 2
                    tcp = pl.op("s", lambda E, bank=bank, kp=kp: E["s"].copy(
                        ktmp[kp][:, :], bank[:, :]),
                        deps=[tk_mm, ktmp_use[kp]], tick=True)
                    projev[pcp] = tcp
                    sl_ = prc % 2
                    tpm = pl.op("t", lambda E, kp=kp, sl_=sl_: E["t"].matmul(
                        pA[:, sl_ * 512:sl_ * 512 + 512], pswap[:, :],
                        ktmp[kp][:, :], start=True, stop=True),
                        deps=[tcp, pA_use[sl_]], tick=True)
                    t1 = pl.op("v", lambda E, kp=kp, tt=tt: E["v"].tensor_tensor(
                        tR1[kp][:, :], ktmp[kp][:, :],
                        cosT[:, tt * TN: tt * TN + TN], OP.mult),
                        deps=[tcp, tR1_use[kp], t_tables], tick=True)
                    t2 = pl.op("v", lambda E, kp=kp, sl_=sl_, tt=tt: E["v"].tensor_tensor(
                        tR2[kp][:, :], pA[:, sl_ * 512:sl_ * 512 + 512],
                        ssinT[:, tt * TN: tt * TN + TN], OP.mult),
                        deps=[tpm, t1, tR2_use[kp]], tick=True)
                    pA_use[sl_] = t2
                    prc += 1
                    ktmp_use[kp] = t2
                    if kind == "q":
                        h = nb
                        tq = pl.op("v", lambda E, kp=kp, h=h, tt=tt: E["v"].tensor_tensor(
                            CT[:, h * TOK + tt * TN: h * TOK + tt * TN + TN],
                            tR1[kp][:, :], tR2[kp][:, :], OP.add),
                            deps=[t2], tick=True)
                        tR1_use[kp] = tq
                        tR2_use[kp] = tq
                    else:
                        ko = pl.op("v", lambda E, kp=kp: E["v"].tensor_tensor(
                            kout[kp][:, :], tR1[kp][:, :], tR2[kp][:, :], OP.add),
                            deps=[t2, kout_use[kp]], tick=True)
                        tR1_use[kp] = ko
                        tR2_use[kp] = ko
                        g_ = nb
                        td = pl.op("y", lambda E, kp=kp, g_=g_, tt=tt: E["y"].dma_start(
                            out=ksrc.ap()[tt * 1024 + g_ * 128: tt * 1024 + g_ * 128 + 128, :],
                            in_=kout[kp][:, :]), deps=[ko], dma=True)
                        kout_use[kp] = td
                        kv_store.append(td)
                else:
                    vp = pc % 2
                    tcp = pl.op("s", lambda E, bank=bank, vp=vp: E["s"].copy(
                        vtmp[vp][:, :], bank[:, :]),
                        deps=[tk_mm, vtmp_use[vp]], tick=True)
                    projev[pcp] = tcp
                    g_ = nb
                    last_v = None
                    for j in range(4):
                        sl_ = prc % 2
                        ttr = pl.op("t", lambda E, vp=vp, j=j, sl_=sl_: E["t"].transpose(
                            pA[:, sl_ * 512:sl_ * 512 + 128],
                            vtmp[vp][:, j * 128:(j + 1) * 128], identf[:, :]),
                            deps=[tcp, pA_use[sl_], t_tables], tick=True)
                        w2 = j % 2
                        tvt = pl.op("s", lambda E, sl_=sl_, w2=w2: E["s"].copy(
                            vtt[w2][:, :], pA[:, sl_ * 512:sl_ * 512 + 128]),
                            deps=[ttr, vtt_use[w2]], tick=True)
                        pA_use[sl_] = tvt
                        prc += 1
                        fo = (g_ * 8 + tt * 4 + j) * 16384
                        tst = pl.op("y", lambda E, w2=w2, fo=fo: E["y"].dma_start(
                            out=vsrc_f[fo:fo + 16384].rearrange("(p d) -> p d", p=128),
                            in_=vtt[w2][:, :]), deps=[tvt], dma=True)
                        vtt_use[w2] = tst
                        kv_store.append(tst)
                        last_v = tst
                    vtmp_use[vp] = last_v
                pc += 1
            nbg += 1
            if kind == "v" and nb == nnb - 1:
                tk_cc_k = pl.op("g", lambda E: E["g"].collective_compute(
                    "AllGather", OP.bypass,
                    replica_groups=[list(range(NCORE))],
                    ins=[ksrc.ap().opt()], outs=[kdst.ap().opt()]),
                    deps=kv_store, tick=True, sem="ck", rawinc=True)
                tk_cc_v = pl.op("g", lambda E: E["g"].collective_compute(
                    "AllGather", OP.bypass,
                    replica_groups=[list(range(NCORE))],
                    ins=[vsrc.ap().opt()], outs=[vdst.ap().opt()]),
                    deps=[], tick=True, sem="cv", rawinc=True)

    base = pl.snap()
    finals = None
    for c in range(NCORE):
        pl.restore(base)
        pl.seg(f"sw{c}")
        first = [("ck", 1), ("cv", 1), t_zsb0]
        for s_ in range(NSLOT):
            b = slot_block(c, s_)
            if b is not None:
                r = owner(b)
                tw = owner_tt(b)
                pl.op("y", lambda E, s_=s_, r=r, tw=tw: E["y"].dma_start(
                    out=keff.ap()[s_ * 1024:(s_ + 1) * 1024, :],
                    in_=kdst.ap()[r * 2048 + tw * 1024: r * 2048 + tw * 1024 + 1024, :]),
                    deps=first, dma=True)
                for g_ in range(8):
                    so = r * (1024 * 1024) + (g_ * 8 + tw * 4) * 16384
                    do = (s_ * 8 + g_) * 65536
                    pl.op("y", lambda E, so=so, do=do: E["y"].dma_start(
                        out=veff_f[do:do + 65536],
                        in_=vdst_f[so:so + 65536]), deps=first, dma=True)
            else:
                pl.op("y", lambda E, s_=s_: E["y"].dma_start(
                    out=keff_f[s_ * 524288: (s_ + 1) * 524288].rearrange(
                        "(p f) -> p f", p=128),
                    in_=zsb0[:, :]), deps=first, dma=True)
                pl.op("y", lambda E, s_=s_: E["y"].dma_start(
                    out=veff_f[s_ * 524288: (s_ + 1) * 524288].rearrange(
                        "(p f) -> p f", p=128),
                    in_=zsb0[:, :]), deps=first, dma=True)
            first = [t_zsb0]
        f = pl.snap()
        if finals is None:
            finals = f[0]
        else:
            assert finals == f[0], f"switch body sem counts differ: {finals} vs {f[0]}"
    pl.restore(base)
    pl.cnt = dict(finals)
    t_relay = pl.tkt("y")
    t_relay2 = pl.tkt("rs")

    pl.seg("main1")
    pl.barrier()

    if "ph1" in dump:
        _dump(nc, pl, {"d_xn": AT, "d_q": CT}, {"d_ksrc": ksrc, "d_vsrc": vsrc})

    regT = nc.alloc_register(mybir.EngineType.PE, "fT")
    regS = nc.alloc_register(mybir.EngineType.Activation, "fS")
    cc = 0
    exp_t = [None, None]
    att_use = [None, None]
    kb_t = [None, None, None]
    vb_t = [None, None, None]
    kb_use = [None, None, None]
    vb_use = [None, None, None]
    avev = {0: None, 1: None}
    rn_use = {0: None, 1: None}
    for g_ in range(8):
        for pr in range(2):
            for sub, srange, qoff in ((0, [7] + list(range(0, 7)), 0),
                                      (1, [23] + list(range(8, 23)), TN)):
                s0 = srange[0]
                s1 = srange[-1]
                def emit_avsums(p):
                    tk_av = None
                    for h2 in range(2):
                        avb = pC if h2 == 0 else pD
                        smb = pE if h2 == 0 else pF
                        pl.op("t", lambda E, lp=p["lp"], j=p["j"], ab=p["ab"], h2=h2,
                              avb=avb, fm=p["fm"]: E["t"].matmul(
                                  avb[:, :], vb[lp][:, j * 128:(j + 1) * 128],
                                  attb[ab][:, h2 * 512:h2 * 512 + 512],
                                  start=fm, stop=True, skip_group_check=True),
                              deps=[p["msk"], p["vbt"], avev[h2]], tick=False)
                        tk_av = pl.op("t", lambda E, s_=p["s"], ab=p["ab"], h2=h2,
                                      smb=smb, fm=p["fm"]: E["t"].matmul(
                                          smb[:, :],
                                          sflags[:, s_ * 128:(s_ + 1) * 128],
                                          attb[ab][:, h2 * 512:h2 * 512 + 512],
                                          start=fm, stop=True,
                                          skip_group_check=True),
                                      deps=[], tick=(h2 == 1))
                    att_use[p["ab"]] = tk_av
                    kb_use[p["lp"]] = tk_av
                    vb_use[p["lp"]] = tk_av

                pending = []
                for si_, s_ in enumerate(srange):
                    lp = (sub * 8 + si_) % 3
                    kb_t[lp] = pl.op("y", lambda E, s_=s_, g_=g_, lp=lp: E["y"].dma_start(
                        out=kb[lp][:, :],
                        in_=keff.ap()[s_ * 1024 + g_ * 128: s_ * 1024 + g_ * 128 + 128, :]),
                        deps=[t_relay, t_relay2, kb_use[lp]], dma=True, sem=f"kv{lp}")
                    vb_t[lp] = pl.op("y", lambda E, s_=s_, g_=g_, lp=lp: E["y"].dma_start(
                        out=vb[lp].ap().rearrange("p (j d) -> p j d", j=4),
                        in_=veff.ap()[(s_ * 8 + g_) * 512: (s_ * 8 + g_) * 512 + 512, :]
                        .rearrange("(j p) d -> p j d", p=128)),
                        deps=[t_relay, t_relay2, vb_use[lp]], dma=True, sem=f"kv{lp}")
                    diag = s_ in (7, 23)
                    for j in range(4):
                        sc = cc % 2
                        scb = pA if sc == 0 else pB
                        tk_sc = None
                        for h2 in range(2):
                            h = g_ * 4 + pr * 2 + h2
                            tk_sc = pl.op("t", lambda E, lp=lp, j=j, h=h, h2=h2, scb=scb, qoff=qoff:
                                          E["t"].matmul(
                                              scb[:, h2 * 512:h2 * 512 + 512],
                                              kb[lp][:, j * 128:(j + 1) * 128],
                                              CT[:, h * TOK + qoff: h * TOK + qoff + TN],
                                              start=True, stop=True),
                                          deps=[kb_t[lp], exp_t[sc]],
                                          tick=(h2 == 1))
                        ab = cc % 2
                        tk_ex = pl.op("s", lambda E, ab=ab, scb=scb: E["s"].activation(
                            attb[ab][:, :], scb[:, :], AF.Exp,
                            bias=cm12[:, 0:1], scale=QKS),
                            deps=[tk_sc, att_use[ab], t_cm12], tick=True)
                        exp_t[sc] = tk_ex
                        tk_msk = tk_ex
                        if diag:
                            tm1 = pl.op("g", lambda E, ab=ab, j=j: E["g"].tensor_tensor(
                                attb[ab][:, 0:512], attb[ab][:, 0:512],
                                dmask[:, j * TN:(j + 1) * TN], OP.mult),
                                deps=[tk_ex, t_tables], tick=True)
                            tk_msk = pl.op("g", lambda E, ab=ab, j=j: E["g"].tensor_tensor(
                                attb[ab][:, 512:1024], attb[ab][:, 512:1024],
                                dmask[:, j * TN:(j + 1) * TN], OP.mult),
                                deps=[tm1], tick=True)
                        pending.append(dict(lp=lp, j=j, ab=ab, msk=tk_msk,
                                            vbt=vb_t[lp], s=s_,
                                            fm=(s_ == s0 and j == 0)))
                        if len(pending) > 1:
                            emit_avsums(pending.pop(0))
                        cc += 1
                while pending:
                    emit_avsums(pending.pop(0))
                tk_raw = []
                for h2 in range(2):
                    avb = pC if h2 == 0 else pD
                    smb = pE if h2 == 0 else pF
                    c1 = pl.op("v", lambda E, h2=h2, avb=avb: E["v"].tensor_copy(
                        avraw[h2][:, :], avb[:, :]),
                        deps=[pl.tkt("t"), avev[h2], rn_use[h2]], tick=True)
                    c2 = pl.op("v", lambda E, h2=h2, smb=smb: E["v"].tensor_copy(
                        smraw[h2][:, :], smb[:, :]), deps=[c1], tick=True)
                    avev[h2] = c2
                    tk_raw.append(c2)
                for h2 in range(2):
                    h = g_ * 4 + pr * 2 + h2
                    tr = pl.op("v", lambda E, h2=h2: E["v"].reciprocal(
                        rinv[h2][:, :], smraw[h2][:, :]),
                        deps=[tk_raw[h2], rn_use[h2]], tick=True)
                    rn_use[h2] = pl.op("v", lambda E, h2=h2, h=h, qoff=qoff:
                                       E["v"].tensor_tensor(
                                           BT[:, h * TOK + qoff: h * TOK + qoff + TN],
                                           avraw[h2][:, :], rinv[h2][:, :], OP.mult),
                                       deps=[tr], tick=True)

    w3ld = [None, None]
    w3use = [None, None]
    for nb in range(2):
        wsrc = wo[:, nb * 128:(nb + 1) * 128].rearrange("(c p) o -> p c o", p=128)
        w3ld[nb] = pl.op("g", lambda E, nb=nb, wsrc=wsrc: E["g"].dma_start(
            out=W3[nb].ap().rearrange("p (c o) -> p c o", c=FC), in_=wsrc),
            deps=[], dma=True, sem=f"w{nb}")
    pl.barrier()
    if "ph2" in dump:
        _dump(nc, pl, {"d_av": BT}, {})

    pc3 = 0
    bank_ev3 = {}
    xt_t = [None, None]
    xt_use = [None, None]
    xp_use = [None, None]
    xpb_use = [None, None]
    last_xpb = None
    for nb in range(FC):
        par = nb % 2
        if nb >= 2:
            wsrc = wo[:, nb * 128:(nb + 1) * 128].rearrange("(c p) o -> p c o", p=128)
            w3ld[par] = pl.op("g", lambda E, par=par, wsrc=wsrc: E["g"].dma_start(
                out=W3[par].ap().rearrange("p (c o) -> p c o", c=FC), in_=wsrc),
                deps=[w3use[par]], dma=True, sem=f"w{par}")
        for tt in range(2):
            pcp = pc3 % 2
            bank = pC if pcp == 0 else pD
            xpar = pc3 % 2
            xt_t[xpar] = pl.op("y", lambda E, nb=nb, tt=tt, xpar=xpar: E["y"].dma_start(
                out=xtile[xpar][:, :],
                in_=xT[nb * 128:(nb + 1) * 128, tt * TN: tt * TN + TN]),
                deps=[xt_use[xpar]], dma=True, sem=f"io{xpar}")
            tk_mm = None
            for dc in range(FC):
                tk_mm = pl.op("t", lambda E, dc=dc, par=par, tt=tt, bank=bank:
                              E["t"].matmul(
                                  bank[:, :], W3[par][:, dc * 128:(dc + 1) * 128],
                                  BT[:, dc * TOK + tt * TN: dc * TOK + tt * TN + TN],
                                  start=(dc == 0), stop=(dc == FC - 1)),
                              deps=[w3ld[par], bank_ev3.get(pcp)],
                              tick=(dc == FC - 1))
            w3use[par] = tk_mm
            ta = pl.op("v", lambda E, xpar=xpar, bank=bank: E["v"].tensor_tensor(
                xpf[xpar][:, :], bank[:, :], xtile[xpar][:, :], OP.add),
                deps=[tk_mm, xt_t[xpar], xp_use[xpar], xpb_use[xpar]], tick=True)
            bank_ev3[pcp] = ta
            xt_use[xpar] = ta
            ts_ = pl.op("y", lambda E, xpar=xpar, nb=nb, tt=tt: E["y"].dma_start(
                out=xpscr.ap()[nb * 128:(nb + 1) * 128, tt * TN: tt * TN + TN],
                in_=xpf[xpar][:, :]), deps=[ta], dma=True, sem=f"io{xpar}")
            last_xpb = pl.op("v", lambda E, xpar=xpar, nb=nb, tt=tt: E["v"].tensor_copy(
                CT[:, nb * TOK + tt * TN: nb * TOK + tt * TN + TN],
                xpf[xpar][:, :]), deps=[ta], tick=True)
            xp_use[xpar] = ts_
            xpb_use[xpar] = last_xpb
            pc3 += 1
    gate_io = [pl.tkt("io0"), pl.tkt("io1")]
    pl.barrier()
    ln2 = []
    ln2fc = []
    for tt in range(2):
        lt, lfc = emit_ln(CT[:, :], tt, last_xpb, m3, r3, var3, sd3, x2b3, lntmp3)
        ln2.append(lt)
        ln2fc.append(lfc)
    if "ph3" in dump:
        _dump(nc, pl, {"d_xn2": CT}, {"d_xp": xpscr})

    pl.op("g", lambda E: E["g"].dma_start(
        out=aaugs.ap().rearrange("p (c e) -> p c e", c=FC),
        in_=aaugT.rearrange("(c p) e -> p c e", p=128)), dma=True, sem="gx")
    pl.op("g", lambda E: E["g"].dma_start(out=balls[:, :], in_=ballT),
          dma=True, sem="gx")
    t_eb = pl.op("g", lambda E: E["g"].dma_start(out=biass[:, :], in_=ebias),
                 dma=True, sem="gx")
    t_aa = t_bb = t_eb

    zmk_t = [None, None]
    prev_zmk = None
    prev_ttp = None
    prev_ttp2 = None
    pc4 = 0
    w4ld = [None, None]
    w4use = [None, None]
    bank_ev4 = {}
    xp_t = [None, None]
    out_use = [None, None]
    y_store = []

    def emit_w4(nb):
        par = nb % 2
        wsrc = wup[:, nb * 128:(nb + 1) * 128].rearrange("(c p) o -> p c o", p=128)
        w4ld[par] = pl.op("g", lambda E, par=par, wsrc=wsrc: E["g"].dma_start(
            out=W3[par].ap().rearrange("p (c o) -> p c o", c=FC), in_=wsrc),
            deps=[w4use[par]], dma=True, sem=f"w{par}")

    def emit_base(nb, tt):
        nonlocal pc4
        par = nb % 2
        pcp = pc4 % 2
        bank = pC if pcp == 0 else pD
        xpar = pc4 % 2
        xp_t[xpar] = pl.op("y", lambda E, nb=nb, tt=tt, xpar=xpar: E["y"].dma_start(
            out=xtile[xpar][:, :],
            in_=xpscr.ap()[nb * 128:(nb + 1) * 128, tt * TN: tt * TN + TN]),
            deps=[out_use[xpar], gate_io[xpar]], dma=True, sem=f"io{xpar}")
        for fc in range(FC):
            pl.op("t", lambda E, fc=fc, par=par, tt=tt, bank=bank: E["t"].matmul(
                bank[:, :], W3[par][:, fc * 128:(fc + 1) * 128],
                CT[:, fc * TOK + tt * TN: fc * TOK + tt * TN + TN],
                start=(fc == 0), stop=False, skip_group_check=True),
                deps=[w4ld[par], ln2fc[tt][fc], bank_ev4.get(pcp)], tick=False)
        pc4 += 1
        return bank, xpar, pcp

    def emit_delta_evac(nb, tt, bank, xpar, pcp):
        tk_mm = pl.op("t", lambda E, nb=nb, tt=tt, bank=bank: E["t"].matmul(
            bank[:, :], balls[:, nb * 128:(nb + 1) * 128], zmk[tt][:, :],
            start=False, stop=True, skip_group_check=True),
            deps=[t_bb, zmk_t[tt]], tick=True)
        w4use[nb % 2] = tk_mm
        ta = pl.op("v", lambda E, xpar=xpar, bank=bank: E["v"].tensor_tensor(
            xpf[xpar][:, :], bank[:, :], xtile[xpar][:, :], OP.add),
            deps=[tk_mm, xp_t[xpar], out_use[xpar]], tick=True)
        bank_ev4[pcp] = ta
        ty = pl.op("y", lambda E, xpar=xpar, nb=nb, tt=tt: E["y"].dma_start(
            out=y[nb * 128:(nb + 1) * 128, tt * TN: tt * TN + TN],
            in_=xpf[xpar][:, :]), deps=[ta], dma=True)
        out_use[xpar] = ty
        y_store.append(ty)

    emit_w4(0)
    for tt in range(2):
        tk_z = None
        prev_v4 = pl.tkt("v")
        for fc in range(FC):
            tk_z = pl.op("t", lambda E, fc=fc, tt=tt: E["t"].matmul(
                pA[0:36, 0:512], aaugs[:, fc * 36:(fc + 1) * 36],
                CT[:, fc * TOK + tt * TN: fc * TOK + tt * TN + TN],
                start=(fc == 0), stop=(fc == FC - 1)),
                deps=[t_aa, ln2fc[tt][fc], prev_v4, prev_ttp],
                tick=(fc == FC - 1))
        t_zf = pl.op("v", lambda E: E["v"].tensor_copy(zsb[:, :], pA[0:36, 0:512]),
                     deps=[tk_z, prev_zmk], tick=True)
        t_lg = pl.op("v", lambda E: E["v"].tensor_scalar(
            lgsb[:, :], pA[32:36, 0:512], biass[32:36, 0:1], 0.0, OP.add),
            deps=[tk_z, t_eb, t_zf, prev_ttp], tick=True)
        bk0, xp0, pp0 = emit_base(0, tt)
        t_tp = None
        for ch in range(4):
            t_tp = pl.op("t", lambda E, ch=ch: E["t"].transpose(
                pA[:, 512 + ch * 4: 512 + ch * 4 + 4],
                lgsb[:, ch * 128:(ch + 1) * 128], identf[0:4, 0:4]),
                deps=[t_lg], tick=(ch == 3))
        t_lgt = pl.op("v", lambda E: E["v"].tensor_copy(lgt[:, :], pA[:, 512:528]),
                      deps=[t_tp, prev_ttp2], tick=True)
        last = t_lgt
        for ch in range(4):
            last = pl.op("v", lambda E, ch=ch: E["v"].reduce_max(
                m4[:, ch:ch + 1], lgt[:, ch * 4:(ch + 1) * 4], axis=AX.X),
                deps=[last], tick=True)
            last = pl.op("v", lambda E, ch=ch: E["v"].tensor_tensor(
                ohtm[:, ch * 4: ch * 4 + 1], lgt[:, ch * 4: ch * 4 + 1],
                m4[:, ch:ch + 1], OP.is_ge), deps=[last], tick=True)
            last = pl.op("v", lambda E, ch=ch: E["v"].tensor_copy(
                tb1[:, 0:1], ohtm[:, ch * 4: ch * 4 + 1]), deps=[last], tick=True)
            for e_ in range(1, 4):
                last = pl.op("v", lambda E, ch=ch, e_=e_: E["v"].tensor_tensor(
                    tb2[:, 0:1], lgt[:, ch * 4 + e_: ch * 4 + e_ + 1],
                    m4[:, ch:ch + 1], OP.is_ge), deps=[last], tick=True)
                last = pl.op("v", lambda E: E["v"].tensor_scalar(
                    tb2[:, 1:2], tb1[:, 0:1], -1.0, 1.0, OP.mult, OP.add),
                    deps=[last], tick=True)
                last = pl.op("v", lambda E, ch=ch, e_=e_: E["v"].tensor_tensor(
                    ohtm[:, ch * 4 + e_: ch * 4 + e_ + 1], tb2[:, 0:1],
                    tb2[:, 1:2], OP.mult), deps=[last], tick=True)
                last = pl.op("v", lambda E, ch=ch, e_=e_: E["v"].tensor_tensor(
                    tb1[:, 0:1], tb1[:, 0:1],
                    ohtm[:, ch * 4 + e_: ch * 4 + e_ + 1], OP.add),
                    deps=[last], tick=True)
        t_tp2 = None
        for ch in range(4):
            t_tp2 = pl.op("t", lambda E, ch=ch: E["t"].transpose(
                pB[0:4, 512 + ch * 128: 512 + ch * 128 + 128],
                ohtm[:, ch * 4:(ch + 1) * 4], identf[:, :]),
                deps=[last], tick=(ch == 3))
        t_oh = pl.op("v", lambda E: E["v"].tensor_copy(
            ohsb[:, :], pB[0:4, 512:1024]), deps=[t_tp2], tick=True)
        t_ohx = pl.op("t", lambda E: E["t"].matmul(
            pB[0:32, 0:512], e32[:, :], ohsb[:, :], start=True, stop=True),
            deps=[t_oh], tick=True)
        zmk_t[tt] = pl.op("v", lambda E, tt=tt: E["v"].tensor_tensor(
            zmk[tt][:, :], zsb[0:32, :], pB[0:32, 0:512], OP.mult),
            deps=[t_ohx, t_zf], tick=True)
        prev_zmk = zmk_t[tt]
        prev_ttp = t_tp
        prev_ttp2 = t_tp2
        emit_delta_evac(0, tt, bk0, xp0, pp0)

    for nb in range(1, FC):
        emit_w4(nb)
        for tt in range(2):
            bk, xp_, pp = emit_base(nb, tt)
            emit_delta_evac(nb, tt, bk, xp_, pp)

    if "ph4" in dump:
        pl.barrier()
        for nm, tns, shp, dt in (("d_zmk0", zmk[0], [32, TN], BF16),
                                 ("d_zmk1", zmk[1], [32, TN], BF16),
                                 ("d_zsb", zsb, [36, TN], F32),
                                 ("d_ohsb", ohsb, [4, TN], BF16),
                                 ("d_lgt", lgt, [128, 16], F32)):
            dap = nc.dram_tensor(nm, shp, dt, kind="ExternalOutput").ap()
            pl.op("y", lambda E, dap=dap, tns=tns: E["y"].dma_start(
                out=dap, in_=tns[:, :]), dma=True, sem="dbg")

    pl.barrier()

    SEMNAMES = ["t", "v", "s", "g", "y", "ck", "cv", "gx", "gy", "gz", "gw",
                "w0", "w1", "kv0", "kv1", "kv2", "io0", "io1", "dbg", "rs"]
    with ExitStack() as st:
        SEMS = {n: st.enter_context(nc.semaphore(n)) for n in SEMNAMES}
        block = st.enter_context(nc.Block())
        E = {"t": nc.tensor, "v": nc.vector, "s": nc.scalar,
             "g": nc.gpsimd, "y": nc.sync}

        def replay_one(r):
            e = E[r["eng"]]
            for (sm, v) in r["waits"]:
                e.wait_ge(SEMS[sm], v)
            if r["fn"] is not None:
                ins = r["fn"](E)
                if r["inc"] is not None:
                    sm, n, raw = r["inc"]
                    if raw:
                        ins.then_inc(SEMS[sm])
                    else:
                        ins.then_inc(SEMS[sm], n)

        def replay(recs):
            i = 0
            while i < len(recs):
                r = recs[i]
                if r.get("ctl") == "sw2_begin":
                    slot = r["slot"]
                    j = i + 1
                    body = []
                    while recs[j].get("ctl") != "sw2_end":
                        body.append(recs[j])
                        j += 1
                    deltas = recs[j]["deltas"]
                    assert all(rr["eng"] in ("t", "s") for rr in body)
                    for ename, reg in (("t", regT), ("s", regS)):
                        e = E[ename]
                        e.reg_load(reg, ifl[0:1, slot:slot + 1])
                        with e.If_ne(reg, 0):
                            for rr in body:
                                if rr["eng"] == ename:
                                    replay_one(rr)
                        with e.Else():
                            if deltas.get(ename):
                                e.sem_inc(SEMS[ename], deltas[ename])
                        e.end_ifs()
                    i = j + 1
                    continue
                replay_one(r)
                i += 1

        replay(pl.segs["main0"])
        pid = nc.partition_id()
        for i in nc.Switch(engines=[nc.sync, nc.scalar], index=[pid, pid], n=NCORE):
            replay(pl.segs[f"sw{i}"])
        replay(pl.segs["main1"])

    return nc


def _dump(nc, pl, sbuf_map, dram_map):
    pl.barrier()
    for name, t in sbuf_map.items():
        d = nc.dram_tensor(name, [128, FC * TOK], BF16, kind="ExternalOutput").ap()
        pl.op("y", lambda E, d=d, t=t: E["y"].dma_start(out=d, in_=t[:, :]),
              dma=True, sem="dbg")
    for name, t in dram_map.items():
        h = t.ap()
        d = nc.dram_tensor(name, list(h.shape), BF16 if name != "d_xp" else F32,
                           kind="ExternalOutput").ap()
        pl.op("y", lambda E, d=d, h=h: E["y"].dma_start(out=d, in_=h),
              dma=True, sem="dbg")
    pl.barrier()


def _prep_inputs(L0_attn_x, L0_attn_wq, L0_attn_wk, L0_attn_wv, L0_attn_wo,
                 L0_ffn_W_up, L0_ffn_gate_w, L0_ffn_expert_bias,
                 L0_ffn_lora_A, L0_ffn_lora_B):
    f32 = lambda a: np.ascontiguousarray(np.asarray(a), dtype=np.float32)
    x = f32(L0_attn_x)[0]
    wq = f32(L0_attn_wq)
    wk = f32(L0_attn_wk)
    wv = f32(L0_attn_wv)
    wo = f32(L0_attn_wo)
    wup = np.ascontiguousarray(f32(L0_ffn_W_up).T)
    gate = f32(L0_ffn_gate_w)
    ebias_ = f32(L0_ffn_expert_bias)
    la = f32(L0_ffn_lora_A)
    lb = f32(L0_ffn_lora_B)

    aaug = np.concatenate([la.reshape(32, D), gate], axis=0)
    aaugT = np.ascontiguousarray(aaug.T)
    ballT = np.ascontiguousarray(lb.transpose(0, 2, 1).reshape(32, D))
    eb36 = np.zeros((36, 1), np.float32)
    eb36[32:36, 0] = ebias_

    identf = np.eye(128, dtype=np.float32)
    onesb = np.ones((128, 128), dtype=BF)
    psw = np.zeros((128, 128), np.float32)
    for k in range(128):
        psw[k, k ^ 1] = 1.0
    pswap = psw.astype(BF)
    e32 = np.zeros((4, 32), np.float32)
    for e_ in range(4):
        e32[e_, e_ * 8:(e_ + 1) * 8] = 1.0
    e32 = e32.astype(BF)
    kk = np.arange(128)[:, None]
    qq = np.arange(TN)[None, :]
    dm = np.zeros((128, 4 * TN), np.float32)
    for j in range(4):
        dm[:, j * TN:(j + 1) * TN] = (qq >= 128 * j + kk).astype(np.float32)
    dmask = dm.astype(BF)

    inv = 1.0 / (10000.0 ** (np.arange(0, HD, 2, dtype=np.float64) / HD))

    in_maps = []
    for c in range(NCORE):
        bA, bB = c, 15 - c
        toks = np.concatenate([np.arange(bA * TN, (bA + 1) * TN),
                               np.arange(bB * TN, (bB + 1) * TN)])
        xc = np.ascontiguousarray(
            np.concatenate([x[bA * TN:(bA + 1) * TN], x[bB * TN:(bB + 1) * TN]], 0).T)
        ang = inv[:, None] * toks[None, :].astype(np.float64)
        cosr = np.cos(ang)
        sinr = np.sin(ang)
        cosT = np.repeat(cosr, 2, axis=0).astype(np.float32)
        ssinT = np.repeat(sinr, 2, axis=0).astype(np.float32)
        ssinT[0::2, :] *= -1.0
        sf = np.zeros((128, NSLOT * 128), np.float32)
        ifl_np = np.zeros((1, 32), np.int32)
        for s_ in range(NSLOT):
            if slot_block(c, s_) is not None:
                sf[:, s_ * 128:(s_ + 1) * 128] = 1.0
                ifl_np[0, s_] = 1
        in_maps.append({
            "xT": xc, "wq": wq, "wk": wk, "wv": wv, "wo": wo, "wup": wup,
            "aaugT": aaugT, "ballT": ballT, "ebias": eb36,
            "cosT": cosT, "ssinT": ssinT, "identf": identf,
            "onesb": onesb, "pswap": pswap, "e32": e32,
            "dmask": dmask, "sflags": sf.astype(BF), "iflags": ifl_np,
        })
    return in_maps


def kernel(L0_attn_x, L0_attn_wq, L0_attn_wk, L0_attn_wv, L0_attn_wo,
           L0_ffn_W_up, L0_ffn_gate_w, L0_ffn_expert_bias,
           L0_ffn_lora_A, L0_ffn_lora_B):
    in_maps = _prep_inputs(L0_attn_x, L0_attn_wq, L0_attn_wk, L0_attn_wv,
                           L0_attn_wo, L0_ffn_W_up, L0_ffn_gate_w,
                           L0_ffn_expert_bias, L0_ffn_lora_A, L0_ffn_lora_B)
    nc = build()
    res = run_bass_kernel_spmd(nc, in_maps, core_ids=list(range(NCORE)))
    out = np.empty((1, S, D), np.float32)
    for c in range(NCORE):
        yc = np.asarray(res.results[c]["y"])
        bA, bB = c, 15 - c
        out[0, bA * TN:(bA + 1) * TN] = yc[:, 0:TN].T
        out[0, bB * TN:(bB + 1) * TN] = yc[:, TN:2 * TN].T
    return out


# version 20
# speedup vs baseline: 1.0695x; 1.0094x over previous
import numpy as np
import ml_dtypes
from contextlib import ExitStack

import concourse.bass as bass
import concourse.mybir as mybir
from concourse.bass_utils import run_bass_kernel_spmd

F32 = mybir.dt.float32
BF16 = mybir.dt.bfloat16
AF = mybir.ActivationFunctionType
OP = mybir.AluOpType
AX = mybir.AxisListType
BF = ml_dtypes.bfloat16

D = 4096
S = 8192
NH = 32
NKV = 8
HD = 128
FC = 32
TOK = 1024
TN = 512
NB = 16
NCORE = 8
EPS = 1e-5
CSH = 12.0
QKS = 1.0 / np.sqrt(128.0)
NSLOT = 24

ENGS = ("t", "v", "s", "g", "y")


def owner(b):
    return b if b < 8 else 15 - b


def owner_tt(b):
    return 0 if b < 8 else 1


def slot_block(c, s):
    if s < 7:
        return s if s < c else None
    if s == 7:
        return c
    if s < 23:
        return (s - 8) if (s - 8) < 15 - c else None
    return 15 - c


class Plan:
    def __init__(self):
        self.segs = {}
        self.cur = None
        self.cnt = {}
        self.lastw = {e: {} for e in ENGS}
        self.last_op = {}

    def seg(self, name):
        self.cur = self.segs.setdefault(name, [])

    def snap(self):
        import copy
        return (dict(self.cnt), copy.deepcopy(self.lastw), dict(self.last_op))

    def restore(self, st):
        import copy
        self.cnt = dict(st[0])
        self.lastw = copy.deepcopy(st[1])
        self.last_op = dict(st[2])

    def op(self, eng, fn, deps=(), tick=False, dma=False, sem=None, rawinc=False):
        waits = []
        for d in deps:
            if d is None:
                continue
            s, v = d
            if v <= 0:
                continue
            if self.lastw[eng].get(s, -1) >= v:
                continue
            self.lastw[eng][s] = v
            waits.append((s, v))
        inc = None
        if dma:
            tick = True
        if tick:
            sm = sem or eng
            n = 16 if dma else 1
            self.cnt[sm] = self.cnt.get(sm, 0) + n
            inc = (sm, n, rawinc)
        rec = dict(eng=eng, fn=fn, waits=waits, inc=inc)
        self.cur.append(rec)
        if fn is not None:
            self.last_op[eng] = rec
        if inc:
            return (inc[0], self.cnt[inc[0]])
        return None

    def tkt(self, sm):
        return (sm, self.cnt.get(sm, 0))

    def ensure_tick(self, eng):
        rec = self.last_op.get(eng)
        if rec is not None and rec["inc"] is None:
            self.cnt[eng] = self.cnt.get(eng, 0) + 1
            rec["inc"] = (eng, 1, False)
        return (eng, self.cnt.get(eng, 0))

    DMASEMS = ("y", "gx", "gy", "gz", "gw", "w0", "w1", "kv0", "kv1", "kv2",
               "io0", "io1", "dbg", "rs")

    def sw2_begin(self, slot):
        self.cur.append(dict(eng=None, ctl="sw2_begin", slot=slot, fn=None))
        import copy
        self._sw2_lastw = copy.deepcopy(self.lastw)
        self._sw2_cnt = dict(self.cnt)

    def sw2_end(self):
        import copy
        deltas = {}
        for sm in ("t", "s"):
            d = self.cnt.get(sm, 0) - self._sw2_cnt.get(sm, 0)
            if d:
                deltas[sm] = d
        self.lastw = copy.deepcopy(self._sw2_lastw)
        self.cur.append(dict(eng=None, ctl="sw2_end", deltas=deltas, fn=None))

    def barrier(self):
        tks = {e: self.ensure_tick(e) for e in ENGS}
        dts = [self.tkt(s) for s in self.DMASEMS]
        for e in ENGS:
            self.op(e, None, deps=[tks[o] for o in ENGS if o != e] + dts)


def build(dump=()):
    nc = bass.Bass("TRN2", target_bir_lowering=False, debug=False)

    xT = nc.dram_tensor("xT", [D, TOK], F32, kind="ExternalInput").ap()
    wq = nc.dram_tensor("wq", [D, D], F32, kind="ExternalInput").ap()
    wk = nc.dram_tensor("wk", [D, 1024], F32, kind="ExternalInput").ap()
    wv = nc.dram_tensor("wv", [D, 1024], F32, kind="ExternalInput").ap()
    wo = nc.dram_tensor("wo", [D, D], F32, kind="ExternalInput").ap()
    wup = nc.dram_tensor("wup", [D, D], F32, kind="ExternalInput").ap()
    aaugT = nc.dram_tensor("aaugT", [D, 36], F32, kind="ExternalInput").ap()
    ballT = nc.dram_tensor("ballT", [32, D], F32, kind="ExternalInput").ap()
    ebias = nc.dram_tensor("ebias", [36, 1], F32, kind="ExternalInput").ap()
    cosT_d = nc.dram_tensor("cosT", [128, TOK], F32, kind="ExternalInput").ap()
    ssinT_d = nc.dram_tensor("ssinT", [128, TOK], F32, kind="ExternalInput").ap()
    identf_d = nc.dram_tensor("identf", [128, 128], F32, kind="ExternalInput").ap()
    onesb_d = nc.dram_tensor("onesb", [128, 128], BF16, kind="ExternalInput").ap()
    pswap_d = nc.dram_tensor("pswap", [128, 128], BF16, kind="ExternalInput").ap()
    e32_d = nc.dram_tensor("e32", [4, 32], BF16, kind="ExternalInput").ap()
    dmask_d = nc.dram_tensor("dmask", [128, 4 * TN], BF16, kind="ExternalInput").ap()
    sflags_d = nc.dram_tensor("sflags", [128, NSLOT * 128], BF16, kind="ExternalInput").ap()
    iflags_d = nc.dram_tensor("iflags", [1, 32], mybir.dt.int32, kind="ExternalInput").ap()
    y = nc.dram_tensor("y", [D, TOK], F32, kind="ExternalOutput").ap()

    ksrc = nc.dram_tensor("ksrc", [2 * 1024, 512], BF16, kind="Internal")
    vsrc = nc.dram_tensor("vsrc", [1024, 1024], BF16, kind="Internal")
    kdst = nc.dram_tensor("kdst", [NCORE * 2 * 1024, 512], BF16, kind="Internal",
                          addr_space="Shared")
    vdst = nc.dram_tensor("vdst", [NCORE * 1024, 1024], BF16, kind="Internal",
                          addr_space="Shared")
    keff = nc.dram_tensor("keff", [NSLOT * 1024, 512], BF16, kind="Internal")
    veff = nc.dram_tensor("veff", [NSLOT * 8 * 4 * 128, 128], BF16, kind="Internal")
    xpscr = nc.dram_tensor("xpscr", [D, TOK], F32, kind="Internal")

    kdst_f = kdst.ap().rearrange("a b -> (a b)")
    vdst_f = vdst.ap().rearrange("a b -> (a b)")
    keff_f = keff.ap().rearrange("a b -> (a b)")
    veff_f = veff.ap().rearrange("a b -> (a b)")
    vsrc_f = vsrc.ap().rearrange("a b -> (a b)")

    SB0 = 20480

    def at(name, shape, dtype, off):
        return nc.alloc_sbuf_tensor_at(name, shape, dtype, offset=SB0 + off)

    AT = at("A_xn", [128, FC * TOK], BF16, 0)
    BT = at("B_av", [128, FC * TOK], BF16, 65536)
    CT = at("C_q", [128, FC * TOK], BF16, 131072)

    identf = at("identf", [128, 128], F32, 196608)
    onesb = at("onesb", [128, 128], BF16, 197120)
    pswap = at("pswap", [128, 128], BF16, 197376)
    e32 = at("e32", [4, 32], BF16, 197632)
    dmask = at("dmask", [128, 4 * TN], BF16, 197696)
    sflags = at("sflags", [128, NSLOT * 128], BF16, 201792)
    cm12 = at("cm12", [128, 1], F32, 207936)
    ifl = at("ifl", [1, 32], mybir.dt.int32, 207968)

    W1 = [at(f"W1_{i}", [128, FC * 128], BF16, 65536 + i * 8192) for i in range(2)]
    cosT = at("cosT", [128, TOK], F32, 81920)
    ssinT = at("ssinT", [128, TOK], F32, 86016)
    m1 = at("m1", [128, TN], F32, 90112)
    r1 = at("r1", [128, TN], F32, 92160)
    var1 = at("var1", [128, TN], F32, 94208)
    sd1 = at("sd1", [128, TN], F32, 96256)
    x2b = [at(f"x2b_{i}", [128, TN], BF16, 98304 + i * 1024) for i in range(2)]
    lntmp = [at(f"lntmp_{i}", [128, TN], F32, 100352 + i * 2048) for i in range(2)]
    ktmp = [at(f"ktmp_{i}", [128, TN], BF16, 104448 + i * 1024) for i in range(2)]
    tR1 = [at(f"tR1_{i}", [128, TN], F32, 106496 + i * 2048) for i in range(2)]
    tR2 = [at(f"tR2_{i}", [128, TN], F32, 110592 + i * 2048) for i in range(2)]
    vtmp = [at(f"vtmp_{i}", [128, TN], F32, 114688 + i * 2048) for i in range(2)]
    vtt = [at(f"vtt_{i}", [128, 128], BF16, 118784 + i * 256) for i in range(2)]
    kout = [at(f"kout_{i}", [128, TN], BF16, 119296 + i * 1024) for i in range(2)]
    zsb0 = at("zsb0", [128, 4096], BF16, 121344)

    kb = [at(f"kb_{i}", [128, 512], BF16, 32768 + i * 1024) for i in range(3)]
    vb = [at(f"vb_{i}", [128, 512], BF16, 35840 + i * 1024) for i in range(3)]
    attb = [at(f"attb_{i}", [128, 1024], BF16, 38912 + i * 2048) for i in range(2)]
    rinv = [at(f"rinv_{i}", [128, 512], F32, 43008 + i * 2048) for i in range(2)]
    avraw = [at(f"avraw_{i}", [128, 512], F32, 47104 + i * 2048) for i in range(2)]
    smraw = [at(f"smraw_{i}", [128, 512], F32, 51200 + i * 2048) for i in range(2)]

    W3 = [at(f"W3_{i}", [128, FC * 128], BF16, 0 + i * 8192) for i in range(2)]
    xtile = [at(f"xtile_{i}", [128, TN], F32, 16384 + i * 2048) for i in range(2)]
    xpf = [at(f"xpf_{i}", [128, TN], F32, 20480 + i * 2048) for i in range(2)]
    m3 = at("m3", [128, TN], F32, 24576)
    r3 = at("r3", [128, TN], F32, 26624)
    var3 = at("var3", [128, TN], F32, 28672)
    sd3 = at("sd3", [128, TN], F32, 30720)
    x2b3 = [at(f"x2b3_{i}", [128, TN], BF16, 32768 + i * 1024) for i in range(2)]
    lntmp3 = [at(f"lntmp3_{i}", [128, TN], F32, 34816 + i * 2048) for i in range(2)]

    zsb = at("zsb", [36, TN], F32, 40960)
    zmk = [at(f"zmk_{i}", [32, TN], BF16, 43008 + i * 1024) for i in range(2)]
    lgsb = at("lgsb", [4, TN], F32, 45056)
    lgt = at("lgt", [128, 16], F32, 47104)
    m4 = at("m4", [128, 4], F32, 47232)
    ohtm = at("ohtm", [128, 16], F32, 47360)
    tb1 = at("tb1", [128, 4], F32, 47488)
    tb2 = at("tb2", [128, 4], F32, 47552)
    ohsb = at("ohsb", [4, TN], BF16, 47616)
    aaugs = at("aaugs", [128, FC * 36], BF16, 65536)
    balls = at("balls", [32, D], BF16, 68096)
    biass = at("biass", [36, 1], F32, 76288)

    pA = nc.alloc_psum_tensor("pA", [128, 1024], F32)
    pB = nc.alloc_psum_tensor("pB", [128, 1024], F32)
    pC = nc.alloc_psum_tensor("pC", [128, 512], F32)
    pD = nc.alloc_psum_tensor("pD", [128, 512], F32)
    pE = nc.alloc_psum_tensor("pE", [128, 512], F32)
    pF = nc.alloc_psum_tensor("pF", [128, 512], F32)

    pl = Plan()
    pl.seg("main0")

    tload = []
    for dst, src in ((identf, identf_d), (onesb, onesb_d), (pswap, pswap_d),
                     (e32, e32_d), (dmask, dmask_d), (sflags, sflags_d),
                     (cosT, cosT_d), (ssinT, ssinT_d), (ifl, iflags_d)):
        tload.append(pl.op("y", lambda E, dst=dst, src=src:
                           E["y"].dma_start(out=dst[:, :], in_=src), dma=True))
    t_tables = pl.tkt("y")
    for wu in range(200):
        pl.op("t", lambda E: E["t"].matmul(
            pC[:, :], onesb[:, :], AT[:, 0:512], start=True, stop=True,
            skip_group_check=True), deps=[])
    t_zsb0 = pl.op("v", lambda E: E["v"].memset(zsb0[:, :], 0.0), tick=True)
    t_cm12 = pl.op("v", lambda E: E["v"].memset(cm12[:, :], -CSH), tick=True)

    x_src = xT.rearrange("(c p) t -> p c t", p=128)
    XSEM = ("gx", "gy", "gz", "gw")
    t_xq = []
    for q4 in range(4):
        t_xq.append(pl.op("g", lambda E, q4=q4: E["g"].dma_start(
            out=AT.ap().rearrange("p (c t) -> p c t", c=FC)[:, q4 * 8:(q4 + 1) * 8, :],
            in_=x_src[:, q4 * 8:(q4 + 1) * 8, :]),
            dma=True, sem=XSEM[q4]))
    wk_pre = [None, None]
    for nb in range(2):
        wsrc = wk[:, nb * 128:(nb + 1) * 128].rearrange("(c p) o -> p c o", p=128)
        wk_pre[nb] = pl.op("g", lambda E, nb=nb, wsrc=wsrc: E["g"].dma_start(
            out=W1[nb].ap().rearrange("p (c o) -> p c o", c=FC), in_=wsrc),
            deps=[], dma=True, sem=f"w{nb}")
    t_xl = [t_xq[3], t_xq[3]]

    def emit_ln(arena, tt, dep_in, mT, rT, varT, sdT, x2T, lnT, dep_extra=(),
                dep_fn=None):
        def sl(fc):
            return arena[:, fc * TOK + tt * TN: fc * TOK + tt * TN + TN]
        if dep_fn is None:
            dep_fn = lambda fc: dep_in
        prev_v = pl.tkt("v")
        prev_t = pl.tkt("t")
        tk_s1 = None
        for fc in range(FC):
            tk_s1 = pl.op("t", lambda E, fc=fc: E["t"].matmul(
                pE[:, :], onesb[:, :], sl(fc), start=(fc == 0), stop=(fc == FC - 1)),
                deps=[dep_fn(fc), t_tables, prev_v] + list(dep_extra),
                tick=(fc == FC - 1))
        sq_pe = [None, None]
        for fc in range(FC):
            par = fc % 2
            d = [dep_fn(fc), prev_t]
            if sq_pe[par] is not None:
                d.append(sq_pe[par])
            tk_sq = pl.op("s", lambda E, fc=fc, par=par: E["s"].activation(
                x2T[par][:, :], sl(fc), AF.Square), deps=d, tick=True)
            sq_pe[par] = pl.op("t", lambda E, par=par, fc=fc: E["t"].matmul(
                pF[:, :], onesb[:, :], x2T[par][:, :],
                start=(fc == 0), stop=(fc == FC - 1)),
                deps=[tk_sq, prev_v], tick=True)
        tk_s2 = sq_pe[(FC - 1) % 2]
        a = pl.op("v", lambda E: E["v"].tensor_scalar(
            mT[:, :], pE[:, :], 1.0 / D, 0.0, OP.mult), deps=[tk_s1], tick=True)
        b = pl.op("v", lambda E: E["v"].tensor_scalar(
            varT[:, :], pF[:, :], 1.0 / D, EPS, OP.mult, OP.add),
            deps=[tk_s2, a], tick=True)
        cgate = pl.op("v", lambda E: E["v"].tensor_tensor(
            sdT[:, :], mT[:, :], mT[:, :], OP.mult), deps=[b], tick=True)
        dd = pl.op("v", lambda E: E["v"].tensor_tensor(
            varT[:, :], varT[:, :], sdT[:, :], OP.subtract), deps=[cgate], tick=True)
        ee = pl.op("s", lambda E: E["s"].activation(
            sdT[:, :], varT[:, :], AF.Sqrt), deps=[dd], tick=True)
        ff = pl.op("v", lambda E: E["v"].reciprocal(rT[:, :], sdT[:, :]),
                   deps=[ee], tick=True)
        lastv = ff
        lastg = None
        fcticks = [None] * FC
        for fc in range(FC):
            par = fc % 2
            eng = "v" if par == 0 else "g"
            prev = lastv if par == 0 else lastg
            g1 = pl.op(eng, lambda E, fc=fc, par=par, eng=eng: E[eng].tensor_tensor(
                lnT[par][:, :], sl(fc), mT[:, :], OP.subtract),
                deps=[prev, ff, tk_s2], tick=True)
            tkn = pl.op(eng, lambda E, fc=fc, par=par, eng=eng: E[eng].tensor_tensor(
                sl(fc), lnT[par][:, :], rT[:, :], OP.mult), deps=[g1], tick=True)
            fcticks[fc] = tkn
            if par == 0:
                lastv = tkn
            else:
                lastg = tkn
        last = pl.op("v", lambda E: E["v"].nop(), deps=[lastv, lastg], tick=True)
        return last, fcticks

    ln1 = []
    ln1fc = []
    for tt in range(2):
        lt, lfc = emit_ln(AT[:, :], tt, t_xq[3], m1, r1, var1, sd1, x2b, lntmp,
                          dep_fn=lambda fc: t_xq[fc // 8])
        ln1.append(lt)
        ln1fc.append(lfc)

    mats = [("k", wk, 8), ("v", wv, 8), ("q", wq, 32)]
    pc = 0
    prc = 0
    wld = [None, None]
    wuse = [None, None]
    projev = {}
    pA_use = [None, None]
    ktmp_use = [None, None]
    vtmp_use = [None, None]
    vtt_use = [None, None]
    kout_use = [None, None]
    tR1_use = [None, None]
    tR2_use = [None, None]
    nbg = 0
    kv_store = []
    for kind, wmat, nnb in mats:
        for nb in range(nnb):
            par = nbg % 2
            wsrc = wmat[:, nb * 128:(nb + 1) * 128].rearrange(
                "(c p) o -> p c o", p=128)
            if kind == "k" and nb < 2:
                wld[par] = wk_pre[nb]
            else:
                wld[par] = pl.op("g", lambda E, par=par, wsrc=wsrc: E["g"].dma_start(
                    out=W1[par].ap().rearrange("p (c o) -> p c o", c=FC), in_=wsrc),
                    deps=[wuse[par]], dma=True, sem=f"w{par}")
            for tt in range(2):
                pcp = pc % 2
                bank = pC if pcp == 0 else pD
                tk_mm = None
                for fc in range(FC):
                    tk_mm = pl.op("t", lambda E, fc=fc, par=par, tt=tt, bank=bank:
                                  E["t"].matmul(
                                      bank[:, :],
                                      W1[par][:, fc * 128:(fc + 1) * 128],
                                      AT[:, fc * TOK + tt * TN: fc * TOK + tt * TN + TN],
                                      start=(fc == 0), stop=(fc == FC - 1)),
                                  deps=[wld[par], ln1fc[tt][fc], projev.get(pcp)],
                                  tick=(fc == FC - 1))
                wuse[par] = tk_mm
                if kind in ("k", "q"):
                    kp = pc % 2
                    tcp = pl.op("s", lambda E, bank=bank, kp=kp: E["s"].copy(
                        ktmp[kp][:, :], bank[:, :]),
                        deps=[tk_mm, ktmp_use[kp]], tick=True)
                    projev[pcp] = tcp
                    sl_ = prc % 2
                    tpm = pl.op("t", lambda E, kp=kp, sl_=sl_: E["t"].matmul(
                        pA[:, sl_ * 512:sl_ * 512 + 512], pswap[:, :],
                        ktmp[kp][:, :], start=True, stop=True),
                        deps=[tcp, pA_use[sl_]], tick=True)
                    t1 = pl.op("v", lambda E, kp=kp, tt=tt: E["v"].tensor_tensor(
                        tR1[kp][:, :], ktmp[kp][:, :],
                        cosT[:, tt * TN: tt * TN + TN], OP.mult),
                        deps=[tcp, tR1_use[kp], t_tables], tick=True)
                    t2 = pl.op("v", lambda E, kp=kp, sl_=sl_, tt=tt: E["v"].tensor_tensor(
                        tR2[kp][:, :], pA[:, sl_ * 512:sl_ * 512 + 512],
                        ssinT[:, tt * TN: tt * TN + TN], OP.mult),
                        deps=[tpm, t1, tR2_use[kp]], tick=True)
                    pA_use[sl_] = t2
                    prc += 1
                    ktmp_use[kp] = t2
                    if kind == "q":
                        h = nb
                        tq = pl.op("v", lambda E, kp=kp, h=h, tt=tt: E["v"].tensor_tensor(
                            CT[:, h * TOK + tt * TN: h * TOK + tt * TN + TN],
                            tR1[kp][:, :], tR2[kp][:, :], OP.add),
                            deps=[t2], tick=True)
                        tR1_use[kp] = tq
                        tR2_use[kp] = tq
                    else:
                        ko = pl.op("v", lambda E, kp=kp: E["v"].tensor_tensor(
                            kout[kp][:, :], tR1[kp][:, :], tR2[kp][:, :], OP.add),
                            deps=[t2, kout_use[kp]], tick=True)
                        tR1_use[kp] = ko
                        tR2_use[kp] = ko
                        g_ = nb
                        td = pl.op("y", lambda E, kp=kp, g_=g_, tt=tt: E["y"].dma_start(
                            out=ksrc.ap()[tt * 1024 + g_ * 128: tt * 1024 + g_ * 128 + 128, :],
                            in_=kout[kp][:, :]), deps=[ko], dma=True)
                        kout_use[kp] = td
                        kv_store.append(td)
                else:
                    vp = pc % 2
                    tcp = pl.op("s", lambda E, bank=bank, vp=vp: E["s"].copy(
                        vtmp[vp][:, :], bank[:, :]),
                        deps=[tk_mm, vtmp_use[vp]], tick=True)
                    projev[pcp] = tcp
                    g_ = nb
                    last_v = None
                    for j in range(4):
                        sl_ = prc % 2
                        ttr = pl.op("t", lambda E, vp=vp, j=j, sl_=sl_: E["t"].transpose(
                            pA[:, sl_ * 512:sl_ * 512 + 128],
                            vtmp[vp][:, j * 128:(j + 1) * 128], identf[:, :]),
                            deps=[tcp, pA_use[sl_], t_tables], tick=True)
                        w2 = j % 2
                        tvt = pl.op("s", lambda E, sl_=sl_, w2=w2: E["s"].copy(
                            vtt[w2][:, :], pA[:, sl_ * 512:sl_ * 512 + 128]),
                            deps=[ttr, vtt_use[w2]], tick=True)
                        pA_use[sl_] = tvt
                        prc += 1
                        fo = (g_ * 8 + tt * 4 + j) * 16384
                        tst = pl.op("y", lambda E, w2=w2, fo=fo: E["y"].dma_start(
                            out=vsrc_f[fo:fo + 16384].rearrange("(p d) -> p d", p=128),
                            in_=vtt[w2][:, :]), deps=[tvt], dma=True)
                        vtt_use[w2] = tst
                        kv_store.append(tst)
                        last_v = tst
                    vtmp_use[vp] = last_v
                pc += 1
            nbg += 1
            if kind == "v" and nb == nnb - 1:
                tk_cc_k = pl.op("g", lambda E: E["g"].collective_compute(
                    "AllGather", OP.bypass,
                    replica_groups=[list(range(NCORE))],
                    ins=[ksrc.ap().opt()], outs=[kdst.ap().opt()]),
                    deps=kv_store, tick=True, sem="ck", rawinc=True)
                tk_cc_v = pl.op("g", lambda E: E["g"].collective_compute(
                    "AllGather", OP.bypass,
                    replica_groups=[list(range(NCORE))],
                    ins=[vsrc.ap().opt()], outs=[vdst.ap().opt()]),
                    deps=[], tick=True, sem="cv", rawinc=True)

    base = pl.snap()
    finals = None
    for c in range(NCORE):
        pl.restore(base)
        pl.seg(f"sw{c}")
        first = [("ck", 1), ("cv", 1), t_zsb0]
        for s_ in range(NSLOT):
            b = slot_block(c, s_)
            if b is not None:
                r = owner(b)
                tw = owner_tt(b)
                pl.op("y", lambda E, s_=s_, r=r, tw=tw: E["y"].dma_start(
                    out=keff.ap()[s_ * 1024:(s_ + 1) * 1024, :],
                    in_=kdst.ap()[r * 2048 + tw * 1024: r * 2048 + tw * 1024 + 1024, :]),
                    deps=first, dma=True)
                for g_ in range(8):
                    so = r * (1024 * 1024) + (g_ * 8 + tw * 4) * 16384
                    do = (s_ * 8 + g_) * 65536
                    pl.op("y", lambda E, so=so, do=do: E["y"].dma_start(
                        out=veff_f[do:do + 65536],
                        in_=vdst_f[so:so + 65536]), deps=first, dma=True)
            else:
                pl.op("y", lambda E, s_=s_: E["y"].dma_start(
                    out=keff_f[s_ * 524288: (s_ + 1) * 524288].rearrange(
                        "(p f) -> p f", p=128),
                    in_=zsb0[:, :]), deps=first, dma=True)
                pl.op("y", lambda E, s_=s_: E["y"].dma_start(
                    out=veff_f[s_ * 524288: (s_ + 1) * 524288].rearrange(
                        "(p f) -> p f", p=128),
                    in_=zsb0[:, :]), deps=first, dma=True)
            first = [t_zsb0]
        f = pl.snap()
        if finals is None:
            finals = f[0]
        else:
            assert finals == f[0], f"switch body sem counts differ: {finals} vs {f[0]}"
    pl.restore(base)
    pl.cnt = dict(finals)
    t_relay = pl.tkt("y")
    t_relay2 = pl.tkt("rs")

    pl.seg("main1")
    pl.barrier()

    if "ph1" in dump:
        _dump(nc, pl, {"d_xn": AT, "d_q": CT}, {"d_ksrc": ksrc, "d_vsrc": vsrc})

    regT = nc.alloc_register(mybir.EngineType.PE, "fT")
    regS = nc.alloc_register(mybir.EngineType.Activation, "fS")
    cc = 0
    exp_t = [None, None]
    att_use = [None, None]
    kb_t = [None, None, None]
    vb_t = [None, None, None]
    kb_use = [None, None, None]
    vb_use = [None, None, None]
    avev = {0: None, 1: None}
    rn_use = {0: None, 1: None}
    for g_ in range(8):
        for pr in range(2):
            for sub, srange, qoff in ((0, [7] + list(range(0, 7)), 0),
                                      (1, [23] + list(range(8, 23)), TN)):
                s0 = srange[0]
                s1 = srange[-1]
                def emit_avsums(p):
                    tk_av = None
                    for h2 in range(2):
                        avb = pC if h2 == 0 else pD
                        smb = pE if h2 == 0 else pF
                        pl.op("t", lambda E, lp=p["lp"], j=p["j"], ab=p["ab"], h2=h2,
                              avb=avb, fm=p["fm"]: E["t"].matmul(
                                  avb[:, :], vb[lp][:, j * 128:(j + 1) * 128],
                                  attb[ab][:, h2 * 512:h2 * 512 + 512],
                                  start=fm, stop=True, skip_group_check=True),
                              deps=[p["msk"], p["vbt"], avev[h2]], tick=False)
                        tk_av = pl.op("t", lambda E, s_=p["s"], ab=p["ab"], h2=h2,
                                      smb=smb, fm=p["fm"]: E["t"].matmul(
                                          smb[:, :],
                                          sflags[:, s_ * 128:(s_ + 1) * 128],
                                          attb[ab][:, h2 * 512:h2 * 512 + 512],
                                          start=fm, stop=True,
                                          skip_group_check=True),
                                      deps=[], tick=(h2 == 1))
                    att_use[p["ab"]] = tk_av
                    kb_use[p["lp"]] = tk_av
                    vb_use[p["lp"]] = tk_av

                pending = []
                for si_, s_ in enumerate(srange):
                    lp = (sub * 8 + si_) % 3
                    kb_t[lp] = pl.op("y", lambda E, s_=s_, g_=g_, lp=lp: E["y"].dma_start(
                        out=kb[lp][:, :],
                        in_=keff.ap()[s_ * 1024 + g_ * 128: s_ * 1024 + g_ * 128 + 128, :]),
                        deps=[t_relay, t_relay2, kb_use[lp]], dma=True, sem=f"kv{lp}")
                    vb_t[lp] = pl.op("y", lambda E, s_=s_, g_=g_, lp=lp: E["y"].dma_start(
                        out=vb[lp].ap().rearrange("p (j d) -> p j d", j=4),
                        in_=veff.ap()[(s_ * 8 + g_) * 512: (s_ * 8 + g_) * 512 + 512, :]
                        .rearrange("(j p) d -> p j d", p=128)),
                        deps=[t_relay, t_relay2, vb_use[lp]], dma=True, sem=f"kv{lp}")
                    diag = s_ in (7, 23)
                    for j in range(4):
                        sc = cc % 2
                        scb = pA if sc == 0 else pB
                        tk_sc = None
                        for h2 in range(2):
                            h = g_ * 4 + pr * 2 + h2
                            tk_sc = pl.op("t", lambda E, lp=lp, j=j, h=h, h2=h2, scb=scb, qoff=qoff:
                                          E["t"].matmul(
                                              scb[:, h2 * 512:h2 * 512 + 512],
                                              kb[lp][:, j * 128:(j + 1) * 128],
                                              CT[:, h * TOK + qoff: h * TOK + qoff + TN],
                                              start=True, stop=True),
                                          deps=[kb_t[lp], exp_t[sc]],
                                          tick=(h2 == 1))
                        ab = cc % 2
                        tk_ex = pl.op("s", lambda E, ab=ab, scb=scb: E["s"].activation(
                            attb[ab][:, :], scb[:, :], AF.Exp,
                            bias=cm12[:, 0:1], scale=QKS),
                            deps=[tk_sc, att_use[ab], t_cm12], tick=True)
                        exp_t[sc] = tk_ex
                        tk_msk = tk_ex
                        if diag:
                            tm1 = pl.op("g", lambda E, ab=ab, j=j: E["g"].tensor_tensor(
                                attb[ab][:, 0:512], attb[ab][:, 0:512],
                                dmask[:, j * TN:(j + 1) * TN], OP.mult),
                                deps=[tk_ex, t_tables], tick=True)
                            tk_msk = pl.op("g", lambda E, ab=ab, j=j: E["g"].tensor_tensor(
                                attb[ab][:, 512:1024], attb[ab][:, 512:1024],
                                dmask[:, j * TN:(j + 1) * TN], OP.mult),
                                deps=[tm1], tick=True)
                        pending.append(dict(lp=lp, j=j, ab=ab, msk=tk_msk,
                                            vbt=vb_t[lp], s=s_,
                                            fm=(s_ == s0 and j == 0)))
                        if len(pending) > 1:
                            emit_avsums(pending.pop(0))
                        cc += 1
                while pending:
                    emit_avsums(pending.pop(0))
                tk_raw = []
                for h2 in range(2):
                    avb = pC if h2 == 0 else pD
                    smb = pE if h2 == 0 else pF
                    c1 = pl.op("v", lambda E, h2=h2, avb=avb: E["v"].tensor_copy(
                        avraw[h2][:, :], avb[:, :]),
                        deps=[pl.tkt("t"), avev[h2], rn_use[h2]], tick=True)
                    c2 = pl.op("v", lambda E, h2=h2, smb=smb: E["v"].tensor_copy(
                        smraw[h2][:, :], smb[:, :]), deps=[c1], tick=True)
                    avev[h2] = c2
                    tk_raw.append(c2)
                for h2 in range(2):
                    h = g_ * 4 + pr * 2 + h2
                    tr = pl.op("v", lambda E, h2=h2: E["v"].reciprocal(
                        rinv[h2][:, :], smraw[h2][:, :]),
                        deps=[tk_raw[h2], rn_use[h2]], tick=True)
                    rn_use[h2] = pl.op("v", lambda E, h2=h2, h=h, qoff=qoff:
                                       E["v"].tensor_tensor(
                                           BT[:, h * TOK + qoff: h * TOK + qoff + TN],
                                           avraw[h2][:, :], rinv[h2][:, :], OP.mult),
                                       deps=[tr], tick=True)

    w3ld = [None, None]
    w3use = [None, None]
    for nb in range(2):
        wsrc = wo[:, nb * 128:(nb + 1) * 128].rearrange("(c p) o -> p c o", p=128)
        w3ld[nb] = pl.op("g", lambda E, nb=nb, wsrc=wsrc: E["g"].dma_start(
            out=W3[nb].ap().rearrange("p (c o) -> p c o", c=FC), in_=wsrc),
            deps=[], dma=True, sem=f"w{nb}")
    pl.barrier()
    if "ph2" in dump:
        _dump(nc, pl, {"d_av": BT}, {})

    pc3 = 0
    bank_ev3 = {}
    xt_t = [None, None]
    xt_use = [None, None]
    xp_use = [None, None]
    xpb_use = [None, None]
    last_xpb = None
    for nb in range(FC):
        par = nb % 2
        if nb >= 2:
            wsrc = wo[:, nb * 128:(nb + 1) * 128].rearrange("(c p) o -> p c o", p=128)
            w3ld[par] = pl.op("g", lambda E, par=par, wsrc=wsrc: E["g"].dma_start(
                out=W3[par].ap().rearrange("p (c o) -> p c o", c=FC), in_=wsrc),
                deps=[w3use[par]], dma=True, sem=f"w{par}")
        for tt in range(2):
            pcp = pc3 % 2
            bank = pC if pcp == 0 else pD
            xpar = pc3 % 2
            xt_t[xpar] = pl.op("y", lambda E, nb=nb, tt=tt, xpar=xpar: E["y"].dma_start(
                out=xtile[xpar][:, :],
                in_=xT[nb * 128:(nb + 1) * 128, tt * TN: tt * TN + TN]),
                deps=[xt_use[xpar]], dma=True, sem=f"io{xpar}")
            tk_mm = None
            for dc in range(FC):
                tk_mm = pl.op("t", lambda E, dc=dc, par=par, tt=tt, bank=bank:
                              E["t"].matmul(
                                  bank[:, :], W3[par][:, dc * 128:(dc + 1) * 128],
                                  BT[:, dc * TOK + tt * TN: dc * TOK + tt * TN + TN],
                                  start=(dc == 0), stop=(dc == FC - 1)),
                              deps=[w3ld[par], bank_ev3.get(pcp)],
                              tick=(dc == FC - 1))
            w3use[par] = tk_mm
            ta = pl.op("v", lambda E, xpar=xpar, bank=bank: E["v"].tensor_tensor(
                xpf[xpar][:, :], bank[:, :], xtile[xpar][:, :], OP.add),
                deps=[tk_mm, xt_t[xpar], xp_use[xpar], xpb_use[xpar]], tick=True)
            bank_ev3[pcp] = ta
            xt_use[xpar] = ta
            ts_ = pl.op("y", lambda E, xpar=xpar, nb=nb, tt=tt: E["y"].dma_start(
                out=xpscr.ap()[nb * 128:(nb + 1) * 128, tt * TN: tt * TN + TN],
                in_=xpf[xpar][:, :]), deps=[ta], dma=True, sem=f"io{xpar}")
            last_xpb = pl.op("v", lambda E, xpar=xpar, nb=nb, tt=tt: E["v"].tensor_copy(
                CT[:, nb * TOK + tt * TN: nb * TOK + tt * TN + TN],
                xpf[xpar][:, :]), deps=[ta], tick=True)
            xp_use[xpar] = ts_
            xpb_use[xpar] = last_xpb
            pc3 += 1
    gate_io = [pl.tkt("io0"), pl.tkt("io1")]
    pl.barrier()
    ln2 = []
    ln2fc = []
    for tt in range(2):
        lt, lfc = emit_ln(CT[:, :], tt, last_xpb, m3, r3, var3, sd3, x2b3, lntmp3)
        ln2.append(lt)
        ln2fc.append(lfc)
    if "ph3" in dump:
        _dump(nc, pl, {"d_xn2": CT}, {"d_xp": xpscr})

    pl.op("g", lambda E: E["g"].dma_start(
        out=aaugs.ap().rearrange("p (c e) -> p c e", c=FC),
        in_=aaugT.rearrange("(c p) e -> p c e", p=128)), dma=True, sem="gx")
    pl.op("g", lambda E: E["g"].dma_start(out=balls[:, :], in_=ballT),
          dma=True, sem="gx")
    t_eb = pl.op("g", lambda E: E["g"].dma_start(out=biass[:, :], in_=ebias),
                 dma=True, sem="gx")
    t_aa = t_bb = t_eb

    zmk_t = [None, None]
    prev_zmk = None
    prev_ttp = None
    prev_ttp2 = None
    pc4 = 0
    w4ld = [None, None]
    w4use = [None, None]
    bank_ev4 = {}
    xp_t = [None, None]
    out_use = [None, None]
    y_store = []

    def emit_w4(nb):
        par = nb % 2
        wsrc = wup[:, nb * 128:(nb + 1) * 128].rearrange("(c p) o -> p c o", p=128)
        w4ld[par] = pl.op("g", lambda E, par=par, wsrc=wsrc: E["g"].dma_start(
            out=W3[par].ap().rearrange("p (c o) -> p c o", c=FC), in_=wsrc),
            deps=[w4use[par]], dma=True, sem=f"w{par}")

    def emit_base(nb, tt):
        nonlocal pc4
        par = nb % 2
        pcp = pc4 % 2
        bank = pC if pcp == 0 else pD
        xpar = pc4 % 2
        xp_t[xpar] = pl.op("y", lambda E, nb=nb, tt=tt, xpar=xpar: E["y"].dma_start(
            out=xtile[xpar][:, :],
            in_=xpscr.ap()[nb * 128:(nb + 1) * 128, tt * TN: tt * TN + TN]),
            deps=[out_use[xpar], gate_io[xpar]], dma=True, sem=f"io{xpar}")
        for fc in range(FC):
            pl.op("t", lambda E, fc=fc, par=par, tt=tt, bank=bank: E["t"].matmul(
                bank[:, :], W3[par][:, fc * 128:(fc + 1) * 128],
                CT[:, fc * TOK + tt * TN: fc * TOK + tt * TN + TN],
                start=(fc == 0), stop=False, skip_group_check=True),
                deps=[w4ld[par], ln2fc[tt][fc], bank_ev4.get(pcp)], tick=False)
        pc4 += 1
        return bank, xpar, pcp

    def emit_delta_evac(nb, tt, bank, xpar, pcp):
        tk_mm = pl.op("t", lambda E, nb=nb, tt=tt, bank=bank: E["t"].matmul(
            bank[:, :], balls[:, nb * 128:(nb + 1) * 128], zmk[tt][:, :],
            start=False, stop=True, skip_group_check=True),
            deps=[t_bb, zmk_t[tt]], tick=True)
        w4use[nb % 2] = tk_mm
        ta = pl.op("v", lambda E, xpar=xpar, bank=bank: E["v"].tensor_tensor(
            xpf[xpar][:, :], bank[:, :], xtile[xpar][:, :], OP.add),
            deps=[tk_mm, xp_t[xpar], out_use[xpar]], tick=True)
        bank_ev4[pcp] = ta
        ty = pl.op("y", lambda E, xpar=xpar, nb=nb, tt=tt: E["y"].dma_start(
            out=y[nb * 128:(nb + 1) * 128, tt * TN: tt * TN + TN],
            in_=xpf[xpar][:, :]), deps=[ta], dma=True)
        out_use[xpar] = ty
        y_store.append(ty)

    emit_w4(0)
    for tt in range(2):
        tk_z = None
        prev_v4 = pl.tkt("v")
        for fc in range(FC):
            tk_z = pl.op("t", lambda E, fc=fc, tt=tt: E["t"].matmul(
                pA[0:36, 0:512], aaugs[:, fc * 36:(fc + 1) * 36],
                CT[:, fc * TOK + tt * TN: fc * TOK + tt * TN + TN],
                start=(fc == 0), stop=(fc == FC - 1)),
                deps=[t_aa, ln2fc[tt][fc], prev_v4, prev_ttp],
                tick=(fc == FC - 1))
        t_zf = pl.op("v", lambda E: E["v"].tensor_copy(zsb[:, :], pA[0:36, 0:512]),
                     deps=[tk_z, prev_zmk], tick=True)
        t_lg = pl.op("v", lambda E: E["v"].tensor_scalar(
            lgsb[:, :], pA[32:36, 0:512], biass[32:36, 0:1], 0.0, OP.add),
            deps=[tk_z, t_eb, t_zf, prev_ttp], tick=True)
        bk0, xp0, pp0 = emit_base(0, tt)
        t_tp = None
        for ch in range(4):
            t_tp = pl.op("t", lambda E, ch=ch: E["t"].transpose(
                pA[:, 512 + ch * 4: 512 + ch * 4 + 4],
                lgsb[:, ch * 128:(ch + 1) * 128], identf[0:4, 0:4]),
                deps=[t_lg], tick=(ch == 3))
        t_lgt = pl.op("v", lambda E: E["v"].tensor_copy(lgt[:, :], pA[:, 512:528]),
                      deps=[t_tp, prev_ttp2], tick=True)
        last = t_lgt
        for ch in range(4):
            last = pl.op("v", lambda E, ch=ch: E["v"].reduce_max(
                m4[:, ch:ch + 1], lgt[:, ch * 4:(ch + 1) * 4], axis=AX.X),
                deps=[last], tick=True)
            last = pl.op("v", lambda E, ch=ch: E["v"].tensor_tensor(
                ohtm[:, ch * 4: ch * 4 + 1], lgt[:, ch * 4: ch * 4 + 1],
                m4[:, ch:ch + 1], OP.is_ge), deps=[last], tick=True)
            last = pl.op("v", lambda E, ch=ch: E["v"].tensor_copy(
                tb1[:, 0:1], ohtm[:, ch * 4: ch * 4 + 1]), deps=[last], tick=True)
            for e_ in range(1, 4):
                last = pl.op("v", lambda E, ch=ch, e_=e_: E["v"].tensor_tensor(
                    tb2[:, 0:1], lgt[:, ch * 4 + e_: ch * 4 + e_ + 1],
                    m4[:, ch:ch + 1], OP.is_ge), deps=[last], tick=True)
                last = pl.op("v", lambda E: E["v"].tensor_scalar(
                    tb2[:, 1:2], tb1[:, 0:1], -1.0, 1.0, OP.mult, OP.add),
                    deps=[last], tick=True)
                last = pl.op("v", lambda E, ch=ch, e_=e_: E["v"].tensor_tensor(
                    ohtm[:, ch * 4 + e_: ch * 4 + e_ + 1], tb2[:, 0:1],
                    tb2[:, 1:2], OP.mult), deps=[last], tick=True)
                last = pl.op("v", lambda E, ch=ch, e_=e_: E["v"].tensor_tensor(
                    tb1[:, 0:1], tb1[:, 0:1],
                    ohtm[:, ch * 4 + e_: ch * 4 + e_ + 1], OP.add),
                    deps=[last], tick=True)
        t_tp2 = None
        for ch in range(4):
            t_tp2 = pl.op("t", lambda E, ch=ch: E["t"].transpose(
                pB[0:4, 512 + ch * 128: 512 + ch * 128 + 128],
                ohtm[:, ch * 4:(ch + 1) * 4], identf[:, :]),
                deps=[last], tick=(ch == 3))
        t_oh = pl.op("v", lambda E: E["v"].tensor_copy(
            ohsb[:, :], pB[0:4, 512:1024]), deps=[t_tp2], tick=True)
        t_ohx = pl.op("t", lambda E: E["t"].matmul(
            pB[0:32, 0:512], e32[:, :], ohsb[:, :], start=True, stop=True),
            deps=[t_oh], tick=True)
        zmk_t[tt] = pl.op("v", lambda E, tt=tt: E["v"].tensor_tensor(
            zmk[tt][:, :], zsb[0:32, :], pB[0:32, 0:512], OP.mult),
            deps=[t_ohx, t_zf], tick=True)
        prev_zmk = zmk_t[tt]
        prev_ttp = t_tp
        prev_ttp2 = t_tp2
        emit_delta_evac(0, tt, bk0, xp0, pp0)

    for nb in range(1, FC):
        emit_w4(nb)
        for tt in range(2):
            bk, xp_, pp = emit_base(nb, tt)
            emit_delta_evac(nb, tt, bk, xp_, pp)

    if "ph4" in dump:
        pl.barrier()
        for nm, tns, shp, dt in (("d_zmk0", zmk[0], [32, TN], BF16),
                                 ("d_zmk1", zmk[1], [32, TN], BF16),
                                 ("d_zsb", zsb, [36, TN], F32),
                                 ("d_ohsb", ohsb, [4, TN], BF16),
                                 ("d_lgt", lgt, [128, 16], F32)):
            dap = nc.dram_tensor(nm, shp, dt, kind="ExternalOutput").ap()
            pl.op("y", lambda E, dap=dap, tns=tns: E["y"].dma_start(
                out=dap, in_=tns[:, :]), dma=True, sem="dbg")

    pl.barrier()

    SEMNAMES = ["t", "v", "s", "g", "y", "ck", "cv", "gx", "gy", "gz", "gw",
                "w0", "w1", "kv0", "kv1", "kv2", "io0", "io1", "dbg", "rs"]
    with ExitStack() as st:
        SEMS = {n: st.enter_context(nc.semaphore(n)) for n in SEMNAMES}
        block = st.enter_context(nc.Block())
        E = {"t": nc.tensor, "v": nc.vector, "s": nc.scalar,
             "g": nc.gpsimd, "y": nc.sync}

        def replay_one(r):
            e = E[r["eng"]]
            for (sm, v) in r["waits"]:
                e.wait_ge(SEMS[sm], v)
            if r["fn"] is not None:
                ins = r["fn"](E)
                if r["inc"] is not None:
                    sm, n, raw = r["inc"]
                    if raw:
                        ins.then_inc(SEMS[sm])
                    else:
                        ins.then_inc(SEMS[sm], n)

        def replay(recs):
            i = 0
            while i < len(recs):
                r = recs[i]
                if r.get("ctl") == "sw2_begin":
                    slot = r["slot"]
                    j = i + 1
                    body = []
                    while recs[j].get("ctl") != "sw2_end":
                        body.append(recs[j])
                        j += 1
                    deltas = recs[j]["deltas"]
                    assert all(rr["eng"] in ("t", "s") for rr in body)
                    for ename, reg in (("t", regT), ("s", regS)):
                        e = E[ename]
                        e.reg_load(reg, ifl[0:1, slot:slot + 1])
                        with e.If_ne(reg, 0):
                            for rr in body:
                                if rr["eng"] == ename:
                                    replay_one(rr)
                        with e.Else():
                            if deltas.get(ename):
                                e.sem_inc(SEMS[ename], deltas[ename])
                        e.end_ifs()
                    i = j + 1
                    continue
                replay_one(r)
                i += 1

        replay(pl.segs["main0"])
        pid = nc.partition_id()
        for i in nc.Switch(engines=[nc.sync, nc.scalar], index=[pid, pid], n=NCORE):
            replay(pl.segs[f"sw{i}"])
        replay(pl.segs["main1"])

    return nc


def _dump(nc, pl, sbuf_map, dram_map):
    pl.barrier()
    for name, t in sbuf_map.items():
        d = nc.dram_tensor(name, [128, FC * TOK], BF16, kind="ExternalOutput").ap()
        pl.op("y", lambda E, d=d, t=t: E["y"].dma_start(out=d, in_=t[:, :]),
              dma=True, sem="dbg")
    for name, t in dram_map.items():
        h = t.ap()
        d = nc.dram_tensor(name, list(h.shape), BF16 if name != "d_xp" else F32,
                           kind="ExternalOutput").ap()
        pl.op("y", lambda E, d=d, h=h: E["y"].dma_start(out=d, in_=h),
              dma=True, sem="dbg")
    pl.barrier()


def _prep_inputs(L0_attn_x, L0_attn_wq, L0_attn_wk, L0_attn_wv, L0_attn_wo,
                 L0_ffn_W_up, L0_ffn_gate_w, L0_ffn_expert_bias,
                 L0_ffn_lora_A, L0_ffn_lora_B):
    f32 = lambda a: np.ascontiguousarray(np.asarray(a), dtype=np.float32)
    x = f32(L0_attn_x)[0]
    wq = f32(L0_attn_wq)
    wk = f32(L0_attn_wk)
    wv = f32(L0_attn_wv)
    wo = f32(L0_attn_wo)
    wup = np.ascontiguousarray(f32(L0_ffn_W_up).T)
    gate = f32(L0_ffn_gate_w)
    ebias_ = f32(L0_ffn_expert_bias)
    la = f32(L0_ffn_lora_A)
    lb = f32(L0_ffn_lora_B)

    aaug = np.concatenate([la.reshape(32, D), gate], axis=0)
    aaugT = np.ascontiguousarray(aaug.T)
    ballT = np.ascontiguousarray(lb.transpose(0, 2, 1).reshape(32, D))
    eb36 = np.zeros((36, 1), np.float32)
    eb36[32:36, 0] = ebias_

    identf = np.eye(128, dtype=np.float32)
    onesb = np.ones((128, 128), dtype=BF)
    psw = np.zeros((128, 128), np.float32)
    for k in range(128):
        psw[k, k ^ 1] = 1.0
    pswap = psw.astype(BF)
    e32 = np.zeros((4, 32), np.float32)
    for e_ in range(4):
        e32[e_, e_ * 8:(e_ + 1) * 8] = 1.0
    e32 = e32.astype(BF)
    kk = np.arange(128)[:, None]
    qq = np.arange(TN)[None, :]
    dm = np.zeros((128, 4 * TN), np.float32)
    for j in range(4):
        dm[:, j * TN:(j + 1) * TN] = (qq >= 128 * j + kk).astype(np.float32)
    dmask = dm.astype(BF)

    inv = 1.0 / (10000.0 ** (np.arange(0, HD, 2, dtype=np.float64) / HD))

    in_maps = []
    for c in range(NCORE):
        bA, bB = c, 15 - c
        toks = np.concatenate([np.arange(bA * TN, (bA + 1) * TN),
                               np.arange(bB * TN, (bB + 1) * TN)])
        xc = np.ascontiguousarray(
            np.concatenate([x[bA * TN:(bA + 1) * TN], x[bB * TN:(bB + 1) * TN]], 0).T)
        ang = inv[:, None] * toks[None, :].astype(np.float64)
        cosr = np.cos(ang)
        sinr = np.sin(ang)
        cosT = np.repeat(cosr, 2, axis=0).astype(np.float32)
        ssinT = np.repeat(sinr, 2, axis=0).astype(np.float32)
        ssinT[0::2, :] *= -1.0
        sf = np.zeros((128, NSLOT * 128), np.float32)
        ifl_np = np.zeros((1, 32), np.int32)
        for s_ in range(NSLOT):
            if slot_block(c, s_) is not None:
                sf[:, s_ * 128:(s_ + 1) * 128] = 1.0
                ifl_np[0, s_] = 1
        in_maps.append({
            "xT": xc, "wq": wq, "wk": wk, "wv": wv, "wo": wo, "wup": wup,
            "aaugT": aaugT, "ballT": ballT, "ebias": eb36,
            "cosT": cosT, "ssinT": ssinT, "identf": identf,
            "onesb": onesb, "pswap": pswap, "e32": e32,
            "dmask": dmask, "sflags": sf.astype(BF), "iflags": ifl_np,
        })
    return in_maps


def kernel(L0_attn_x, L0_attn_wq, L0_attn_wk, L0_attn_wv, L0_attn_wo,
           L0_ffn_W_up, L0_ffn_gate_w, L0_ffn_expert_bias,
           L0_ffn_lora_A, L0_ffn_lora_B):
    in_maps = _prep_inputs(L0_attn_x, L0_attn_wq, L0_attn_wk, L0_attn_wv,
                           L0_attn_wo, L0_ffn_W_up, L0_ffn_gate_w,
                           L0_ffn_expert_bias, L0_ffn_lora_A, L0_ffn_lora_B)
    nc = build()
    res = run_bass_kernel_spmd(nc, in_maps, core_ids=list(range(NCORE)))
    out = np.empty((1, S, D), np.float32)
    for c in range(NCORE):
        yc = np.asarray(res.results[c]["y"])
        bA, bB = c, 15 - c
        out[0, bA * TN:(bA + 1) * TN] = yc[:, 0:TN].T
        out[0, bB * TN:(bB + 1) * TN] = yc[:, TN:2 * TN].T
    return out


# version 21
# speedup vs baseline: 1.1469x; 1.0723x over previous
import numpy as np
import ml_dtypes
from contextlib import ExitStack

import concourse.bass as bass
import concourse.mybir as mybir
from concourse.bass_utils import run_bass_kernel_spmd

F32 = mybir.dt.float32
BF16 = mybir.dt.bfloat16
AF = mybir.ActivationFunctionType
OP = mybir.AluOpType
AX = mybir.AxisListType
BF = ml_dtypes.bfloat16

D = 4096
S = 8192
NH = 32
NKV = 8
HD = 128
FC = 32
TOK = 1024
TN = 512
NB = 16
NCORE = 8
EPS = 1e-5
CSH = 12.0
QKS = 1.0 / np.sqrt(128.0)
NSLOT = 24

ENGS = ("t", "v", "s", "g", "y")


def owner(b):
    return b if b < 8 else 15 - b


def owner_tt(b):
    return 0 if b < 8 else 1


def slot_block(c, s):
    if s < 7:
        return s if s < c else None
    if s == 7:
        return c
    if s < 23:
        return (s - 8) if (s - 8) < 15 - c else None
    return 15 - c


class Plan:
    def __init__(self):
        self.segs = {}
        self.cur = None
        self.cnt = {}
        self.lastw = {e: {} for e in ENGS}
        self.last_op = {}

    def seg(self, name):
        self.cur = self.segs.setdefault(name, [])

    def snap(self):
        import copy
        return (dict(self.cnt), copy.deepcopy(self.lastw), dict(self.last_op))

    def restore(self, st):
        import copy
        self.cnt = dict(st[0])
        self.lastw = copy.deepcopy(st[1])
        self.last_op = dict(st[2])

    def op(self, eng, fn, deps=(), tick=False, dma=False, sem=None, rawinc=False):
        waits = []
        for d in deps:
            if d is None:
                continue
            s, v = d
            if v <= 0:
                continue
            if self.lastw[eng].get(s, -1) >= v:
                continue
            self.lastw[eng][s] = v
            waits.append((s, v))
        inc = None
        if dma:
            tick = True
        if tick:
            sm = sem or eng
            n = 16 if dma else 1
            self.cnt[sm] = self.cnt.get(sm, 0) + n
            inc = (sm, n, rawinc)
        rec = dict(eng=eng, fn=fn, waits=waits, inc=inc)
        self.cur.append(rec)
        if fn is not None:
            self.last_op[eng] = rec
        if inc:
            return (inc[0], self.cnt[inc[0]])
        return None

    def tkt(self, sm):
        return (sm, self.cnt.get(sm, 0))

    def ensure_tick(self, eng):
        rec = self.last_op.get(eng)
        if rec is not None and rec["inc"] is None:
            self.cnt[eng] = self.cnt.get(eng, 0) + 1
            rec["inc"] = (eng, 1, False)
        return (eng, self.cnt.get(eng, 0))

    DMASEMS = ("y", "gx", "gy", "gz", "gw", "w0", "w1",
               "w0q0", "w0q1", "w0q2", "w0q3", "w1q0", "w1q1", "w1q2", "w1q3",
               "kv0", "kv1", "kv2", "io0", "io1", "dbg", "rs")

    def sw2_begin(self, slot):
        self.cur.append(dict(eng=None, ctl="sw2_begin", slot=slot, fn=None))
        import copy
        self._sw2_lastw = copy.deepcopy(self.lastw)
        self._sw2_cnt = dict(self.cnt)

    def sw2_end(self):
        import copy
        deltas = {}
        for sm in ("t", "s"):
            d = self.cnt.get(sm, 0) - self._sw2_cnt.get(sm, 0)
            if d:
                deltas[sm] = d
        self.lastw = copy.deepcopy(self._sw2_lastw)
        self.cur.append(dict(eng=None, ctl="sw2_end", deltas=deltas, fn=None))

    def barrier(self):
        tks = {e: self.ensure_tick(e) for e in ENGS}
        dts = [self.tkt(s) for s in self.DMASEMS]
        for e in ENGS:
            self.op(e, None, deps=[tks[o] for o in ENGS if o != e] + dts)


def build(dump=()):
    nc = bass.Bass("TRN2", target_bir_lowering=False, debug=False)

    xT = nc.dram_tensor("xT", [D, TOK], F32, kind="ExternalInput").ap()
    wq = nc.dram_tensor("wq", [D, D], F32, kind="ExternalInput").ap()
    wk = nc.dram_tensor("wk", [D, 1024], F32, kind="ExternalInput").ap()
    wv = nc.dram_tensor("wv", [D, 1024], F32, kind="ExternalInput").ap()
    wo = nc.dram_tensor("wo", [D, D], F32, kind="ExternalInput").ap()
    wup = nc.dram_tensor("wup", [D, D], F32, kind="ExternalInput").ap()
    aaugT = nc.dram_tensor("aaugT", [D, 36], F32, kind="ExternalInput").ap()
    ballT = nc.dram_tensor("ballT", [32, D], F32, kind="ExternalInput").ap()
    ebias = nc.dram_tensor("ebias", [36, 1], F32, kind="ExternalInput").ap()
    cosT_d = nc.dram_tensor("cosT", [128, TOK], F32, kind="ExternalInput").ap()
    ssinT_d = nc.dram_tensor("ssinT", [128, TOK], F32, kind="ExternalInput").ap()
    identf_d = nc.dram_tensor("identf", [128, 128], F32, kind="ExternalInput").ap()
    onesb_d = nc.dram_tensor("onesb", [128, 128], BF16, kind="ExternalInput").ap()
    pswap_d = nc.dram_tensor("pswap", [128, 128], BF16, kind="ExternalInput").ap()
    e32_d = nc.dram_tensor("e32", [4, 32], BF16, kind="ExternalInput").ap()
    dmask_d = nc.dram_tensor("dmask", [128, 4 * TN], BF16, kind="ExternalInput").ap()
    sflags_d = nc.dram_tensor("sflags", [128, NSLOT * 128], BF16, kind="ExternalInput").ap()
    iflags_d = nc.dram_tensor("iflags", [1, 32], mybir.dt.int32, kind="ExternalInput").ap()
    y = nc.dram_tensor("y", [D, TOK], F32, kind="ExternalOutput").ap()

    ksrc = nc.dram_tensor("ksrc", [2 * 1024, 512], BF16, kind="Internal")
    vsrc = nc.dram_tensor("vsrc", [1024, 1024], BF16, kind="Internal")
    kdst = nc.dram_tensor("kdst", [NCORE * 2 * 1024, 512], BF16, kind="Internal",
                          addr_space="Shared")
    vdst = nc.dram_tensor("vdst", [NCORE * 1024, 1024], BF16, kind="Internal",
                          addr_space="Shared")
    keff = nc.dram_tensor("keff", [NSLOT * 1024, 512], BF16, kind="Internal")
    veff = nc.dram_tensor("veff", [NSLOT * 8 * 4 * 128, 128], BF16, kind="Internal")
    xpscr = nc.dram_tensor("xpscr", [D, TOK], F32, kind="Internal")

    kdst_f = kdst.ap().rearrange("a b -> (a b)")
    vdst_f = vdst.ap().rearrange("a b -> (a b)")
    keff_f = keff.ap().rearrange("a b -> (a b)")
    veff_f = veff.ap().rearrange("a b -> (a b)")
    vsrc_f = vsrc.ap().rearrange("a b -> (a b)")

    SB0 = 20480

    def at(name, shape, dtype, off):
        return nc.alloc_sbuf_tensor_at(name, shape, dtype, offset=SB0 + off)

    AT = at("A_xn", [128, FC * TOK], BF16, 0)
    BT = at("B_av", [128, FC * TOK], BF16, 65536)
    CT = at("C_q", [128, FC * TOK], BF16, 131072)

    identf = at("identf", [128, 128], F32, 196608)
    onesb = at("onesb", [128, 128], BF16, 197120)
    pswap = at("pswap", [128, 128], BF16, 197376)
    e32 = at("e32", [4, 32], BF16, 197632)
    dmask = at("dmask", [128, 4 * TN], BF16, 197696)
    sflags = at("sflags", [128, NSLOT * 128], BF16, 201792)
    cm12 = at("cm12", [128, 1], F32, 207936)
    ifl = at("ifl", [1, 32], mybir.dt.int32, 207968)

    W1 = [at(f"W1_{i}", [128, FC * 128], BF16, 65536 + i * 8192) for i in range(2)]
    cosT = at("cosT", [128, TOK], F32, 81920)
    ssinT = at("ssinT", [128, TOK], F32, 86016)
    m1 = at("m1", [128, TN], F32, 90112)
    r1 = at("r1", [128, TN], F32, 92160)
    var1 = at("var1", [128, TN], F32, 94208)
    sd1 = at("sd1", [128, TN], F32, 96256)
    x2b = [at(f"x2b_{i}", [128, TN], BF16, 98304 + i * 1024) for i in range(2)]
    lntmp = [at(f"lntmp_{i}", [128, TN], F32, 100352 + i * 2048) for i in range(2)]
    ktmp = [at(f"ktmp_{i}", [128, TN], BF16, 104448 + i * 1024) for i in range(2)]
    tR1 = [at(f"tR1_{i}", [128, TN], F32, 106496 + i * 2048) for i in range(2)]
    tR2 = [at(f"tR2_{i}", [128, TN], F32, 110592 + i * 2048) for i in range(2)]
    vtmp = [at(f"vtmp_{i}", [128, TN], F32, 114688 + i * 2048) for i in range(2)]
    vtt = [at(f"vtt_{i}", [128, 128], BF16, 118784 + i * 256) for i in range(2)]
    kout = [at(f"kout_{i}", [128, TN], BF16, 119296 + i * 1024) for i in range(2)]
    zsb0 = at("zsb0", [128, 4096], BF16, 121344)

    kb = [at(f"kb_{i}", [128, 512], BF16, 32768 + i * 1024) for i in range(3)]
    vb = [at(f"vb_{i}", [128, 512], BF16, 35840 + i * 1024) for i in range(3)]
    attb = [at(f"attb_{i}", [128, 1024], BF16, 38912 + i * 2048) for i in range(2)]
    rinv = [at(f"rinv_{i}", [128, 512], F32, 43008 + i * 2048) for i in range(2)]
    avraw = [at(f"avraw_{i}", [128, 512], F32, 47104 + i * 2048) for i in range(2)]
    smraw = [at(f"smraw_{i}", [128, 512], F32, 51200 + i * 2048) for i in range(2)]

    W3 = [at(f"W3_{i}", [128, FC * 128], BF16, 0 + i * 8192) for i in range(2)]
    xtile = [at(f"xtile_{i}", [128, TN], F32, 16384 + i * 2048) for i in range(2)]
    xpf = [at(f"xpf_{i}", [128, TN], F32, 20480 + i * 2048) for i in range(2)]
    m3 = at("m3", [128, TN], F32, 24576)
    r3 = at("r3", [128, TN], F32, 26624)
    var3 = at("var3", [128, TN], F32, 28672)
    sd3 = at("sd3", [128, TN], F32, 30720)
    x2b3 = [at(f"x2b3_{i}", [128, TN], BF16, 32768 + i * 1024) for i in range(2)]
    lntmp3 = [at(f"lntmp3_{i}", [128, TN], F32, 34816 + i * 2048) for i in range(2)]

    zsb = at("zsb", [36, TN], F32, 40960)
    zmk = [at(f"zmk_{i}", [32, TN], BF16, 43008 + i * 1024) for i in range(2)]
    lgsb = at("lgsb", [4, TN], F32, 45056)
    lgt = at("lgt", [128, 16], F32, 47104)
    m4 = at("m4", [128, 4], F32, 47232)
    ohtm = at("ohtm", [128, 16], F32, 47360)
    tb1 = at("tb1", [128, 4], F32, 47488)
    tb2 = at("tb2", [128, 4], F32, 47552)
    ohsb = at("ohsb", [4, TN], BF16, 47616)
    aaugs = at("aaugs", [128, FC * 36], BF16, 65536)
    balls = at("balls", [32, D], BF16, 68096)
    biass = at("biass", [36, 1], F32, 76288)

    pA = nc.alloc_psum_tensor("pA", [128, 1024], F32)
    pB = nc.alloc_psum_tensor("pB", [128, 1024], F32)
    pC = nc.alloc_psum_tensor("pC", [128, 512], F32)
    pD = nc.alloc_psum_tensor("pD", [128, 512], F32)
    pE = nc.alloc_psum_tensor("pE", [128, 512], F32)
    pF = nc.alloc_psum_tensor("pF", [128, 512], F32)

    pl = Plan()
    pl.seg("main0")

    tload = []
    for dst, src in ((identf, identf_d), (onesb, onesb_d), (pswap, pswap_d),
                     (e32, e32_d), (dmask, dmask_d), (sflags, sflags_d),
                     (cosT, cosT_d), (ssinT, ssinT_d), (ifl, iflags_d)):
        tload.append(pl.op("y", lambda E, dst=dst, src=src:
                           E["y"].dma_start(out=dst[:, :], in_=src), dma=True))
    t_tables = pl.tkt("y")
    t_zsb0 = pl.op("v", lambda E: E["v"].memset(zsb0[:, :], 0.0), tick=True)
    t_cm12 = pl.op("v", lambda E: E["v"].memset(cm12[:, :], -CSH), tick=True)

    x_src = xT.rearrange("(c p) t -> p c t", p=128)
    XSEM = ("gx", "gy", "gz", "gw")
    t_xq = []
    for q4 in range(4):
        t_xq.append(pl.op("g", lambda E, q4=q4: E["g"].dma_start(
            out=AT.ap().rearrange("p (c t) -> p c t", c=FC)[:, q4 * 8:(q4 + 1) * 8, :],
            in_=x_src[:, q4 * 8:(q4 + 1) * 8, :]),
            dma=True, sem=XSEM[q4]))
    wk_pre = [None, None]
    for nb in range(2):
        wsrc = wk[:, nb * 128:(nb + 1) * 128].rearrange("(c p) o -> p c o", p=128)
        wk_pre[nb] = pl.op("g", lambda E, nb=nb, wsrc=wsrc: E["g"].dma_start(
            out=W1[nb].ap().rearrange("p (c o) -> p c o", c=FC), in_=wsrc),
            deps=[], dma=True, sem=f"w{nb}")
    t_xl = [t_xq[3], t_xq[3]]

    def emit_ln(arena, tt, dep_in, mT, rT, varT, sdT, x2T, lnT, dep_extra=(),
                dep_fn=None):
        def sl(fc):
            return arena[:, fc * TOK + tt * TN: fc * TOK + tt * TN + TN]
        if dep_fn is None:
            dep_fn = lambda fc: dep_in
        prev_v = pl.tkt("v")
        prev_t = pl.tkt("t")
        tk_s1 = None
        for fc in range(FC):
            tk_s1 = pl.op("t", lambda E, fc=fc: E["t"].matmul(
                pE[:, :], onesb[:, :], sl(fc), start=(fc == 0), stop=(fc == FC - 1)),
                deps=[dep_fn(fc), t_tables, prev_v] + list(dep_extra),
                tick=(fc == FC - 1))
        sq_pe = [None, None]
        for fc in range(FC):
            par = fc % 2
            d = [dep_fn(fc), prev_t]
            if sq_pe[par] is not None:
                d.append(sq_pe[par])
            tk_sq = pl.op("s", lambda E, fc=fc, par=par: E["s"].activation(
                x2T[par][:, :], sl(fc), AF.Square), deps=d, tick=True)
            sq_pe[par] = pl.op("t", lambda E, par=par, fc=fc: E["t"].matmul(
                pF[:, :], onesb[:, :], x2T[par][:, :],
                start=(fc == 0), stop=(fc == FC - 1)),
                deps=[tk_sq, prev_v], tick=True)
        tk_s2 = sq_pe[(FC - 1) % 2]
        a = pl.op("v", lambda E: E["v"].tensor_scalar(
            mT[:, :], pE[:, :], 1.0 / D, 0.0, OP.mult), deps=[tk_s1], tick=True)
        b = pl.op("v", lambda E: E["v"].tensor_scalar(
            varT[:, :], pF[:, :], 1.0 / D, EPS, OP.mult, OP.add),
            deps=[tk_s2, a], tick=True)
        cgate = pl.op("v", lambda E: E["v"].tensor_tensor(
            sdT[:, :], mT[:, :], mT[:, :], OP.mult), deps=[b], tick=True)
        dd = pl.op("v", lambda E: E["v"].tensor_tensor(
            varT[:, :], varT[:, :], sdT[:, :], OP.subtract), deps=[cgate], tick=True)
        ee = pl.op("s", lambda E: E["s"].activation(
            sdT[:, :], varT[:, :], AF.Sqrt), deps=[dd], tick=True)
        ff = pl.op("v", lambda E: E["v"].reciprocal(rT[:, :], sdT[:, :]),
                   deps=[ee], tick=True)
        lastv = ff
        lastg = None
        fcticks = [None] * FC
        for fc in range(FC):
            par = fc % 2
            eng = "v" if par == 0 else "g"
            prev = lastv if par == 0 else lastg
            g1 = pl.op(eng, lambda E, fc=fc, par=par, eng=eng: E[eng].tensor_tensor(
                lnT[par][:, :], sl(fc), mT[:, :], OP.subtract),
                deps=[prev, ff, tk_s2], tick=True)
            tkn = pl.op(eng, lambda E, fc=fc, par=par, eng=eng: E[eng].tensor_tensor(
                sl(fc), lnT[par][:, :], rT[:, :], OP.mult), deps=[g1], tick=True)
            fcticks[fc] = tkn
            if par == 0:
                lastv = tkn
            else:
                lastg = tkn
        last = pl.op("v", lambda E: E["v"].nop(), deps=[lastv, lastg], tick=True)
        return last, fcticks

    ln1 = []
    ln1fc = []
    for tt in range(2):
        lt, lfc = emit_ln(AT[:, :], tt, t_xq[3], m1, r1, var1, sd1, x2b, lntmp,
                          dep_fn=lambda fc: t_xq[fc // 8])
        ln1.append(lt)
        ln1fc.append(lfc)

    mats = [("k", wk, 8), ("v", wv, 8), ("q", wq, 32)]
    pc = 0
    prc = 0
    wld = [None, None]
    wuse = [None, None]
    projev = {}
    pA_use = [None, None]
    ktmp_use = [None, None]
    vtmp_use = [None, None]
    vtt_use = [None, None]
    kout_use = [None, None]
    tR1_use = [None, None]
    tR2_use = [None, None]
    nbg = 0
    kv_store = []
    for kind, wmat, nnb in mats:
        for nb in range(nnb):
            par = nbg % 2
            wsrc = wmat[:, nb * 128:(nb + 1) * 128].rearrange(
                "(c p) o -> p c o", p=128)
            if kind == "k" and nb < 2:
                wldq = [wk_pre[nb]] * 4
            else:
                wldq = []
                for q4 in range(4):
                    wldq.append(pl.op("g", lambda E, par=par, wsrc=wsrc, q4=q4:
                                      E["g"].dma_start(
                        out=W1[par].ap().rearrange(
                            "p (c o) -> p c o", c=FC)[:, q4 * 8:(q4 + 1) * 8, :],
                        in_=wsrc[:, q4 * 8:(q4 + 1) * 8, :]),
                        deps=[wuse[par]], dma=True, sem=f"w{par}q{q4}"))
            for tt in range(2):
                pcp = pc % 2
                bank = pC if pcp == 0 else pD
                tk_mm = None
                for fc in range(FC):
                    tk_mm = pl.op("t", lambda E, fc=fc, par=par, tt=tt, bank=bank:
                                  E["t"].matmul(
                                      bank[:, :],
                                      W1[par][:, fc * 128:(fc + 1) * 128],
                                      AT[:, fc * TOK + tt * TN: fc * TOK + tt * TN + TN],
                                      start=(fc == 0), stop=(fc == FC - 1)),
                                  deps=[wldq[fc // 8], ln1fc[tt][fc],
                                        projev.get(pcp)],
                                  tick=(fc == FC - 1))
                wuse[par] = tk_mm
                if kind in ("k", "q"):
                    kp = pc % 2
                    tcp = pl.op("s", lambda E, bank=bank, kp=kp: E["s"].copy(
                        ktmp[kp][:, :], bank[:, :]),
                        deps=[tk_mm, ktmp_use[kp]], tick=True)
                    projev[pcp] = tcp
                    sl_ = prc % 2
                    tpm = pl.op("t", lambda E, kp=kp, sl_=sl_: E["t"].matmul(
                        pA[:, sl_ * 512:sl_ * 512 + 512], pswap[:, :],
                        ktmp[kp][:, :], start=True, stop=True),
                        deps=[tcp, pA_use[sl_]], tick=True)
                    t1 = pl.op("v", lambda E, kp=kp, tt=tt: E["v"].tensor_tensor(
                        tR1[kp][:, :], ktmp[kp][:, :],
                        cosT[:, tt * TN: tt * TN + TN], OP.mult),
                        deps=[tcp, tR1_use[kp], t_tables], tick=True)
                    t2 = pl.op("v", lambda E, kp=kp, sl_=sl_, tt=tt: E["v"].tensor_tensor(
                        tR2[kp][:, :], pA[:, sl_ * 512:sl_ * 512 + 512],
                        ssinT[:, tt * TN: tt * TN + TN], OP.mult),
                        deps=[tpm, t1, tR2_use[kp]], tick=True)
                    pA_use[sl_] = t2
                    prc += 1
                    ktmp_use[kp] = t2
                    if kind == "q":
                        h = nb
                        tq = pl.op("v", lambda E, kp=kp, h=h, tt=tt: E["v"].tensor_tensor(
                            CT[:, h * TOK + tt * TN: h * TOK + tt * TN + TN],
                            tR1[kp][:, :], tR2[kp][:, :], OP.add),
                            deps=[t2], tick=True)
                        tR1_use[kp] = tq
                        tR2_use[kp] = tq
                    else:
                        ko = pl.op("v", lambda E, kp=kp: E["v"].tensor_tensor(
                            kout[kp][:, :], tR1[kp][:, :], tR2[kp][:, :], OP.add),
                            deps=[t2, kout_use[kp]], tick=True)
                        tR1_use[kp] = ko
                        tR2_use[kp] = ko
                        g_ = nb
                        td = pl.op("y", lambda E, kp=kp, g_=g_, tt=tt: E["y"].dma_start(
                            out=ksrc.ap()[tt * 1024 + g_ * 128: tt * 1024 + g_ * 128 + 128, :],
                            in_=kout[kp][:, :]), deps=[ko], dma=True)
                        kout_use[kp] = td
                        kv_store.append(td)
                else:
                    vp = pc % 2
                    tcp = pl.op("s", lambda E, bank=bank, vp=vp: E["s"].copy(
                        vtmp[vp][:, :], bank[:, :]),
                        deps=[tk_mm, vtmp_use[vp]], tick=True)
                    projev[pcp] = tcp
                    g_ = nb
                    last_v = None
                    for j in range(4):
                        sl_ = prc % 2
                        ttr = pl.op("t", lambda E, vp=vp, j=j, sl_=sl_: E["t"].transpose(
                            pA[:, sl_ * 512:sl_ * 512 + 128],
                            vtmp[vp][:, j * 128:(j + 1) * 128], identf[:, :]),
                            deps=[tcp, pA_use[sl_], t_tables], tick=True)
                        w2 = j % 2
                        tvt = pl.op("s", lambda E, sl_=sl_, w2=w2: E["s"].copy(
                            vtt[w2][:, :], pA[:, sl_ * 512:sl_ * 512 + 128]),
                            deps=[ttr, vtt_use[w2]], tick=True)
                        pA_use[sl_] = tvt
                        prc += 1
                        fo = (g_ * 8 + tt * 4 + j) * 16384
                        tst = pl.op("y", lambda E, w2=w2, fo=fo: E["y"].dma_start(
                            out=vsrc_f[fo:fo + 16384].rearrange("(p d) -> p d", p=128),
                            in_=vtt[w2][:, :]), deps=[tvt], dma=True)
                        vtt_use[w2] = tst
                        kv_store.append(tst)
                        last_v = tst
                    vtmp_use[vp] = last_v
                pc += 1
            nbg += 1
            if kind == "v" and nb == nnb - 1:
                tk_cc_k = pl.op("g", lambda E: E["g"].collective_compute(
                    "AllGather", OP.bypass,
                    replica_groups=[list(range(NCORE))],
                    ins=[ksrc.ap().opt()], outs=[kdst.ap().opt()]),
                    deps=kv_store, tick=True, sem="ck", rawinc=True)
                tk_cc_v = pl.op("g", lambda E: E["g"].collective_compute(
                    "AllGather", OP.bypass,
                    replica_groups=[list(range(NCORE))],
                    ins=[vsrc.ap().opt()], outs=[vdst.ap().opt()]),
                    deps=[], tick=True, sem="cv", rawinc=True)

    base = pl.snap()
    finals = None
    for c in range(NCORE):
        pl.restore(base)
        pl.seg(f"sw{c}")
        first = [("ck", 1), ("cv", 1), t_zsb0]
        for s_ in range(NSLOT):
            b = slot_block(c, s_)
            if b is not None:
                r = owner(b)
                tw = owner_tt(b)
                pl.op("y", lambda E, s_=s_, r=r, tw=tw: E["y"].dma_start(
                    out=keff.ap()[s_ * 1024:(s_ + 1) * 1024, :],
                    in_=kdst.ap()[r * 2048 + tw * 1024: r * 2048 + tw * 1024 + 1024, :]),
                    deps=first, dma=True)
                for g_ in range(8):
                    so = r * (1024 * 1024) + (g_ * 8 + tw * 4) * 16384
                    do = (s_ * 8 + g_) * 65536
                    pl.op("y", lambda E, so=so, do=do: E["y"].dma_start(
                        out=veff_f[do:do + 65536],
                        in_=vdst_f[so:so + 65536]), deps=first, dma=True)
            else:
                pl.op("y", lambda E, s_=s_: E["y"].dma_start(
                    out=keff_f[s_ * 524288: (s_ + 1) * 524288].rearrange(
                        "(p f) -> p f", p=128),
                    in_=zsb0[:, :]), deps=first, dma=True)
                pl.op("y", lambda E, s_=s_: E["y"].dma_start(
                    out=veff_f[s_ * 524288: (s_ + 1) * 524288].rearrange(
                        "(p f) -> p f", p=128),
                    in_=zsb0[:, :]), deps=first, dma=True)
            first = [t_zsb0]
        f = pl.snap()
        if finals is None:
            finals = f[0]
        else:
            assert finals == f[0], f"switch body sem counts differ: {finals} vs {f[0]}"
    pl.restore(base)
    pl.cnt = dict(finals)
    t_relay = pl.tkt("y")
    t_relay2 = pl.tkt("rs")

    pl.seg("main1")
    pl.barrier()

    if "ph1" in dump:
        _dump(nc, pl, {"d_xn": AT, "d_q": CT}, {"d_ksrc": ksrc, "d_vsrc": vsrc})

    regT = nc.alloc_register(mybir.EngineType.PE, "fT")
    regS = nc.alloc_register(mybir.EngineType.Activation, "fS")
    cc = 0
    exp_t = [None, None]
    att_use = [None, None]
    kb_t = [None, None, None]
    vb_t = [None, None, None]
    kb_use = [None, None, None]
    vb_use = [None, None, None]
    avev = {0: None, 1: None}
    rn_use = {0: None, 1: None}
    for g_ in range(8):
        for pr in range(2):
            for sub, srange, qoff in ((0, [7] + list(range(0, 7)), 0),
                                      (1, [23] + list(range(8, 23)), TN)):
                s0 = srange[0]
                s1 = srange[-1]
                def emit_avsums(p):
                    tk_av = None
                    for h2 in range(2):
                        avb = pC if h2 == 0 else pD
                        smb = pE if h2 == 0 else pF
                        pl.op("t", lambda E, lp=p["lp"], j=p["j"], ab=p["ab"], h2=h2,
                              avb=avb, fm=p["fm"]: E["t"].matmul(
                                  avb[:, :], vb[lp][:, j * 128:(j + 1) * 128],
                                  attb[ab][:, h2 * 512:h2 * 512 + 512],
                                  start=fm, stop=True, skip_group_check=True),
                              deps=[p["msk"], p["vbt"], avev[h2]], tick=False)
                        tk_av = pl.op("t", lambda E, s_=p["s"], ab=p["ab"], h2=h2,
                                      smb=smb, fm=p["fm"]: E["t"].matmul(
                                          smb[:, :],
                                          sflags[:, s_ * 128:(s_ + 1) * 128],
                                          attb[ab][:, h2 * 512:h2 * 512 + 512],
                                          start=fm, stop=True,
                                          skip_group_check=True),
                                      deps=[], tick=(h2 == 1))
                    att_use[p["ab"]] = tk_av
                    kb_use[p["lp"]] = tk_av
                    vb_use[p["lp"]] = tk_av

                pending = []
                for si_, s_ in enumerate(srange):
                    lp = (sub * 8 + si_) % 3
                    kb_t[lp] = pl.op("y", lambda E, s_=s_, g_=g_, lp=lp: E["y"].dma_start(
                        out=kb[lp][:, :],
                        in_=keff.ap()[s_ * 1024 + g_ * 128: s_ * 1024 + g_ * 128 + 128, :]),
                        deps=[t_relay, t_relay2, kb_use[lp]], dma=True, sem=f"kv{lp}")
                    vb_t[lp] = pl.op("y", lambda E, s_=s_, g_=g_, lp=lp: E["y"].dma_start(
                        out=vb[lp].ap().rearrange("p (j d) -> p j d", j=4),
                        in_=veff.ap()[(s_ * 8 + g_) * 512: (s_ * 8 + g_) * 512 + 512, :]
                        .rearrange("(j p) d -> p j d", p=128)),
                        deps=[t_relay, t_relay2, vb_use[lp]], dma=True, sem=f"kv{lp}")
                    diag = s_ in (7, 23)
                    for j in range(4):
                        sc = cc % 2
                        scb = pA if sc == 0 else pB
                        tk_sc = None
                        for h2 in range(2):
                            h = g_ * 4 + pr * 2 + h2
                            tk_sc = pl.op("t", lambda E, lp=lp, j=j, h=h, h2=h2, scb=scb, qoff=qoff:
                                          E["t"].matmul(
                                              scb[:, h2 * 512:h2 * 512 + 512],
                                              kb[lp][:, j * 128:(j + 1) * 128],
                                              CT[:, h * TOK + qoff: h * TOK + qoff + TN],
                                              start=True, stop=True),
                                          deps=[kb_t[lp], exp_t[sc]],
                                          tick=(h2 == 1))
                        ab = cc % 2
                        tk_ex = pl.op("s", lambda E, ab=ab, scb=scb: E["s"].activation(
                            attb[ab][:, :], scb[:, :], AF.Exp,
                            bias=cm12[:, 0:1], scale=QKS),
                            deps=[tk_sc, att_use[ab], t_cm12], tick=True)
                        exp_t[sc] = tk_ex
                        tk_msk = tk_ex
                        if diag:
                            tm1 = pl.op("g", lambda E, ab=ab, j=j: E["g"].tensor_tensor(
                                attb[ab][:, 0:512], attb[ab][:, 0:512],
                                dmask[:, j * TN:(j + 1) * TN], OP.mult),
                                deps=[tk_ex, t_tables], tick=True)
                            tk_msk = pl.op("g", lambda E, ab=ab, j=j: E["g"].tensor_tensor(
                                attb[ab][:, 512:1024], attb[ab][:, 512:1024],
                                dmask[:, j * TN:(j + 1) * TN], OP.mult),
                                deps=[tm1], tick=True)
                        pending.append(dict(lp=lp, j=j, ab=ab, msk=tk_msk,
                                            vbt=vb_t[lp], s=s_,
                                            fm=(s_ == s0 and j == 0)))
                        if len(pending) > 1:
                            emit_avsums(pending.pop(0))
                        cc += 1
                while pending:
                    emit_avsums(pending.pop(0))
                tk_raw = []
                for h2 in range(2):
                    avb = pC if h2 == 0 else pD
                    smb = pE if h2 == 0 else pF
                    c1 = pl.op("v", lambda E, h2=h2, avb=avb: E["v"].tensor_copy(
                        avraw[h2][:, :], avb[:, :]),
                        deps=[pl.tkt("t"), avev[h2], rn_use[h2]], tick=True)
                    c2 = pl.op("v", lambda E, h2=h2, smb=smb: E["v"].tensor_copy(
                        smraw[h2][:, :], smb[:, :]), deps=[c1], tick=True)
                    avev[h2] = c2
                    tk_raw.append(c2)
                for h2 in range(2):
                    h = g_ * 4 + pr * 2 + h2
                    tr = pl.op("v", lambda E, h2=h2: E["v"].reciprocal(
                        rinv[h2][:, :], smraw[h2][:, :]),
                        deps=[tk_raw[h2], rn_use[h2]], tick=True)
                    rn_use[h2] = pl.op("v", lambda E, h2=h2, h=h, qoff=qoff:
                                       E["v"].tensor_tensor(
                                           BT[:, h * TOK + qoff: h * TOK + qoff + TN],
                                           avraw[h2][:, :], rinv[h2][:, :], OP.mult),
                                       deps=[tr], tick=True)

    w3ld = [None, None]
    w3use = [None, None]
    for nb in range(2):
        wsrc = wo[:, nb * 128:(nb + 1) * 128].rearrange("(c p) o -> p c o", p=128)
        w3ld[nb] = pl.op("g", lambda E, nb=nb, wsrc=wsrc: E["g"].dma_start(
            out=W3[nb].ap().rearrange("p (c o) -> p c o", c=FC), in_=wsrc),
            deps=[], dma=True, sem=f"w{nb}")
    pl.barrier()
    if "ph2" in dump:
        _dump(nc, pl, {"d_av": BT}, {})

    pc3 = 0
    bank_ev3 = {}
    xt_t = [None, None]
    xt_use = [None, None]
    xp_use = [None, None]
    xpb_use = [None, None]
    last_xpb = None
    for nb in range(FC):
        par = nb % 2
        if nb >= 2:
            wsrc = wo[:, nb * 128:(nb + 1) * 128].rearrange("(c p) o -> p c o", p=128)
            w3ld[par] = pl.op("g", lambda E, par=par, wsrc=wsrc: E["g"].dma_start(
                out=W3[par].ap().rearrange("p (c o) -> p c o", c=FC), in_=wsrc),
                deps=[w3use[par]], dma=True, sem=f"w{par}")
        for tt in range(2):
            pcp = pc3 % 2
            bank = pC if pcp == 0 else pD
            xpar = pc3 % 2
            xt_t[xpar] = pl.op("y", lambda E, nb=nb, tt=tt, xpar=xpar: E["y"].dma_start(
                out=xtile[xpar][:, :],
                in_=xT[nb * 128:(nb + 1) * 128, tt * TN: tt * TN + TN]),
                deps=[xt_use[xpar]], dma=True, sem=f"io{xpar}")
            tk_mm = None
            for dc in range(FC):
                tk_mm = pl.op("t", lambda E, dc=dc, par=par, tt=tt, bank=bank:
                              E["t"].matmul(
                                  bank[:, :], W3[par][:, dc * 128:(dc + 1) * 128],
                                  BT[:, dc * TOK + tt * TN: dc * TOK + tt * TN + TN],
                                  start=(dc == 0), stop=(dc == FC - 1)),
                              deps=[w3ld[par], bank_ev3.get(pcp)],
                              tick=(dc == FC - 1))
            w3use[par] = tk_mm
            ta = pl.op("v", lambda E, xpar=xpar, bank=bank: E["v"].tensor_tensor(
                xpf[xpar][:, :], bank[:, :], xtile[xpar][:, :], OP.add),
                deps=[tk_mm, xt_t[xpar], xp_use[xpar], xpb_use[xpar]], tick=True)
            bank_ev3[pcp] = ta
            xt_use[xpar] = ta
            ts_ = pl.op("y", lambda E, xpar=xpar, nb=nb, tt=tt: E["y"].dma_start(
                out=xpscr.ap()[nb * 128:(nb + 1) * 128, tt * TN: tt * TN + TN],
                in_=xpf[xpar][:, :]), deps=[ta], dma=True, sem=f"io{xpar}")
            last_xpb = pl.op("v", lambda E, xpar=xpar, nb=nb, tt=tt: E["v"].tensor_copy(
                CT[:, nb * TOK + tt * TN: nb * TOK + tt * TN + TN],
                xpf[xpar][:, :]), deps=[ta], tick=True)
            xp_use[xpar] = ts_
            xpb_use[xpar] = last_xpb
            pc3 += 1
    gate_io = [pl.tkt("io0"), pl.tkt("io1")]
    pl.barrier()
    ln2 = []
    ln2fc = []
    for tt in range(2):
        lt, lfc = emit_ln(CT[:, :], tt, last_xpb, m3, r3, var3, sd3, x2b3, lntmp3)
        ln2.append(lt)
        ln2fc.append(lfc)
    if "ph3" in dump:
        _dump(nc, pl, {"d_xn2": CT}, {"d_xp": xpscr})

    pl.op("g", lambda E: E["g"].dma_start(
        out=aaugs.ap().rearrange("p (c e) -> p c e", c=FC),
        in_=aaugT.rearrange("(c p) e -> p c e", p=128)), dma=True, sem="gx")
    pl.op("g", lambda E: E["g"].dma_start(out=balls[:, :], in_=ballT),
          dma=True, sem="gx")
    t_eb = pl.op("g", lambda E: E["g"].dma_start(out=biass[:, :], in_=ebias),
                 dma=True, sem="gx")
    t_aa = t_bb = t_eb

    zmk_t = [None, None]
    prev_zmk = None
    prev_ttp = None
    prev_ttp2 = None
    pc4 = 0
    w4ld = [None, None]
    w4use = [None, None]
    bank_ev4 = {}
    xp_t = [None, None]
    out_use = [None, None]
    y_store = []

    def emit_w4(nb):
        par = nb % 2
        wsrc = wup[:, nb * 128:(nb + 1) * 128].rearrange("(c p) o -> p c o", p=128)
        w4ld[par] = pl.op("g", lambda E, par=par, wsrc=wsrc: E["g"].dma_start(
            out=W3[par].ap().rearrange("p (c o) -> p c o", c=FC), in_=wsrc),
            deps=[w4use[par]], dma=True, sem=f"w{par}")

    def emit_base(nb, tt):
        nonlocal pc4
        par = nb % 2
        pcp = pc4 % 2
        bank = pC if pcp == 0 else pD
        xpar = pc4 % 2
        xp_t[xpar] = pl.op("y", lambda E, nb=nb, tt=tt, xpar=xpar: E["y"].dma_start(
            out=xtile[xpar][:, :],
            in_=xpscr.ap()[nb * 128:(nb + 1) * 128, tt * TN: tt * TN + TN]),
            deps=[out_use[xpar], gate_io[xpar]], dma=True, sem=f"io{xpar}")
        for fc in range(FC):
            pl.op("t", lambda E, fc=fc, par=par, tt=tt, bank=bank: E["t"].matmul(
                bank[:, :], W3[par][:, fc * 128:(fc + 1) * 128],
                CT[:, fc * TOK + tt * TN: fc * TOK + tt * TN + TN],
                start=(fc == 0), stop=False, skip_group_check=True),
                deps=[w4ld[par], ln2fc[tt][fc], bank_ev4.get(pcp)], tick=False)
        pc4 += 1
        return bank, xpar, pcp

    def emit_delta_evac(nb, tt, bank, xpar, pcp):
        tk_mm = pl.op("t", lambda E, nb=nb, tt=tt, bank=bank: E["t"].matmul(
            bank[:, :], balls[:, nb * 128:(nb + 1) * 128], zmk[tt][:, :],
            start=False, stop=True, skip_group_check=True),
            deps=[t_bb, zmk_t[tt]], tick=True)
        w4use[nb % 2] = tk_mm
        ta = pl.op("v", lambda E, xpar=xpar, bank=bank: E["v"].tensor_tensor(
            xpf[xpar][:, :], bank[:, :], xtile[xpar][:, :], OP.add),
            deps=[tk_mm, xp_t[xpar], out_use[xpar]], tick=True)
        bank_ev4[pcp] = ta
        ty = pl.op("y", lambda E, xpar=xpar, nb=nb, tt=tt: E["y"].dma_start(
            out=y[nb * 128:(nb + 1) * 128, tt * TN: tt * TN + TN],
            in_=xpf[xpar][:, :]), deps=[ta], dma=True)
        out_use[xpar] = ty
        y_store.append(ty)

    emit_w4(0)
    for tt in range(2):
        tk_z = None
        prev_v4 = pl.tkt("v")
        for fc in range(FC):
            tk_z = pl.op("t", lambda E, fc=fc, tt=tt: E["t"].matmul(
                pA[0:36, 0:512], aaugs[:, fc * 36:(fc + 1) * 36],
                CT[:, fc * TOK + tt * TN: fc * TOK + tt * TN + TN],
                start=(fc == 0), stop=(fc == FC - 1)),
                deps=[t_aa, ln2fc[tt][fc], prev_v4, prev_ttp],
                tick=(fc == FC - 1))
        t_zf = pl.op("v", lambda E: E["v"].tensor_copy(zsb[:, :], pA[0:36, 0:512]),
                     deps=[tk_z, prev_zmk], tick=True)
        t_lg = pl.op("v", lambda E: E["v"].tensor_scalar(
            lgsb[:, :], pA[32:36, 0:512], biass[32:36, 0:1], 0.0, OP.add),
            deps=[tk_z, t_eb, t_zf, prev_ttp], tick=True)
        bk0, xp0, pp0 = emit_base(0, tt)
        t_tp = None
        for ch in range(4):
            t_tp = pl.op("t", lambda E, ch=ch: E["t"].transpose(
                pA[:, 512 + ch * 4: 512 + ch * 4 + 4],
                lgsb[:, ch * 128:(ch + 1) * 128], identf[0:4, 0:4]),
                deps=[t_lg], tick=(ch == 3))
        t_lgt = pl.op("v", lambda E: E["v"].tensor_copy(lgt[:, :], pA[:, 512:528]),
                      deps=[t_tp, prev_ttp2], tick=True)
        last = t_lgt
        for ch in range(4):
            last = pl.op("v", lambda E, ch=ch: E["v"].reduce_max(
                m4[:, ch:ch + 1], lgt[:, ch * 4:(ch + 1) * 4], axis=AX.X),
                deps=[last], tick=True)
            last = pl.op("v", lambda E, ch=ch: E["v"].tensor_tensor(
                ohtm[:, ch * 4: ch * 4 + 1], lgt[:, ch * 4: ch * 4 + 1],
                m4[:, ch:ch + 1], OP.is_ge), deps=[last], tick=True)
            last = pl.op("v", lambda E, ch=ch: E["v"].tensor_copy(
                tb1[:, 0:1], ohtm[:, ch * 4: ch * 4 + 1]), deps=[last], tick=True)
            for e_ in range(1, 4):
                last = pl.op("v", lambda E, ch=ch, e_=e_: E["v"].tensor_tensor(
                    tb2[:, 0:1], lgt[:, ch * 4 + e_: ch * 4 + e_ + 1],
                    m4[:, ch:ch + 1], OP.is_ge), deps=[last], tick=True)
                last = pl.op("v", lambda E: E["v"].tensor_scalar(
                    tb2[:, 1:2], tb1[:, 0:1], -1.0, 1.0, OP.mult, OP.add),
                    deps=[last], tick=True)
                last = pl.op("v", lambda E, ch=ch, e_=e_: E["v"].tensor_tensor(
                    ohtm[:, ch * 4 + e_: ch * 4 + e_ + 1], tb2[:, 0:1],
                    tb2[:, 1:2], OP.mult), deps=[last], tick=True)
                last = pl.op("v", lambda E, ch=ch, e_=e_: E["v"].tensor_tensor(
                    tb1[:, 0:1], tb1[:, 0:1],
                    ohtm[:, ch * 4 + e_: ch * 4 + e_ + 1], OP.add),
                    deps=[last], tick=True)
        t_tp2 = None
        for ch in range(4):
            t_tp2 = pl.op("t", lambda E, ch=ch: E["t"].transpose(
                pB[0:4, 512 + ch * 128: 512 + ch * 128 + 128],
                ohtm[:, ch * 4:(ch + 1) * 4], identf[:, :]),
                deps=[last], tick=(ch == 3))
        t_oh = pl.op("v", lambda E: E["v"].tensor_copy(
            ohsb[:, :], pB[0:4, 512:1024]), deps=[t_tp2], tick=True)
        t_ohx = pl.op("t", lambda E: E["t"].matmul(
            pB[0:32, 0:512], e32[:, :], ohsb[:, :], start=True, stop=True),
            deps=[t_oh], tick=True)
        zmk_t[tt] = pl.op("v", lambda E, tt=tt: E["v"].tensor_tensor(
            zmk[tt][:, :], zsb[0:32, :], pB[0:32, 0:512], OP.mult),
            deps=[t_ohx, t_zf], tick=True)
        prev_zmk = zmk_t[tt]
        prev_ttp = t_tp
        prev_ttp2 = t_tp2
        emit_delta_evac(0, tt, bk0, xp0, pp0)

    for nb in range(1, FC):
        emit_w4(nb)
        for tt in range(2):
            bk, xp_, pp = emit_base(nb, tt)
            emit_delta_evac(nb, tt, bk, xp_, pp)

    if "ph4" in dump:
        pl.barrier()
        for nm, tns, shp, dt in (("d_zmk0", zmk[0], [32, TN], BF16),
                                 ("d_zmk1", zmk[1], [32, TN], BF16),
                                 ("d_zsb", zsb, [36, TN], F32),
                                 ("d_ohsb", ohsb, [4, TN], BF16),
                                 ("d_lgt", lgt, [128, 16], F32)):
            dap = nc.dram_tensor(nm, shp, dt, kind="ExternalOutput").ap()
            pl.op("y", lambda E, dap=dap, tns=tns: E["y"].dma_start(
                out=dap, in_=tns[:, :]), dma=True, sem="dbg")

    pl.barrier()

    SEMNAMES = ["t", "v", "s", "g", "y", "ck", "cv", "gx", "gy", "gz", "gw",
                "w0", "w1", "w0q0", "w0q1", "w0q2", "w0q3",
                "w1q0", "w1q1", "w1q2", "w1q3",
                "kv0", "kv1", "kv2", "io0", "io1", "dbg", "rs"]
    with ExitStack() as st:
        SEMS = {n: st.enter_context(nc.semaphore(n)) for n in SEMNAMES}
        block = st.enter_context(nc.Block())
        E = {"t": nc.tensor, "v": nc.vector, "s": nc.scalar,
             "g": nc.gpsimd, "y": nc.sync}

        def replay_one(r):
            e = E[r["eng"]]
            for (sm, v) in r["waits"]:
                e.wait_ge(SEMS[sm], v)
            if r["fn"] is not None:
                ins = r["fn"](E)
                if r["inc"] is not None:
                    sm, n, raw = r["inc"]
                    if raw:
                        ins.then_inc(SEMS[sm])
                    else:
                        ins.then_inc(SEMS[sm], n)

        def replay(recs):
            i = 0
            while i < len(recs):
                r = recs[i]
                if r.get("ctl") == "sw2_begin":
                    slot = r["slot"]
                    j = i + 1
                    body = []
                    while recs[j].get("ctl") != "sw2_end":
                        body.append(recs[j])
                        j += 1
                    deltas = recs[j]["deltas"]
                    assert all(rr["eng"] in ("t", "s") for rr in body)
                    for ename, reg in (("t", regT), ("s", regS)):
                        e = E[ename]
                        e.reg_load(reg, ifl[0:1, slot:slot + 1])
                        with e.If_ne(reg, 0):
                            for rr in body:
                                if rr["eng"] == ename:
                                    replay_one(rr)
                        with e.Else():
                            if deltas.get(ename):
                                e.sem_inc(SEMS[ename], deltas[ename])
                        e.end_ifs()
                    i = j + 1
                    continue
                replay_one(r)
                i += 1

        replay(pl.segs["main0"])
        pid = nc.partition_id()
        for i in nc.Switch(engines=[nc.sync, nc.scalar], index=[pid, pid], n=NCORE):
            replay(pl.segs[f"sw{i}"])
        replay(pl.segs["main1"])

    return nc


def _dump(nc, pl, sbuf_map, dram_map):
    pl.barrier()
    for name, t in sbuf_map.items():
        d = nc.dram_tensor(name, [128, FC * TOK], BF16, kind="ExternalOutput").ap()
        pl.op("y", lambda E, d=d, t=t: E["y"].dma_start(out=d, in_=t[:, :]),
              dma=True, sem="dbg")
    for name, t in dram_map.items():
        h = t.ap()
        d = nc.dram_tensor(name, list(h.shape), BF16 if name != "d_xp" else F32,
                           kind="ExternalOutput").ap()
        pl.op("y", lambda E, d=d, h=h: E["y"].dma_start(out=d, in_=h),
              dma=True, sem="dbg")
    pl.barrier()


def _prep_inputs(L0_attn_x, L0_attn_wq, L0_attn_wk, L0_attn_wv, L0_attn_wo,
                 L0_ffn_W_up, L0_ffn_gate_w, L0_ffn_expert_bias,
                 L0_ffn_lora_A, L0_ffn_lora_B):
    f32 = lambda a: np.ascontiguousarray(np.asarray(a), dtype=np.float32)
    x = f32(L0_attn_x)[0]
    wq = f32(L0_attn_wq)
    wk = f32(L0_attn_wk)
    wv = f32(L0_attn_wv)
    wo = f32(L0_attn_wo)
    wup = np.ascontiguousarray(f32(L0_ffn_W_up).T)
    gate = f32(L0_ffn_gate_w)
    ebias_ = f32(L0_ffn_expert_bias)
    la = f32(L0_ffn_lora_A)
    lb = f32(L0_ffn_lora_B)

    aaug = np.concatenate([la.reshape(32, D), gate], axis=0)
    aaugT = np.ascontiguousarray(aaug.T)
    ballT = np.ascontiguousarray(lb.transpose(0, 2, 1).reshape(32, D))
    eb36 = np.zeros((36, 1), np.float32)
    eb36[32:36, 0] = ebias_

    identf = np.eye(128, dtype=np.float32)
    onesb = np.ones((128, 128), dtype=BF)
    psw = np.zeros((128, 128), np.float32)
    for k in range(128):
        psw[k, k ^ 1] = 1.0
    pswap = psw.astype(BF)
    e32 = np.zeros((4, 32), np.float32)
    for e_ in range(4):
        e32[e_, e_ * 8:(e_ + 1) * 8] = 1.0
    e32 = e32.astype(BF)
    kk = np.arange(128)[:, None]
    qq = np.arange(TN)[None, :]
    dm = np.zeros((128, 4 * TN), np.float32)
    for j in range(4):
        dm[:, j * TN:(j + 1) * TN] = (qq >= 128 * j + kk).astype(np.float32)
    dmask = dm.astype(BF)

    inv = 1.0 / (10000.0 ** (np.arange(0, HD, 2, dtype=np.float64) / HD))

    in_maps = []
    for c in range(NCORE):
        bA, bB = c, 15 - c
        toks = np.concatenate([np.arange(bA * TN, (bA + 1) * TN),
                               np.arange(bB * TN, (bB + 1) * TN)])
        xc = np.ascontiguousarray(
            np.concatenate([x[bA * TN:(bA + 1) * TN], x[bB * TN:(bB + 1) * TN]], 0).T)
        ang = inv[:, None] * toks[None, :].astype(np.float64)
        cosr = np.cos(ang)
        sinr = np.sin(ang)
        cosT = np.repeat(cosr, 2, axis=0).astype(np.float32)
        ssinT = np.repeat(sinr, 2, axis=0).astype(np.float32)
        ssinT[0::2, :] *= -1.0
        sf = np.zeros((128, NSLOT * 128), np.float32)
        ifl_np = np.zeros((1, 32), np.int32)
        for s_ in range(NSLOT):
            if slot_block(c, s_) is not None:
                sf[:, s_ * 128:(s_ + 1) * 128] = 1.0
                ifl_np[0, s_] = 1
        in_maps.append({
            "xT": xc, "wq": wq, "wk": wk, "wv": wv, "wo": wo, "wup": wup,
            "aaugT": aaugT, "ballT": ballT, "ebias": eb36,
            "cosT": cosT, "ssinT": ssinT, "identf": identf,
            "onesb": onesb, "pswap": pswap, "e32": e32,
            "dmask": dmask, "sflags": sf.astype(BF), "iflags": ifl_np,
        })
    return in_maps


def kernel(L0_attn_x, L0_attn_wq, L0_attn_wk, L0_attn_wv, L0_attn_wo,
           L0_ffn_W_up, L0_ffn_gate_w, L0_ffn_expert_bias,
           L0_ffn_lora_A, L0_ffn_lora_B):
    in_maps = _prep_inputs(L0_attn_x, L0_attn_wq, L0_attn_wk, L0_attn_wv,
                           L0_attn_wo, L0_ffn_W_up, L0_ffn_gate_w,
                           L0_ffn_expert_bias, L0_ffn_lora_A, L0_ffn_lora_B)
    nc = build()
    res = run_bass_kernel_spmd(nc, in_maps, core_ids=list(range(NCORE)))
    out = np.empty((1, S, D), np.float32)
    for c in range(NCORE):
        yc = np.asarray(res.results[c]["y"])
        bA, bB = c, 15 - c
        out[0, bA * TN:(bA + 1) * TN] = yc[:, 0:TN].T
        out[0, bB * TN:(bB + 1) * TN] = yc[:, TN:2 * TN].T
    return out
